# version 19
# baseline (speedup 1.0000x reference)
import numpy as np
import concourse.bass as bass
import concourse.mybir as mybir
from concourse.bass_utils import run_bass_kernel_spmd
from concourse.bass import IndirectOffsetOnAxis

F32 = mybir.dt.float32
BF16 = mybir.dt.bfloat16
I32 = mybir.dt.int32
U32 = mybir.dt.uint32
AF = mybir.ActivationFunctionType
ALU = mybir.AluOpType
AX = mybir.AxisListType

B, T, D = 2, 8192, 1024
UW = 8320
W = 64
NSWEEP = 26
OMEGA = 1.2
KPIC = 20
HALO = 32
CAP_B = 192
CAP_D = 1024
LN_EPS = 1e-5
NCORE = 8
TPC = 2048


class Ctr:
    def __init__(self):
        self.v = {}
    def inc(self, sem, n=1):
        k = sem.name
        self.v[k] = self.v.get(k, 0) + n
        return self.v[k]
    def val(self, sem):
        return self.v.get(sem.name, 0)


def emit_ln_tokmajor(nc, ctr, dve, act, src, dst, mu, ssq, rstd, sqs, epsb, gt=None, bt=None):
    @_vec(nc)
    def _(v):
        v.reduce_sum(mu[:], src[:], axis=AX.X)
        v.drain()
        v.tensor_scalar_mul(mu[:], mu[:], 1.0 / D)
        v.drain()
        v.tensor_scalar(dst[:], src[:], mu[:], None, ALU.subtract).then_inc(dve, 1)
    ctr.inc(dve, 1)
    dvev = ctr.val(dve)
    @_scl(nc)
    def _(s):
        s.wait_ge(dve, dvev)
        s.activation(sqs[:], dst[:], AF.Square, accum_out=ssq[:])
        s.drain()
        s.activation(rstd[:], ssq[:], AF.Ln, bias=epsb[:], scale=1.0 / D)
        s.drain()
        s.activation(rstd[:], rstd[:], AF.Exp, scale=-0.5).then_inc(act, 1)
    ctr.inc(act, 1)
    actv = ctr.val(act)
    @_vec(nc)
    def _(v):
        v.wait_ge(act, actv)
        if gt is not None:
            v.scalar_tensor_tensor(dst[:], dst[:], rstd[:], gt[:], ALU.mult, ALU.mult)
            v.drain()
            v.tensor_add(dst[:], dst[:], bt[:])
        else:
            v.tensor_scalar(dst[:], dst[:], rstd[:], None, ALU.mult)
        v.drain()
        v.engine_nop().then_inc(dve, 1)
    ctr.inc(dve, 1)


def emit_ln_fast(nc, ctr, dve, act, src, dst, cen, mu, ssq, rstd, sqs, epsb, tmp1,
                 gt=None, bt=None):
    dvev0 = ctr.val(dve)
    @_scl(nc)
    def _(s):
        s.wait_ge(dve, dvev0)
        s.activation(sqs[:], src[:], AF.Square, accum_out=ssq[:]).then_inc(act, 1)
    ctr.inc(act, 1)
    @_vec(nc)
    def _(v):
        v.reduce_sum(mu[:], src[:], axis=AX.X)
        v.drain()
        v.tensor_scalar_mul(mu[:], mu[:], 1.0 / D)
        v.drain()
        v.tensor_mul(tmp1[:], mu[:], mu[:])
        v.drain()
        v.tensor_sub(tmp1[:], epsb[:], tmp1[:]).then_inc(dve, 1)
        v.drain()
        v.tensor_scalar(cen[:], src[:], mu[:], None, ALU.subtract).then_inc(dve, 1)
    ctr.inc(dve, 2)
    dvev1 = ctr.val(dve) - 1
    @_scl(nc)
    def _(s):
        s.drain()
        s.wait_ge(dve, dvev1)
        s.activation(rstd[:], ssq[:], AF.Ln, bias=tmp1[:], scale=1.0 / D)
        s.drain()
        s.activation(rstd[:], rstd[:], AF.Exp, scale=-0.5).then_inc(act, 1)
    ctr.inc(act, 1)
    actv = ctr.val(act)
    @_vec(nc)
    def _(v):
        v.wait_ge(act, actv)
        v.drain()
        if gt is not None:
            v.scalar_tensor_tensor(dst[:], cen[:], rstd[:], gt[:], ALU.mult, ALU.mult)
            v.drain()
            v.tensor_add(dst[:], dst[:], bt[:])
        else:
            v.tensor_scalar(dst[:], cen[:], rstd[:], None, ALU.mult)
        v.drain()
        v.engine_nop().then_inc(dve, 1)
    ctr.inc(dve, 1)


class _eng_ctx:
    def __init__(self, eng):
        self.eng = eng
    def __call__(self, f):
        f(self.eng)

def _vec(nc):
    return _eng_ctx(nc.vector)
def _scl(nc):
    return _eng_ctx(nc.scalar)
def _ten(nc):
    return _eng_ctx(nc.tensor)
def _gps(nc):
    return _eng_ctx(nc.gpsimd)
def _syn(nc):
    return _eng_ctx(nc.sync)


def emit_stage_c(nc, ctr, sems, io):
    _pref = io.get("prefix", "c_")
    (x2full, A_T, B_T, C_T, gbc, bbc, x3s, identC) = (
        io["x2full"], io["A_T"], io["B_T"], io["C_T"],
        io["gbc"], io["bbc"], io["x3s"], io["ident"])
    NH = TPC + HALO
    hoff = io["halo_off"]
    roff = hoff + HALO
    ZW = NH + 1
    TT = []
    t0 = 0
    while t0 < NH:
        TT.append((t0, min(512, NH - t0)))
        t0 += 512

    from contextlib import ExitStack
    with ExitStack() as es:
        E = es.enter_context
        ATt = E(nc.sbuf_tensor(_pref + "ATt", [128, 8 * D], BF16))
        CTt = E(nc.sbuf_tensor(_pref + "CTt", [128, 8 * D], BF16))
        xb = E(nc.sbuf_tensor(_pref + "xb", [128, 8 * NH], BF16))
        z0 = E(nc.sbuf_tensor(_pref + "z0", [128, 8 * ZW], BF16))
        z1 = E(nc.sbuf_tensor(_pref + "z1", [128, 8 * ZW], BF16))
        scr = E(nc.sbuf_tensor(_pref + "scr", [128, 4 * 512], F32))
        idn_c = E(nc.sbuf_tensor(_pref + "idn_c", [128, 128], F32))
        mu = E(nc.sbuf_tensor(_pref + "mu", [128, 1], F32))
        ssq = E(nc.sbuf_tensor(_pref + "ssq", [128, 1], F32))
        rstd = E(nc.sbuf_tensor(_pref + "rstd", [128, 1], F32))
        epsb = E(nc.sbuf_tensor(_pref + "epsb", [128, 1], F32))
        pa = E(nc.psum_tensor(_pref + "pa", [128, 512], F32))
        pb = E(nc.psum_tensor(_pref + "pb", [128, 512], F32))
        pc = E(nc.psum_tensor(_pref + "pc", [128, 512], F32))
        pd = E(nc.psum_tensor(_pref + "pd", [128, 512], F32))
        ld = sems["ld"]
        ldx = sems["ldx"]
        ldr0 = sems["ldr0"]
        ldr1 = sems["ldr1"]
        pe = sems["pe"]
        dve = sems["dve"]
        act = sems["act"]
        st = sems["st"]
        zbuf = [z0, z1]
        psum = [pa, pb]

        es_p1 = ExitStack()
        x2T = es_p1.enter_context(nc.sbuf_tensor(_pref + "x2T", [128, 8 * NH], BF16))
        BTt = es_p1.enter_context(nc.sbuf_tensor(_pref + "BTt", [128, 8 * D], BF16))

        @_gps(nc)
        def _(g):
            g.dma_start(out=ATt[:].rearrange("p (k e) -> p k e", e=D),
                        in_=A_T.rearrange("(k p) e -> p k e", p=128)).then_inc(ld, 16)
            g.dma_start(out=CTt[:].rearrange("p (k e) -> p k e", e=D),
                        in_=C_T.rearrange("(k p) e -> p k e", p=128)).then_inc(ld, 16)
            g.dma_start(out=BTt[:].rearrange("p (k e) -> p k e", e=D),
                        in_=B_T.rearrange("(k p) e -> p k e", p=128)).then_inc(ld, 16)
            g.dma_start(out=idn_c[:], in_=identC).then_inc(ld, 16)
        ctr.inc(ld, 64)
        ld_all = ctr.val(ld)

        x2q = es_p1.enter_context(nc.sbuf_tensor(_pref + "x2q", [128, D], F32))
        nchk = (NH + 127) // 128
        for kchk in range(nchk):
            rows = min(128, NH - kchk * 128)
            dve_now = ctr.val(dve)
            @_syn(nc)
            def _(s, kchk=kchk, rows=rows, dve_now=dve_now):
                if kchk > 0:
                    s.wait_ge(dve, dve_now)
                s.dma_start(out=x2q[0:rows, :],
                            in_=x2full[hoff + kchk * 128: hoff + kchk * 128 + rows, :]
                            ).then_inc(ldx, 16)
            ctr.inc(ldx, 16)
            ldxv = ctr.val(ldx)
            for kd in range(8):
                dve_t = ctr.val(dve)
                @_ten(nc)
                def _(t, kchk=kchk, rows=rows, kd=kd, ldxv=ldxv, dve_t=dve_t):
                    if kd == 0:
                        t.wait_ge(ldx, ldxv)
                        if kchk == 0:
                            t.wait_ge(ld, ld_all)
                    else:
                        t.wait_ge(dve, dve_t)
                    t.transpose(pa[:, 0:rows], x2q[0:rows, kd * 128:(kd + 1) * 128],
                                idn_c[0:rows, 0:rows]).then_inc(pe, 1)
                ctr.inc(pe, 1)
                pev = ctr.val(pe)
                @_vec(nc)
                def _(v, kchk=kchk, rows=rows, kd=kd, pev=pev):
                    v.wait_ge(pe, pev)
                    v.tensor_copy(x2T[:, kd * NH + kchk * 128: kd * NH + kchk * 128 + rows],
                                  pa[:, 0:rows]).then_inc(dve, 1)
                ctr.inc(dve, 1)

        @_vec(nc)
        def _(v):
            v.memset(z0[:], 0.0)
            v.drain()
            v.memset(z1[:], 0.0)
            v.drain()
            v.memset(epsb[:], LN_EPS)
            v.drain()
            v.engine_nop().then_inc(dve, 1)
        ctr.inc(dve, 1)
        dve_z = ctr.val(dve)

        dve_tp = ctr.val(dve)
        @_ten(nc)
        def _(t):
            t.wait_ge(ld, ld_all)
            t.wait_ge(dve, dve_tp)
        psum4x = [pa, pb, pc, pd]
        ti = 0
        for (tt0, tw) in TT:
            for ke in range(8):
                par = ti % 4
                @_ten(nc)
                def _(t, tt0=tt0, tw=tw, ke=ke, par=par, ti=ti):
                    if ti >= 4:
                        t.wait_ge(dve, dve_z + ti - 3)
                    last = None
                    for kd in range(8):
                        last = t.matmul(psum4x[par][:, 0:tw],
                                        BTt[:, kd * D + ke * 128: kd * D + ke * 128 + 128],
                                        x2T[:, kd * NH + tt0: kd * NH + tt0 + tw],
                                        start=(kd == 0), stop=(kd == 7))
                    last.then_inc(pe, 1)
                ctr.inc(pe, 1)
                pev = ctr.val(pe)
                @_vec(nc)
                def _(v, tt0=tt0, tw=tw, ke=ke, par=par, pev=pev):
                    v.wait_ge(pe, pev)
                    v.tensor_copy(xb[:, ke * NH + tt0: ke * NH + tt0 + tw],
                                  psum4x[par][:, 0:tw]).then_inc(dve, 1)
                ctr.inc(dve, 1)
                ti += 1

        pe_phase1 = ctr.val(pe)
        es_p1.close()

        psum4 = [pa, pb, pc, pd]
        TILES = [(m, tt0, tw) for (tt0, tw) in TT for m in range(8)]
        ntile = len(TILES)
        dve_pic0 = ctr.val(dve)
        act_pic0 = ctr.val(act)
        dve_base0 = ctr.val(dve)
        for sw in range(KPIC):
            rd, wr = sw % 2, (sw + 1) % 2
            for i, (m, tt0, tw) in enumerate(TILES):
                g = sw * ntile + i
                par = g % 4
                ct = i // 8
                wav = act_pic0 + (sw - 1) * ntile + (ct + 1) * 8 if sw > 0 else 0
                @_ten(nc)
                def _(t, m=m, tt0=tt0, tw=tw, par=par, g=g, sw=sw, i=i, rd=rd, wav=wav):
                    if sw == 0 and i == 0:
                        t.wait_ge(dve, dve_base0)
                    if wav > 0 and (i % 8 == 0 or i == 0):
                        t.wait_ge(act, wav)
                    if g >= 4:
                        t.wait_ge(dve, dve_pic0 + g - 3)
                    last = None
                    for kd in range(8):
                        last = t.matmul(psum4[par][:, 0:tw],
                                        ATt[:, kd * D + m * 128: kd * D + m * 128 + 128],
                                        zbuf[rd][:, kd * ZW + tt0: kd * ZW + tt0 + tw],
                                        start=(kd == 0), stop=(kd == 7))
                    last.then_inc(pe, 1)
                ctr.inc(pe, 1)
                pev = ctr.val(pe)
                act_guard = act_pic0 + g - 3 if g >= 4 else 0
                @_vec(nc)
                def _(v, m=m, tt0=tt0, tw=tw, par=par, pev=pev, act_guard=act_guard):
                    v.wait_ge(pe, pev)
                    if act_guard > 0:
                        v.wait_ge(act, act_guard)
                    v.tensor_add(scr[:, par * 512: par * 512 + tw],
                                 psum4[par][:, 0:tw],
                                 xb[:, m * NH + tt0: m * NH + tt0 + tw]).then_inc(dve, 1)
                ctr.inc(dve, 1)
                dvev = ctr.val(dve)
                @_scl(nc)
                def _(s, m=m, tt0=tt0, tw=tw, par=par, dvev=dvev, wr=wr):
                    s.wait_ge(dve, dvev)
                    s.activation(zbuf[wr][:, m * ZW + 1 + tt0: m * ZW + 1 + tt0 + tw],
                                 scr[:, par * 512: par * 512 + tw],
                                 AF.Tanh).then_inc(act, 1)
                ctr.inc(act, 1)
        pic_lastbase = act_pic0 + (KPIC - 1) * ntile
        pic_done = ctr.val(act)
        pic_dve_done = ctr.val(dve)
        zfin = zbuf[KPIC % 2]

        gt = E(nc.sbuf_tensor(_pref + "gt", [128, D], F32))
        bt = E(nc.sbuf_tensor(_pref + "bt", [128, D], F32))
        x2r = E(nc.sbuf_tensor(_pref + "x2r", [128, 2 * D], F32))
        att = E(nc.sbuf_tensor(_pref + "att", [128, 2 * D], F32))
        sqs = E(nc.sbuf_tensor(_pref + "sqs", [128, 2 * D], F32))
        x3t = E(nc.sbuf_tensor(_pref + "x3t", [128, 2 * D], F32))
        mu2 = E(nc.sbuf_tensor(_pref + "mu2", [128, 4], F32))
        ssq2 = E(nc.sbuf_tensor(_pref + "ssq2", [128, 2], F32))
        rstd2 = E(nc.sbuf_tensor(_pref + "rstd2", [128, 2], F32))
        pep = [E(nc.psum_tensor(_pref + "pe%d" % j, [128, 512], F32)) for j in range(4)]
        @_syn(nc)
        def _(sync):
            sync.wait_ge(pe, pe_phase1)
            sync.dma_start(out=gt[:], in_=gbc).then_inc(ldx, 16)
            sync.dma_start(out=bt[:], in_=bbc).then_inc(ldx, 16)
        ctr.inc(ldx, 32)
        ldx_gb = ctr.val(ldx)
        nt = TPC // 128
        tile_act_end = {}
        tile_add_dve = {}
        st_base_c = ctr.val(st)
        for it in range(nt):
            tb = HALO + it * 128
            par = it % 2
            ct_hi = (tb + 127) // 512
            wav = pic_lastbase + min((ct_hi + 1) * 8, ntile)
            st_guard = st_base_c + (it - 1) * 16 if it >= 2 else 0
            @_syn(nc)
            def _(sync, it=it, par=par, st_guard=st_guard, wav=wav):
                if st_guard > 0:
                    sync.wait_ge(st, st_guard)
                else:
                    sync.wait_ge(act, wav)
                ldr = ldr0 if par == 0 else ldr1
                sync.dma_start(out=x2r[:, par * D: par * D + D],
                               in_=x2full[roff + it * 128: roff + (it + 1) * 128, :]).then_inc(ldr, 16)
            ctr.inc(ldr0 if par == 0 else ldr1, 16)
            ldv = ctr.val(ldr0 if par == 0 else ldr1)
            pguard = tile_add_dve.get(it - 2, 0)
            for eh in range(2):
                @_ten(nc)
                def _(t, tb=tb, eh=eh, par=par, wav=wav, pguard=pguard, it=it):
                    if eh == 0:
                        t.wait_ge(act, wav)
                        if it == 0:
                            t.wait_ge(dve, pic_dve_done)
                        if pguard > 0:
                            t.wait_ge(dve, pguard)
                    last = None
                    for m in range(8):
                        last = t.matmul(pep[2 * par + eh][:, 0:512],
                                        zfin[:, m * ZW + 1 + tb: m * ZW + 1 + tb + 128],
                                        CTt[:, m * D + eh * 512: m * D + eh * 512 + 512],
                                        start=(m == 0), stop=(m == 7))
                    last.then_inc(pe, 1)
                ctr.inc(pe, 1)
            pev = ctr.val(pe)
            aguard = tile_act_end.get(it - 2, 0)
            @_vec(nc)
            def _(v, it=it, par=par, pev=pev, ldv=ldv, aguard=aguard):
                v.wait_ge(pe, pev)
                v.wait_ge(ldx, ldx_gb)
                v.wait_ge(ldr0 if par == 0 else ldr1, ldv)
                if aguard > 0:
                    v.wait_ge(act, aguard)
                if it < 2:
                    v.wait_ge(act, pic_done)
                x2t = x2r[:, par * D: par * D + D]
                v.tensor_add(att[:, par * D: par * D + 512], pep[2 * par][:, 0:512], x2t[:, 0:512])
                v.drain()
                v.tensor_add(att[:, par * D + 512: par * D + 1024], pep[2 * par + 1][:, 0:512],
                             x2t[:, 512:1024])
                v.drain()
                v.engine_nop().then_inc(dve, 1)
            ctr.inc(dve, 1)
            tile_add_dve[it] = ctr.val(dve)
            attp = att[:, par * D: par * D + D]
            x3tp = x3t[:, par * D: par * D + D]
            sqsp = sqs[:, par * D: par * D + D]
            cen1 = scr[:, par * D: par * D + D]
            cen2 = att[:, (1 - par) * D: (1 - par) * D + D]
            mup = mu2[:, par: par + 1]
            tmp1p = mu2[:, 2 + par: 3 + par]
            ssqp = ssq2[:, par: par + 1]
            rstdp = rstd2[:, par: par + 1]
            emit_ln_fast(nc, ctr, dve, act, attp, attp, cen1, mup, ssqp, rstdp, sqsp,
                         epsb, tmp1p, gt, bt)
            dvev = ctr.val(dve)
            @_vec(nc)
            def _(v, par=par, dvev=dvev, attp=attp):
                v.wait_ge(dve, dvev)
                x2t = x2r[:, par * D: par * D + D]
                v.tensor_add(attp, attp, x2t)
                v.drain()
                v.engine_nop().then_inc(dve, 1)
            ctr.inc(dve, 1)
            emit_ln_fast(nc, ctr, dve, act, attp, x3tp, cen2, mup, ssqp, rstdp, sqsp,
                         epsb, tmp1p)
            tile_act_end[it] = ctr.val(act)
            dvev = ctr.val(dve)
            st_prev = ctr.val(st)
            @_gps(nc)
            def _(g, it=it, dvev=dvev, x3tp=x3tp, st_prev=st_prev):
                g.wait_ge(dve, dvev)
                if it > 0:
                    g.wait_ge(st, st_prev)
                g.dma_start(out=x3s[it * 128:(it + 1) * 128, :], in_=x3tp).then_inc(st, 16)
            ctr.inc(st, 16)
        @_gps(nc)
        def _(g):
            g.wait_ge(st, ctr.val(st))
    return nc


def build_stage_c():
    nc = bass.Bass("TRN2", target_bir_lowering=False, debug=False)
    NH = TPC + HALO
    io = {}
    io["x2full"] = nc.dram_tensor("x2full", [NH, D], F32, kind="ExternalInput").ap()
    io["halo_off"] = 0
    io["ident"] = nc.dram_tensor("ident", [128, 128], F32, kind="ExternalInput").ap()
    io["A_T"] = nc.dram_tensor("A_T", [D, D], F32, kind="ExternalInput").ap()
    io["B_T"] = nc.dram_tensor("B_T", [D, D], F32, kind="ExternalInput").ap()
    io["C_T"] = nc.dram_tensor("C_T", [D, D], F32, kind="ExternalInput").ap()
    io["gbc"] = nc.dram_tensor("gbc", [128, D], F32, kind="ExternalInput").ap()
    io["bbc"] = nc.dram_tensor("bbc", [128, D], F32, kind="ExternalInput").ap()
    io["x3s"] = nc.dram_tensor("x3s", [TPC, D], F32, kind="ExternalOutput").ap()
    ctr = Ctr()
    sems = {}
    from contextlib import ExitStack
    with ExitStack() as ses:
        for sname in ["ld", "ldx", "ldr0", "ldr1", "pe", "dve", "act", "st"]:
            sems[sname] = ses.enter_context(nc.semaphore(sname))
        emit_stage_c(nc, ctr, sems, io)
    return nc


def run_stage_c(inp, x2_full):
    nc = build_stage_c()
    xf = x2_full
    gamma, beta = inp["L1_attn_gamma"], inp["L1_attn_beta"]
    shared = {
        "ident": np.eye(128, dtype=np.float32),
        "A_T": np.ascontiguousarray(inp["L1_attn_A"].T),
        "B_T": np.ascontiguousarray(inp["L1_attn_B"].T),
        "C_T": np.ascontiguousarray(inp["L1_attn_C"].T),
        "gbc": np.ascontiguousarray(np.broadcast_to(gamma, (128, D))),
        "bbc": np.ascontiguousarray(np.broadcast_to(beta, (128, D))),
    }
    in_maps = []
    for c in range(NCORE):
        lo = c * TPC
        sl = xf[lo:lo + TPC]
        if c % 4 == 0:
            halo = np.zeros((HALO, D), np.float32)
        else:
            halo = xf[lo - HALO:lo]
        x2h = np.concatenate([halo, sl], 0)
        in_maps.append(dict(shared, x2full=np.ascontiguousarray(x2h)))
    res = run_bass_kernel_spmd(nc, in_maps, core_ids=list(range(NCORE)))
    return np.concatenate([np.asarray(r["x3s"]) for r in res.results], 0)




def emit_stage_a(nc, ctr, sems, io):
    _pref = io.get("prefix", "a_")
    (fT, f_tm, wq_T, w2col, wqbcol, maskM, ones164, onesr128, w2b_in, ident,
     cc_init, cr_init, gbcA, bbcA, x1s) = (
        io["fT"], io["f_tm"], io["wq_T"], io["w2col"], io["wqbcol"], io["maskM"],
        io["ones164"], io["onesr128"], io["w2b_in"], io["ident"],
        io["cc_init"], io["cr_init"], io["gbcA"], io["bbcA"], io["x1s"])
    from contextlib import ExitStack
    with ExitStack() as es:
        E = es.enter_context
        cc0 = E(nc.sbuf_tensor(_pref + "cc0", [128, UW], F32))
        cc1 = E(nc.sbuf_tensor(_pref + "cc1", [128, UW], F32))
        cr0 = E(nc.sbuf_tensor(_pref + "cr0", [128, 65 * 65], F32))
        cr1 = E(nc.sbuf_tensor(_pref + "cr1", [128, 65 * 65], F32))
        Psb = E(nc.sbuf_tensor(_pref + "Psb", [128, T], F32))
        F0sb = E(nc.sbuf_tensor(_pref + "F0sb", [64, D], F32))
        F0T = E(nc.sbuf_tensor(_pref + "F0T", [128, 8 * 64], F32))
        Msb = E(nc.sbuf_tensor(_pref + "Msb", [128, 8 * 64], F32))
        w2sb = E(nc.sbuf_tensor(_pref + "w2sb", [128, 8], F32))
        wqbsb = E(nc.sbuf_tensor(_pref + "wqbsb", [128, 8], F32))
        rsb = E(nc.sbuf_tensor(_pref + "rsb", [64, 1], F32))
        msk = E(nc.sbuf_tensor(_pref + "msk", [128, 3 * 256], F32))
        o164 = E(nc.sbuf_tensor(_pref + "o164", [1, 64], F32))
        or128 = E(nc.sbuf_tensor(_pref + "or128", [1, 128], F32))
        w2bsb = E(nc.sbuf_tensor(_pref + "w2bsb", [1, 1], F32))
        w2bb = E(nc.sbuf_tensor(_pref + "w2bb", [128, 1], F32))
        ebv = E(nc.sbuf_tensor(_pref + "ebv", [1, 1], F32))
        idn = E(nc.sbuf_tensor(_pref + "idn", [128, 128], F32))
        esb = [E(nc.sbuf_tensor(_pref + "esb0", [128, 3 * 256], F32)),
               E(nc.sbuf_tensor(_pref + "esb1", [128, 3 * 256], F32))]
        zr = [E(nc.sbuf_tensor(_pref + "zr0", [1, 256], F32)),
              E(nc.sbuf_tensor(_pref + "zr1", [1, 256], F32))]
        rzb = [E(nc.sbuf_tensor(_pref + "rzb0", [64, 256], F32)),
               E(nc.sbuf_tensor(_pref + "rzb1", [64, 256], F32))]
        tmpc = [E(nc.sbuf_tensor(_pref + "tmpc0", [64, 256], F32)),
                E(nc.sbuf_tensor(_pref + "tmpc1", [64, 256], F32))]
        lnw = E(nc.sbuf_tensor(_pref + "lnw", [1, 1], F32))
        gtA = E(nc.sbuf_tensor(_pref + "gtA", [128, D], F32))
        btA = E(nc.sbuf_tensor(_pref + "btA", [128, D], F32))
        mu = E(nc.sbuf_tensor(_pref + "mu", [128, 8], F32))
        ssq = E(nc.sbuf_tensor(_pref + "ssq", [128, 4], F32))
        rstd = E(nc.sbuf_tensor(_pref + "rstd", [128, 4], F32))
        epsb = E(nc.sbuf_tensor(_pref + "epsb", [128, 1], F32))
        pS = [E(nc.psum_tensor(_pref + "pS0", [128, 3 * 256], F32)),
              E(nc.psum_tensor(_pref + "pS1", [128, 3 * 256], F32))]
        pM = [E(nc.psum_tensor(_pref + "pM0", [128, 512], F32)),
              E(nc.psum_tensor(_pref + "pM1", [128, 512], F32))]
        pV = E(nc.psum_tensor(_pref + "pV", [128, 1024], F32))
        ld = sems["ld"]
        lds = sems["lds"]
        ldf0 = sems["ldf0"]
        ldf1 = sems["ldf1"]
        ldr0 = sems["ldr0"]
        ldr1 = sems["ldr1"]
        pe = sems["pe"]
        dve = sems["dve"]
        act = sems["act"]
        st = sems["st"]
        ccb = [cc0, cc1]
        crb = [cr0, cr1]

        @_syn(nc)
        def _(s):
            s.dma_start(out=w2sb[:], in_=w2col).then_inc(lds, 16)
            s.dma_start(out=wqbsb[:], in_=wqbcol).then_inc(lds, 16)
            s.dma_start(out=msk[:], in_=maskM).then_inc(lds, 16)
            s.dma_start(out=o164[:], in_=ones164).then_inc(lds, 16)
            s.dma_start(out=or128[:], in_=onesr128).then_inc(lds, 16)
            s.dma_start(out=w2bsb[:], in_=w2b_in).then_inc(lds, 16)
            s.dma_start(out=idn[:], in_=ident).then_inc(lds, 16)
            s.dma_start(out=F0sb[:], in_=f_tm[0:W, :]).then_inc(lds, 16)
            s.dma_start(out=cc0[:], in_=cc_init).then_inc(lds, 16)
            s.dma_start(out=cc1[:], in_=cc_init).then_inc(lds, 16)
            s.dma_start(out=cr0[:], in_=cr_init).then_inc(lds, 16)
            s.dma_start(out=cr1[:], in_=cr_init).then_inc(lds, 16)
            s.dma_start(out=gtA[:], in_=gbcA).then_inc(lds, 16)
            s.dma_start(out=btA[:], in_=bbcA).then_inc(lds, 16)
        ctr.inc(lds, 14 * 16)
        lds_all = ctr.val(lds)
        @_gps(nc)
        def _(g):
            g.dma_start(out=F0T[:].rearrange("p (k j) -> p k j", j=64),
                        in_=fT.rearrange("(k p) t -> p k t", p=128)[:, :, 0:64]).then_inc(ld, 16)
        ctr.inc(ld, 16)
        ld_all = ctr.val(ld)

        @_vec(nc)
        def _(v):
            v.wait_ge(lds, lds_all)
            v.wait_ge(ld, ld_all)
            for k in range(8):
                v.tensor_scalar(F0T[:, k * 64:(k + 1) * 64], F0T[:, k * 64:(k + 1) * 64],
                                w2sb[:, k:k + 1], None, ALU.mult)
                v.drain()
            v.memset(epsb[:], LN_EPS)
            v.drain()
            v.memset(lnw[:], 0.0)
            v.drain()
            v.memset(Psb[64:128, :], 0.0)
            v.drain()
            v.engine_nop().then_inc(dve, 1)
        ctr.inc(dve, 1)
        prep_dve = ctr.val(dve)

        @_ten(nc)
        def _(t):
            t.wait_ge(dve, prep_dve)
            last = None
            for k in range(8):
                last = t.matmul(pM[0][0:64, 0:1], F0T[:, k * 64:(k + 1) * 64],
                                wqbsb[:, k:k + 1], start=(k == 0), stop=(k == 7))
            last.then_inc(pe, 1)
        ctr.inc(pe, 1)
        pev = ctr.val(pe)
        @_vec(nc)
        def _(v, pev=pev):
            v.wait_ge(pe, pev)
            v.tensor_copy(rsb[:], pM[0][0:64, 0:1]).then_inc(dve, 1)
        ctr.inc(dve, 1)
        r_dve = ctr.val(dve)

        @_ten(nc)
        def _(t):
            t.matmul(pM[0][:, 4:5], or128[:], w2bsb[:], start=True, stop=True).then_inc(pe, 1)
        ctr.inc(pe, 1)
        pev = ctr.val(pe)
        @_vec(nc)
        def _(v, pev=pev):
            v.wait_ge(pe, pev)
            v.tensor_copy(w2bb[:], pM[0][:, 4:5]).then_inc(dve, 1)
        ctr.inc(dve, 1)
        @_scl(nc)
        def _(s):
            s.wait_ge(dve, ctr.val(dve))
            s.activation(ebv[:], w2bsb[:], AF.Exp).then_inc(act, 1)
        ctr.inc(act, 1)

        es_mp = ExitStack()
        fsl = es_mp.enter_context(nc.sbuf_tensor(_pref + "fsl", [128, 8 * 512], F32))
        for m in range(8):
            par = 0
            dve_now = ctr.val(dve)
            @_syn(nc)
            def _(s, m=m, par=par, dve_now=dve_now):
                if m >= 1:
                    s.wait_ge(dve, dve_now)
                s.dma_start(out=fsl[:, par * 4096: par * 4096 + 1024].rearrange("p (k c) -> p k c", c=128),
                            in_=wq_T.rearrange("(k p) d -> p k d", p=128)[:, :, m * 128:(m + 1) * 128]
                            ).then_inc(ldf0 if par == 0 else ldf1, 16)
            ctr.inc(ldf0 if par == 0 else ldf1, 16)
            ldfv = ctr.val(ldf0 if par == 0 else ldf1)
            @_ten(nc)
            def _(t, m=m, par=par, ldfv=ldfv):
                t.wait_ge(ldf0 if par == 0 else ldf1, ldfv)
                last = None
                for kq in range(8):
                    last = t.matmul(pS[0][:, 0:64],
                                    fsl[:, par * 4096 + kq * 128: par * 4096 + kq * 128 + 128],
                                    F0T[:, kq * 64:(kq + 1) * 64],
                                    start=(kq == 0), stop=(kq == 7))
                last.then_inc(pe, 1)
            ctr.inc(pe, 1)
            pev = ctr.val(pe)
            @_vec(nc)
            def _(v, m=m, pev=pev):
                v.wait_ge(pe, pev)
                v.tensor_copy(Msb[:, m * 64:(m + 1) * 64], pS[0][:, 0:64]).then_inc(dve, 1)
            ctr.inc(dve, 1)
        m_done_dve = ctr.val(dve)

        for sidx in range(16):
            par = 0
            @_syn(nc)
            def _(s, sidx=sidx, par=par):
                s.wait_ge(dve, m_done_dve + sidx)
                s.dma_start(out=fsl[:, par * 4096: (par + 1) * 4096].rearrange("p (k c) -> p k c", c=512),
                            in_=fT.rearrange("(k p) t -> p k t", p=128)[:, :, sidx * 512:(sidx + 1) * 512]
                            ).then_inc(ldf0 if par == 0 else ldf1, 16)
            ctr.inc(ldf0 if par == 0 else ldf1, 16)
            ldfv = ctr.val(ldf0 if par == 0 else ldf1)
            @_ten(nc)
            def _(t, sidx=sidx, par=par, ldfv=ldfv):
                t.wait_ge(ldf0 if par == 0 else ldf1, ldfv)
                last = None
                for k in range(8):
                    last = t.matmul(pV[0:64, 0:512], Msb[:, k * 64:(k + 1) * 64],
                                    fsl[:, par * 4096 + k * 512: par * 4096 + (k + 1) * 512],
                                    start=(k == 0), stop=(k == 7))
                last.then_inc(pe, 1)
            ctr.inc(pe, 1)
            pev = ctr.val(pe)
            @_vec(nc)
            def _(v, sidx=sidx, pev=pev):
                v.wait_ge(pe, pev)
                v.tensor_scalar(Psb[0:64, sidx * 512:(sidx + 1) * 512], pV[0:64, 0:512],
                                rsb[:], None, ALU.add).then_inc(dve, 1)
            ctr.inc(dve, 1)
        p_done_dve = ctr.val(dve)
        es_mp.close()
        fr = E(nc.sbuf_tensor(_pref + "fr", [128, 2 * D], F32))
        th = E(nc.sbuf_tensor(_pref + "th", [128, 2 * D], F32))
        att = E(nc.sbuf_tensor(_pref + "att", [128, 4 * D], F32))
        sqs = E(nc.sbuf_tensor(_pref + "sqs", [128, D], F32))
        x1t = E(nc.sbuf_tensor(_pref + "x1t", [128, 2 * D], F32))

        t_tiles = []
        for g in range(16):
            blk = []
            base = 64 + g * 512
            for (o, wdt) in ((0, 256), (256, 256)):
                ta = base + o
                tw = min(wdt, T - ta)
                if tw > 0:
                    blk.append((ta, tw))
            t_tiles.append(blk)

        @_ten(nc)
        def _(t):
            t.wait_ge(dve, p_done_dve)

        for g in range(16):
            dve_blk = ctr.val(dve)
            evac_prev = [0, 0]
            for sw in range(NSWEEP):
                rd, wr = sw % 2, (sw + 1) % 2
                tiles = t_tiles[g]
                nt_ = len(tiles)
                pevs = []
                for i, (ta, tw) in enumerate(tiles):
                    ua = ta + 64
                    @_ten(nc)
                    def _(t, ta=ta, tw=tw, ua=ua, rd=rd, i=i, sw=sw, dve_blk=dve_blk):
                        if sw == 0 and i == 0:
                            t.wait_ge(dve, dve_blk)
                        last = None
                        for c in range(3):
                            u0 = ua - 128 + c * 128
                            t.matmul(pS[i][:, c * 256: c * 256 + tw],
                                     ccb[rd][:, u0:u0 + 128],
                                     Psb[:, ta:ta + tw],
                                     start=True, stop=False)
                            last = t.matmul(pS[i][:, c * 256: c * 256 + tw],
                                            idn[:],
                                            msk[:, c * 256: c * 256 + tw],
                                            start=False, stop=True)
                        last.then_inc(pe, 1)
                    ctr.inc(pe, 1)
                    pevs.append(ctr.val(pe))
                actvs = []
                for i, (ta, tw) in enumerate(tiles):
                    @_scl(nc)
                    def _(s, i=i, pev=pevs[i]):
                        s.wait_ge(pe, pev)
                        s.activation(esb[i][:], pS[i][:, 0:768], AF.Exp,
                                     bias=w2bb[:, 0:1]).then_inc(act, 1)
                    ctr.inc(act, 1)
                    actvs.append(ctr.val(act))
                pevs = []
                for i, (ta, tw) in enumerate(tiles):
                    ua = ta + 64
                    ev_g = evac_prev[i]
                    @_ten(nc)
                    def _(t, ua=ua, tw=tw, rd=rd, i=i, actv=actvs[i], ev_g=ev_g):
                        t.wait_ge(act, actv)
                        if ev_g > 0:
                            t.wait_ge(dve, ev_g)
                        last = None
                        for c in range(3):
                            cch = (ua - 128) // 128 + c
                            last = t.matmul(pM[i][0:65, 0:tw],
                                            crb[rd][:, cch * 65: cch * 65 + 65],
                                            esb[i][:, c * 256: c * 256 + tw],
                                            start=(c == 0), stop=(c == 2))
                        last.then_inc(pe, 1)
                    ctr.inc(pe, 1)
                    pevs.append(ctr.val(pe))
                actvs = []
                for i, (ta, tw) in enumerate(tiles):
                    @_scl(nc)
                    def _(s, tw=tw, i=i, pev=pevs[i]):
                        s.wait_ge(pe, pev)
                        s.activation(zr[i][:, 0:tw], pM[i][64:65, 0:tw], AF.Ln,
                                     bias=ebv[0:1, 0:1])
                        s.drain()
                        s.activation(zr[i][:, 0:tw], zr[i][:, 0:tw], AF.Exp,
                                     bias=lnw[0:1, 0:1], scale=-1.0).then_inc(act, 1)
                    ctr.inc(act, 1)
                    actvs.append(ctr.val(act))
                pevs = []
                for i, (ta, tw) in enumerate(tiles):
                    @_ten(nc)
                    def _(t, tw=tw, i=i, actv=actvs[i]):
                        t.wait_ge(act, actv)
                        t.matmul(pM[i][0:64, 256:256 + tw], o164[:], zr[i][:, 0:tw],
                                 start=True, stop=True).then_inc(pe, 1)
                    ctr.inc(pe, 1)
                    pevs.append(ctr.val(pe))
                actvs = []
                for i, (ta, tw) in enumerate(tiles):
                    @_scl(nc)
                    def _(s, tw=tw, i=i, pev=pevs[i]):
                        s.wait_ge(pe, pev)
                        s.copy(rzb[i][:, 0:tw], pM[i][0:64, 256:256 + tw]).then_inc(act, 1)
                    ctr.inc(act, 1)
                    actvs.append(ctr.val(act))
                dvevs = []
                for i, (ta, tw) in enumerate(tiles):
                    ua = ta + 64
                    @_vec(nc)
                    def _(v, ua=ua, tw=tw, wr=wr, rd=rd, i=i, actv=actvs[i]):
                        v.wait_ge(act, actv)
                        v.tensor_mul(tmpc[i][:, 0:tw], pM[i][0:64, 0:tw],
                                     rzb[i][:, 0:tw])
                        v.drain()
                        v.scalar_tensor_tensor(ccb[wr][0:64, ua:ua + tw],
                                               ccb[rd][0:64, ua:ua + tw],
                                               1.0 - float(np.float32(OMEGA)),
                                               tmpc[i][:, 0:tw],
                                               ALU.mult, ALU.add).then_inc(dve, 1)
                    ctr.inc(dve, 1)
                    dvevs.append(ctr.val(dve))
                for i, (ta, tw) in enumerate(tiles):
                    ua = ta + 64
                    for sub in range((tw + 127) // 128):
                        swd = min(128, tw - sub * 128)
                        dve_pre = ctr.val(dve)
                        @_ten(nc)
                        def _(t, ua=ua, sub=sub, swd=swd, wr=wr, i=i,
                              dvev=dvevs[i], dve_pre=dve_pre):
                            t.wait_ge(dve, dvev if sub == 0 else dve_pre)
                            t.transpose(pM[i][0:swd, 448:512],
                                        ccb[wr][0:64, ua + sub * 128: ua + sub * 128 + swd],
                                        idn[0:64, 0:64]).then_inc(pe, 1)
                        ctr.inc(pe, 1)
                        pev = ctr.val(pe)
                        @_vec(nc)
                        def _(v, ua=ua, sub=sub, swd=swd, wr=wr, i=i, pev=pev):
                            v.wait_ge(pe, pev)
                            cch = ua // 128 + sub
                            v.tensor_copy(crb[wr][0:swd, cch * 65: cch * 65 + 64],
                                          pM[i][0:swd, 448:512]).then_inc(dve, 1)
                        ctr.inc(dve, 1)
                        evac_prev[i] = ctr.val(dve)
            last_par = NSWEEP % 2
            u0b = 64 + g * 512 + 64
            u1b = min(u0b + 512, UW)
            dvev = ctr.val(dve)
            pev_all = ctr.val(pe)
            @_vec(nc)
            def _(v, u0b=u0b, u1b=u1b, last_par=last_par, dvev=dvev, pev_all=pev_all):
                v.wait_ge(dve, dvev)
                v.wait_ge(pe, pev_all)
                v.tensor_copy(ccb[1 - last_par][0:64, u0b:u1b], ccb[last_par][0:64, u0b:u1b])
                v.drain()
                c0, c1 = u0b // 128, (u1b + 127) // 128
                v.tensor_copy(crb[1 - last_par][:, c0 * 65: c1 * 65],
                              crb[last_par][:, c0 * 65: c1 * 65])
                v.drain()
                v.engine_nop().then_inc(dve, 1)
            ctr.inc(dve, 1)
        scan_dve = ctr.val(dve)
        ccf = ccb[NSWEEP % 2]

        @_ten(nc)
        def _(t):
            t.wait_ge(dve, scan_dve)
        tile_tanh_act = {}
        tile_mul_dve = {}
        tile_fadd_dve = {}
        tile_act_end = {}
        st_base_a = ctr.val(st)
        for it in range(64):
            tb = it * 128
            par = it % 2
            par4 = it % 4
            fguard = tile_fadd_dve.get(it - 2, 0)
            @_syn(nc)
            def _(s, it=it, par=par, fguard=fguard):
                if fguard > 0:
                    s.wait_ge(dve, fguard)
                elif it == 0:
                    s.wait_ge(dve, p_done_dve)
                ldr = ldr0 if par == 0 else ldr1
                s.dma_start(out=fr[:, par * D: par * D + D],
                            in_=f_tm[it * 128:(it + 1) * 128, :]).then_inc(ldr, 16)
            ctr.inc(ldr0 if par == 0 else ldr1, 16)
            ldrv = ctr.val(ldr0 if par == 0 else ldr1)
            vguard = tile_tanh_act.get(it - 2, 0)
            @_ten(nc)
            def _(t, tb=tb, par=par, vguard=vguard):
                if vguard > 0:
                    t.wait_ge(act, vguard)
                for vh in range(2):
                    pdst = pV[:, vh * 512:(vh + 1) * 512] if par == 0 \
                        else pS[vh][:, 0:512]
                    t.matmul(pdst, ccf[0:64, tb + 64: tb + 64 + 128],
                             F0sb[:, vh * 512:(vh + 1) * 512],
                             start=True, stop=True).then_inc(pe, 1)
            ctr.inc(pe, 2)
            pev = ctr.val(pe)
            tguard = tile_mul_dve.get(it - 2, 0)
            @_scl(nc)
            def _(s, par=par, pev=pev, tguard=tguard):
                s.wait_ge(pe, pev)
                if tguard > 0:
                    s.wait_ge(dve, tguard)
                if par == 0:
                    s.activation(th[:, 0:D], pV[:, 0:1024], AF.Tanh).then_inc(act, 1)
                else:
                    s.activation(th[:, D: D + 512], pS[0][:, 0:512], AF.Tanh)
                    s.drain()
                    s.activation(th[:, D + 512: 2 * D], pS[1][:, 0:512],
                                 AF.Tanh).then_inc(act, 1)
            ctr.inc(act, 1)
            tile_tanh_act[it] = ctr.val(act)
            actv = ctr.val(act)
            aguard = tile_act_end.get(it - 4, 0)
            @_vec(nc)
            def _(v, par=par, par4=par4, actv=actv, ldrv=ldrv, aguard=aguard):
                v.wait_ge(act, actv)
                v.wait_ge(ldr0 if par == 0 else ldr1, ldrv)
                if aguard > 0:
                    v.wait_ge(act, aguard)
                frt = fr[:, par * D: par * D + D]
                thp = th[:, par * D: par * D + D]
                v.scalar_tensor_tensor(att[:, par4 * D: par4 * D + D], thp, 1.0,
                                       frt, ALU.add, ALU.mult)
                v.drain()
                v.engine_nop().then_inc(dve, 1)
            ctr.inc(dve, 1)
            tile_mul_dve[it] = ctr.val(dve)
            attp = att[:, par4 * D: par4 * D + D]
            x1tp = x1t[:, par * D: par * D + D]
            cen1 = att[:, ((par4 + 1) % 4) * D: ((par4 + 1) % 4) * D + D]
            cen2 = att[:, ((par4 + 2) % 4) * D: ((par4 + 2) % 4) * D + D]
            mup = mu[:, par4: par4 + 1]
            tmp1p = mu[:, 4 + par4: 5 + par4]
            ssqp = ssq[:, par4: par4 + 1]
            rstdp = rstd[:, par4: par4 + 1]
            emit_ln_fast(nc, ctr, dve, act, attp, attp, cen1, mup, ssqp, rstdp, sqs,
                         epsb, tmp1p, gtA, btA)
            dvev = ctr.val(dve)
            st_w = st_base_a + (it - 1) * 16 if it >= 2 else 0
            @_vec(nc)
            def _(v, par=par, dvev=dvev, attp=attp, st_w=st_w):
                v.wait_ge(dve, dvev)
                if st_w > 0:
                    v.wait_ge(st, st_w)
                frt = fr[:, par * D: par * D + D]
                v.tensor_add(attp, attp, frt)
                v.drain()
                v.engine_nop().then_inc(dve, 1)
            ctr.inc(dve, 1)
            tile_fadd_dve[it] = ctr.val(dve)
            emit_ln_fast(nc, ctr, dve, act, attp, x1tp, cen2, mup, ssqp, rstdp, sqs,
                         epsb, tmp1p)
            tile_act_end[it] = ctr.val(act)
            dvev = ctr.val(dve)
            st_prev = ctr.val(st)
            @_gps(nc)
            def _(gg, it=it, dvev=dvev, x1tp=x1tp, st_prev=st_prev):
                gg.wait_ge(dve, dvev)
                if it > 0:
                    gg.wait_ge(st, st_prev)
                gg.dma_start(out=x1s[it * 128:(it + 1) * 128, :], in_=x1tp).then_inc(st, 16)
            ctr.inc(st, 16)
        @_gps(nc)
        def _(g):
            g.wait_ge(st, ctr.val(st))
    return nc


def build_stage_a():
    nc = bass.Bass("TRN2", target_bir_lowering=False, debug=False)
    io = {}
    io["fT"] = nc.dram_tensor("fT", [D, T], F32, kind="ExternalInput").ap()
    io["f_tm"] = nc.dram_tensor("f_tm", [T, D], F32, kind="ExternalInput").ap()
    io["wq_T"] = nc.dram_tensor("wq_T", [D, D], F32, kind="ExternalInput").ap()
    io["w2col"] = nc.dram_tensor("w2col", [128, 8], F32, kind="ExternalInput").ap()
    io["wqbcol"] = nc.dram_tensor("wqbcol", [128, 8], F32, kind="ExternalInput").ap()
    io["maskM"] = nc.dram_tensor("maskM", [128, 3 * 256], F32, kind="ExternalInput").ap()
    io["ones164"] = nc.dram_tensor("ones164", [1, 64], F32, kind="ExternalInput").ap()
    io["onesr128"] = nc.dram_tensor("onesr128", [1, 128], F32, kind="ExternalInput").ap()
    io["w2b_in"] = nc.dram_tensor("w2b_in", [1, 1], F32, kind="ExternalInput").ap()
    io["ident"] = nc.dram_tensor("ident", [128, 128], F32, kind="ExternalInput").ap()
    io["cc_init"] = nc.dram_tensor("cc_init", [128, UW], F32, kind="ExternalInput").ap()
    io["cr_init"] = nc.dram_tensor("cr_init", [128, 65 * 65], F32, kind="ExternalInput").ap()
    io["gbcA"] = nc.dram_tensor("gbcA", [128, D], F32, kind="ExternalInput").ap()
    io["bbcA"] = nc.dram_tensor("bbcA", [128, D], F32, kind="ExternalInput").ap()
    io["x1s"] = nc.dram_tensor("x1s", [T, D], F32, kind="ExternalOutput").ap()
    ctr = Ctr()
    sems = {}
    from contextlib import ExitStack
    with ExitStack() as ses:
        for sname in ["ld", "lds", "ldf0", "ldf1", "ldr0", "ldr1", "pe", "dve", "act", "st"]:
            sems[sname] = ses.enter_context(nc.semaphore(sname))
        emit_stage_a(nc, ctr, sems, io)
    return nc


def make_stage_a_consts():
    UWl = UW
    cc = np.zeros((128, UWl), np.float32)
    cc[0:64, 64:128] = np.eye(64, dtype=np.float32)
    cr = np.zeros((128, 65 * 65), np.float32)
    for ch in range(65):
        u0 = ch * 128
        for p in range(128):
            u = u0 + p
            if 64 <= u < 128:
                cr[p, ch * 65 + (u - 64)] = 1.0
        cr[:, ch * 65 + 64] = 1.0
    msk = np.full((128, 3 * 256), -1e9, np.float32)
    for c in range(3):
        for tr in range(256):
            for taur in range(128):
                dtt = tr - (taur + c * 128 - 128)
                if 0 < dtt <= 64:
                    msk[taur, c * 256 + tr] = 0.0
    return cc, cr, msk


def run_stage_a(inp):
    nc = build_stage_a()
    cc, cr, msk = make_stage_a_consts()
    f = inp["L0_attn_feature"]
    wq = inp["L0_attn_wq_w"]
    w2 = inp["L0_attn_w2_w"]
    wqb = inp["L0_attn_wq_b"]
    g_, b_ = inp["L0_attn_ln_g"], inp["L0_attn_ln_b"]
    in_maps = []
    for c in range(NCORE):
        bb = c % 2
        in_maps.append({
            "fT": np.ascontiguousarray(f[bb].T),
            "f_tm": np.ascontiguousarray(f[bb]),
            "wq_T": np.ascontiguousarray(wq.T),
            "w2col": np.ascontiguousarray(w2.reshape(8, 128).T),
            "wqbcol": np.ascontiguousarray(wqb.reshape(8, 128).T),
            "maskM": msk,
            "ones164": np.full((1, 64), np.float32(OMEGA), np.float32),
            "onesr128": np.ones((1, 128), np.float32),
            "w2b_in": np.asarray(inp["L0_attn_w2_b"], dtype=np.float32).reshape(1, 1),
            "ident": np.eye(128, dtype=np.float32),
            "cc_init": cc,
            "cr_init": cr,
            "gbcA": np.ascontiguousarray(np.broadcast_to(g_, (128, D))),
            "bbcA": np.ascontiguousarray(np.broadcast_to(b_, (128, D))),
        })
    res = run_bass_kernel_spmd(nc, in_maps, core_ids=list(range(NCORE)))
    x1 = np.stack([np.asarray(res.results[0]["x1s"]), np.asarray(res.results[1]["x1s"])], 0)
    return x1.reshape(-1, D)




def emit_stage_b(nc, ctr, sems, io):
    _pref = io.get("prefix", "b_")
    (x1full, bgidx, bjunk, rw_T, rb_bc, iota32, iotac, lt128, ones_col, ones_row,
     ident, w1_t, b1c, w2_t, b2bc, x2s, xstage, tokstage, ydram) = (
        io["x1full"], io["bgidx"], io["bjunk"], io["rw_T"], io["rb_bc"], io["iota32"],
        io["iotac"], io["lt128"], io["ones_col"], io["ones_row"], io["ident"],
        io["w1_t"], io["b1c"], io["w2_t"], io["b2bc"], io["x2s"],
        io["xstage"], io["tokstage"], io["ydram"])
    NSLOT = 32 * CAP_B
    NTOK = io["ntok"]
    NX1 = io["nx1"]
    NT = NTOK // 128
    from contextlib import ExitStack
    with ExitStack() as es:
        E = es.enter_context
        rwt = E(nc.sbuf_tensor(_pref + "rwt", [128, 8 * 32], F32))
        rbt = E(nc.sbuf_tensor(_pref + "rbt", [128, 32], F32))
        io32 = E(nc.sbuf_tensor(_pref + "io32", [128, 32], F32))
        ioc = E(nc.sbuf_tensor(_pref + "ioc", [128, 1], F32))
        tcol = E(nc.sbuf_tensor(_pref + "tcol", [128, 1], F32))
        ltt = E(nc.sbuf_tensor(_pref + "ltt", [128, 128], F32))
        onc = E(nc.sbuf_tensor(_pref + "onc", [128, 1], F32))
        onr = E(nc.sbuf_tensor(_pref + "onr", [1, 128], F32))
        idn = E(nc.sbuf_tensor(_pref + "idn", [128, 128], F32))
        hug = E(nc.sbuf_tensor(_pref + "hug", [128, NSLOT // 128], F32))
        zt = E(nc.sbuf_tensor(_pref + "zt", [128, D], F32))
        xsl = E(nc.sbuf_tensor(_pref + "xsl", [128, 8 * 128], F32))
        idxt = E(nc.sbuf_tensor(_pref + "idxt", [128, 2], I32))
        jkt = E(nc.sbuf_tensor(_pref + "jkt", [128, 1], F32))
        x1r = E(nc.sbuf_tensor(_pref + "x1r", [128, 2 * D], F32))
        lg = E(nc.sbuf_tensor(_pref + "lg", [128, 32], F32))
        mx8 = E(nc.sbuf_tensor(_pref + "mx8", [128, 8], F32))
        ix8 = E(nc.sbuf_tensor(_pref + "ix8", [128, 8], U32))
        ef32 = E(nc.sbuf_tensor(_pref + "ef32", [128, 1], F32))
        eq = E(nc.sbuf_tensor(_pref + "eq", [128, 32], F32))
        bb32 = E(nc.sbuf_tensor(_pref + "bb32", [128, 32], F32))
        tmp32 = E(nc.sbuf_tensor(_pref + "tmp32", [128, 32], F32))
        basev = E(nc.sbuf_tensor(_pref + "basev", [1, 32], F32))
        rankv = E(nc.sbuf_tensor(_pref + "rankv", [128, 1], F32))
        posv = E(nc.sbuf_tensor(_pref + "posv", [128, 1], F32))
        ovf = E(nc.sbuf_tensor(_pref + "ovf", [128, 1], F32))
        posi = E(nc.sbuf_tensor(_pref + "posi", [128, 1], I32))
        scr32 = E(nc.sbuf_tensor(_pref + "scr32", [128, 32], F32))
        xg = E(nc.sbuf_tensor(_pref + "xg", [128, D], F32))
        xgT = E(nc.sbuf_tensor(_pref + "xgT", [128, 8 * 128], BF16))
        w1sb = E(nc.sbuf_tensor(_pref + "w1sb", [128, 8 * 256], BF16))
        w2sb = E(nc.sbuf_tensor(_pref + "w2sb", [128, 2 * D], BF16))
        b1sb = E(nc.sbuf_tensor(_pref + "b1sb", [128, 2], F32))
        b2sb = E(nc.sbuf_tensor(_pref + "b2sb", [128, D], F32))
        hg = E(nc.sbuf_tensor(_pref + "hg", [128, 2 * 128], BF16))
        ysb = E(nc.sbuf_tensor(_pref + "ysb", [128, D], F32))
        tif = E(nc.sbuf_tensor(_pref + "tif", [128, 1], F32))
        ti32 = E(nc.sbuf_tensor(_pref + "ti32", [128, 1], I32))
        att = E(nc.sbuf_tensor(_pref + "att", [128, 2 * D], F32))
        sqs = E(nc.sbuf_tensor(_pref + "sqs", [128, D], F32))
        x2t = E(nc.sbuf_tensor(_pref + "x2t", [128, 2 * D], F32))
        yr = E(nc.sbuf_tensor(_pref + "yr", [128, 2 * D], F32))
        mu = E(nc.sbuf_tensor(_pref + "mu", [128, 4], F32))
        ssq = E(nc.sbuf_tensor(_pref + "ssq", [128, 2], F32))
        rstd = E(nc.sbuf_tensor(_pref + "rstd", [128, 2], F32))
        epsb = E(nc.sbuf_tensor(_pref + "epsb", [128, 1], F32))
        pL = E(nc.psum_tensor(_pref + "pL", [128, 128], F32))
        pQ = E(nc.psum_tensor(_pref + "pQ", [128, 1024], F32))
        pH = E(nc.psum_tensor(_pref + "pH", [128, 256], F32))
        lds = sems["lds"]
        ldq = sems["ldq"]
        ldg = sems["ldg"]
        sct = sems["sct"]
        yst = sems["yst"]
        zst = sems["zst"]
        gth = sems["gth"]
        pe = sems["pe"]
        dve = sems["dve"]
        act = sems["act"]
        st = sems["st"]
        rb_slot = nc.gpsimd.to_reg(NSLOT)
        rb_tok = nc.gpsimd.to_reg(NTOK)
        rb_x1 = nc.gpsimd.to_reg(NX1)

        @_syn(nc)
        def _(s):
            s.dma_start(out=rwt[:].rearrange("p (k e) -> p k e", e=32),
                        in_=rw_T.rearrange("(k p) e -> p k e", p=128)).then_inc(lds, 16)
            s.dma_start(out=rbt[:], in_=rb_bc).then_inc(lds, 16)
            s.dma_start(out=io32[:], in_=iota32).then_inc(lds, 16)
            s.dma_start(out=ioc[:], in_=iotac).then_inc(lds, 16)
            s.dma_start(out=ltt[:], in_=lt128).then_inc(lds, 16)
            s.dma_start(out=onc[:], in_=ones_col).then_inc(lds, 16)
            s.dma_start(out=onr[:], in_=ones_row).then_inc(lds, 16)
            s.dma_start(out=idn[:], in_=ident).then_inc(lds, 16)
        ctr.inc(lds, 8 * 16)
        lds_all = ctr.val(lds)

        @_vec(nc)
        def _(v):
            v.memset(hug[:], 1.0e9)
            v.drain()
            v.memset(zt[:], 0.0)
            v.drain()
            v.memset(basev[:], 0.0)
            v.drain()
            v.memset(epsb[:], LN_EPS)
            v.drain()
            v.engine_nop().then_inc(dve, 1)
        ctr.inc(dve, 1)
        ms_dve = ctr.val(dve)

        @_syn(nc)
        def _(s):
            s.wait_ge(dve, ms_dve)
            s.dma_start(out=tokstage.rearrange("(b a) c -> b (a c)", b=128),
                        in_=hug[:]).then_inc(zst, 16)
            for i in range(NT):
                s.dma_start(out=ydram[i * 128:(i + 1) * 128, :], in_=zt[:]).then_inc(zst, 16)
            for i in range(NSLOT // 128):
                s.dma_start(out=xstage[i * 128:(i + 1) * 128, :], in_=zt[:]).then_inc(zst, 16)
        ctr.inc(zst, 16 * (1 + NT + NSLOT // 128))
        zst_all = ctr.val(zst)

        sct_rt0 = ctr.val(sct)
        for it in range(NT):
            sct_prev = sct_rt0 + 32 * it
            sct_base = ctr.val(sct)
            @_syn(nc)
            def _(s, it=it, sct_prev=sct_prev):
                if it > 0:
                    s.wait_ge(sct, sct_prev)
                s.dma_start(out=idxt[:, 0:1], in_=bgidx[it * 128:(it + 1) * 128, :]).then_inc(ldq, 16)
                s.dma_start(out=jkt[:], in_=bjunk[it * 128:(it + 1) * 128, :]).then_inc(ldq, 16)
            ctr.inc(ldq, 32)
            ldqv = ctr.val(ldq)
            dve_m = ctr.val(dve)
            @_vec(nc)
            def _(v, ldqv=ldqv, it=it, sct_prev=sct_prev):
                v.wait_ge(ldq, ldqv)
                if it > 0:
                    v.wait_ge(sct, sct_prev)
                v.memset(x1r[:, 0:D], 0.0)
                v.drain()
                v.engine_nop().then_inc(dve, 1)
            ctr.inc(dve, 1)
            dvev = ctr.val(dve)
            @_gps(nc)
            def _(g, dvev=dvev):
                g.wait_ge(dve, dvev)
                g.indirect_dma_start(x1r[:, 0:D], None, x1full,
                                     IndirectOffsetOnAxis(ap=idxt[:, 0:1], axis=0),
                                     bounds_check=rb_x1, oob_is_err=False).then_inc(gth, 16)
            ctr.inc(gth, 16)
            gthv = ctr.val(gth)
            for kd in range(8):
                dve_t = ctr.val(dve)
                @_ten(nc)
                def _(t, kd=kd, gthv=gthv, dve_t=dve_t):
                    if kd == 0:
                        t.wait_ge(lds, lds_all)
                        t.wait_ge(gth, gthv)
                    else:
                        t.wait_ge(dve, dve_t)
                    t.transpose(pL[:, 0:128], x1r[:, kd * 128:(kd + 1) * 128],
                                idn[:]).then_inc(pe, 1)
                ctr.inc(pe, 1)
                pev = ctr.val(pe)
                @_vec(nc)
                def _(v, kd=kd, pev=pev):
                    v.wait_ge(pe, pev)
                    v.tensor_copy(xsl[:, kd * 128:(kd + 1) * 128], pL[:, 0:128]).then_inc(dve, 1)
                ctr.inc(dve, 1)
            dvev = ctr.val(dve)
            @_ten(nc)
            def _(t, dvev=dvev):
                t.wait_ge(dve, dvev)
                last = None
                for kd in range(8):
                    last = t.matmul(pL[:, 0:32], xsl[:, kd * 128:(kd + 1) * 128],
                                    rwt[:, kd * 32:(kd + 1) * 32],
                                    start=(kd == 0), stop=(kd == 7))
                last.then_inc(pe, 1)
            ctr.inc(pe, 1)
            pev = ctr.val(pe)
            @_vec(nc)
            def _(v, it=it, pev=pev):
                v.wait_ge(pe, pev)
                v.tensor_add(lg[:], pL[:, 0:32], rbt[:])
                v.drain()
                v.max(mx8[:], lg[:])
                v.drain()
                v.max_index(ix8[:], mx8[:], lg[:])
                v.drain()
                v.tensor_copy(ef32[:], ix8[:, 0:1])
                v.drain()
                v.tensor_scalar(eq[:], io32[:], ef32[:], None, ALU.is_equal)
                v.drain()
                v.tensor_scalar(tcol[:], ioc[:], float(it * 128), None, ALU.add)
                v.drain()
                v.engine_nop().then_inc(dve, 1)
            ctr.inc(dve, 1)
            dvev = ctr.val(dve)
            @_ten(nc)
            def _(t, dvev=dvev):
                t.wait_ge(dve, dvev)
                t.matmul(pL[:, 32:64], ltt[:], eq[:], start=True, stop=True)
                t.matmul(pL[:, 64:96], onr[:], basev[:], start=True, stop=True)
                t.matmul(pL[0:1, 96:128], onc[:], eq[:], start=True, stop=True).then_inc(pe, 1)
            ctr.inc(pe, 1)
            pev = ctr.val(pe)
            @_scl(nc)
            def _(s, pev=pev):
                s.wait_ge(pe, pev)
                s.copy(bb32[:], pL[:, 64:96]).then_inc(act, 1)
            ctr.inc(act, 1)
            actv = ctr.val(act)
            @_vec(nc)
            def _(v, actv=actv):
                v.wait_ge(act, actv)
                v.tensor_add(tmp32[:], pL[:, 32:64], bb32[:])
                v.drain()
                v.scalar_tensor_tensor(scr32[:], tmp32[:], 1.0, eq[:], ALU.mult, ALU.mult,
                                       accum_out=rankv[:])
                v.drain()
                v.tensor_scalar(posv[:], ef32[:], float(CAP_B), None, ALU.mult)
                v.drain()
                v.tensor_add(posv[:], posv[:], rankv[:])
                v.drain()
                v.tensor_add(posv[:], posv[:], jkt[:])
                v.drain()
                v.tensor_scalar(ovf[:], rankv[:], float(CAP_B), None, ALU.is_ge)
                v.drain()
                v.scalar_tensor_tensor(posv[:], ovf[:], 1.0e9, posv[:], ALU.mult, ALU.add)
                v.drain()
                v.tensor_copy(posi[:], posv[:])
                v.drain()
                v.tensor_add(basev[:], basev[:], pL[0:1, 96:128])
                v.drain()
                v.engine_nop().then_inc(dve, 1)
            ctr.inc(dve, 1)
            dvev = ctr.val(dve)
            @_gps(nc)
            def _(g, dvev=dvev):
                g.wait_ge(zst, zst_all)
                g.wait_ge(dve, dvev)
                g.indirect_dma_start(
                    xstage, IndirectOffsetOnAxis(ap=posi[:], axis=0), x1r[:, 0:D], None,
                    bounds_check=rb_slot, oob_is_err=False,
                ).then_inc(sct, 16)
                g.indirect_dma_start(
                    tokstage, IndirectOffsetOnAxis(ap=posi[:], axis=0), tcol[:], None,
                    bounds_check=rb_slot, oob_is_err=False,
                ).then_inc(sct, 16)
            ctr.inc(sct, 32)
        sct_all = ctr.val(sct)

        NSUB = (CAP_B + 127) // 128
        for e in range(32):
            dve_now = ctr.val(dve)
            @_gps(nc)
            def _(g, e=e, dve_now=dve_now):
                if e == 0:
                    g.wait_ge(sct, sct_all)
                else:
                    g.wait_ge(dve, dve_now)
                g.dma_start(out=w1sb[:].rearrange("p (k h) -> p k h", h=256),
                            in_=w1_t[e].rearrange("(k p) h -> p k h", p=128)).then_inc(ldg, 16)
                g.dma_start(out=w2sb[:].rearrange("p (k d) -> p k d", d=D),
                            in_=w2_t[e].rearrange("(k p) d -> p k d", p=128)).then_inc(ldg, 16)
            ctr.inc(ldg, 32)
            ldgv = ctr.val(ldg)
            @_syn(nc)
            def _(s, e=e, dve_now=dve_now):
                s.wait_ge(sct, sct_all)
                if e > 0:
                    s.wait_ge(dve, dve_now)
                s.dma_start(out=b1sb[:], in_=b1c[e]).then_inc(ldq, 16)
                s.dma_start(out=b2sb[:], in_=b2bc[e]).then_inc(ldq, 16)
            ctr.inc(ldq, 32)
            for sub in range(NSUB):
                stt = min(128, CAP_B - sub * 128)
                base_slot = e * CAP_B + sub * 128
                dve_sub = ctr.val(dve)
                @_syn(nc)
                def _(s, base_slot=base_slot, stt=stt, sub=sub, dve_sub=dve_sub, edve=dve_now):
                    s.wait_ge(sct, sct_all)
                    if sub > 0:
                        s.wait_ge(dve, dve_sub)
                    elif edve > 0:
                        s.wait_ge(dve, edve)
                    s.dma_start(out=xg[0:stt, :], in_=xstage[base_slot:base_slot + stt, :]).then_inc(ldq, 16)
                    s.dma_start(out=tif[0:stt, :], in_=tokstage[base_slot:base_slot + stt, :]).then_inc(ldq, 16)
                ctr.inc(ldq, 32)
                ldqv = ctr.val(ldq)
                for kd in range(8):
                    dve_t = ctr.val(dve)
                    @_ten(nc)
                    def _(t, stt=stt, kd=kd, ldqv=ldqv, dve_t=dve_t):
                        if kd == 0:
                            t.wait_ge(ldq, ldqv)
                        else:
                            t.wait_ge(dve, dve_t)
                        t.transpose(pL[0:128, 0:stt], xg[0:stt, kd * 128:(kd + 1) * 128],
                                    idn[0:stt, 0:stt]).then_inc(pe, 1)
                    ctr.inc(pe, 1)
                    pev = ctr.val(pe)
                    @_vec(nc)
                    def _(v, stt=stt, kd=kd, pev=pev):
                        v.wait_ge(pe, pev)
                        v.tensor_copy(xgT[:, kd * 128: kd * 128 + stt],
                                      pL[0:128, 0:stt]).then_inc(dve, 1)
                    ctr.inc(dve, 1)
                dvev = ctr.val(dve)
                @_ten(nc)
                def _(t, stt=stt, dvev=dvev, ldgv=ldgv):
                    t.wait_ge(dve, dvev)
                    t.wait_ge(ldg, ldgv)
                    last = None
                    for hc in range(2):
                        for kd in range(8):
                            last = t.matmul(pH[:, hc * 128: hc * 128 + stt],
                                            w1sb[:, kd * 256 + hc * 128: kd * 256 + hc * 128 + 128],
                                            xgT[:, kd * 128: kd * 128 + stt],
                                            start=(kd == 0), stop=(kd == 7))
                    last.then_inc(pe, 1)
                ctr.inc(pe, 1)
                pev = ctr.val(pe)
                @_scl(nc)
                def _(s, stt=stt, pev=pev):
                    s.wait_ge(pe, pev)
                    s.activation(hg[:, 0:stt], pH[:, 0:stt], AF.Relu, bias=b1sb[:, 0:1])
                    s.drain()
                    s.activation(hg[:, 128:128 + stt], pH[:, 128:128 + stt], AF.Relu,
                                 bias=b1sb[:, 1:2]).then_inc(act, 1)
                ctr.inc(act, 1)
                actv = ctr.val(act)
                @_ten(nc)
                def _(t, stt=stt, actv=actv):
                    t.wait_ge(act, actv)
                    last = None
                    for dh in range(2):
                        for kh in range(2):
                            last = t.matmul(pQ[0:stt, dh * 512: dh * 512 + 512],
                                            hg[:, kh * 128: kh * 128 + stt],
                                            w2sb[:, kh * D + dh * 512: kh * D + dh * 512 + 512],
                                            start=(kh == 0), stop=(kh == 1))
                    last.then_inc(pe, 1)
                ctr.inc(pe, 1)
                pev = ctr.val(pe)
                yst_prev = ctr.val(yst)
                @_vec(nc)
                def _(v, stt=stt, pev=pev, yst_prev=yst_prev):
                    v.wait_ge(pe, pev)
                    if yst_prev > 0:
                        v.wait_ge(yst, yst_prev)
                    v.scalar_tensor_tensor(ysb[0:stt, :], pQ[0:stt, :], 1.0,
                                           b2sb[0:stt, :], ALU.mult, ALU.add)
                    v.drain()
                    v.tensor_scalar_max(ysb[0:stt, :], ysb[0:stt, :], 0.0)
                    v.drain()
                    v.tensor_copy(ti32[0:stt, :], tif[0:stt, :])
                    v.drain()
                    v.engine_nop().then_inc(dve, 1)
                ctr.inc(dve, 1)
                dvev = ctr.val(dve)
                @_gps(nc)
                def _(g, stt=stt, dvev=dvev):
                    g.wait_ge(dve, dvev)
                    g.indirect_dma_start(
                        ydram, IndirectOffsetOnAxis(ap=ti32[0:stt, :], axis=0),
                        ysb[0:stt, :], None,
                        bounds_check=rb_tok, oob_is_err=False,
                    ).then_inc(yst, 16)
                ctr.inc(yst, 16)
        yst_all = ctr.val(yst)

        st_base_f = ctr.val(st)
        fin_gth = {}
        fin_add = {}
        for it in range(NT):
            par = it % 2
            ldq_prev = ctr.val(ldq)
            gguard = fin_gth.get(it - 2, 0)
            aguard2 = fin_add.get(it - 2, 0)
            @_syn(nc)
            def _(s, it=it, par=par, ldq_prev=ldq_prev, gguard=gguard, aguard2=aguard2):
                if it == 0:
                    s.wait_ge(yst, yst_all)
                else:
                    s.wait_ge(ldq, ldq_prev)
                if gguard > 0:
                    s.wait_ge(gth, gguard)
                if aguard2 > 0:
                    s.wait_ge(dve, aguard2)
                s.dma_start(out=yr[:, par * D: par * D + D],
                            in_=ydram[it * 128:(it + 1) * 128, :]).then_inc(ldq, 16)
                s.dma_start(out=idxt[:, par: par + 1],
                            in_=bgidx[it * 128:(it + 1) * 128, :]).then_inc(ldq, 16)
            ctr.inc(ldq, 32)
            ldqv = ctr.val(ldq)
            @_vec(nc)
            def _(v, par=par, ldqv=ldqv):
                v.wait_ge(ldq, ldqv)
                v.memset(x1r[:, par * D: par * D + D], 0.0)
                v.drain()
                v.engine_nop().then_inc(dve, 1)
            ctr.inc(dve, 1)
            dvev = ctr.val(dve)
            gth_prev = ctr.val(gth)
            @_gps(nc)
            def _(g, par=par, dvev=dvev, gth_prev=gth_prev, it=it):
                g.wait_ge(dve, dvev)
                if it > 0:
                    g.wait_ge(gth, gth_prev)
                g.indirect_dma_start(x1r[:, par * D: par * D + D], None, x1full,
                                     IndirectOffsetOnAxis(ap=idxt[:, par: par + 1], axis=0),
                                     bounds_check=rb_x1, oob_is_err=False).then_inc(gth, 16)
            ctr.inc(gth, 16)
            gthv = ctr.val(gth)
            fin_gth[it] = gthv
            st_w = st_base_f + (it - 1) * 16 if it >= 2 else 0
            @_vec(nc)
            def _(v, par=par, gthv=gthv, st_w=st_w):
                v.wait_ge(gth, gthv)
                if st_w > 0:
                    v.wait_ge(st, st_w)
                v.tensor_add(att[:, par * D: par * D + D],
                             x1r[:, par * D: par * D + D],
                             yr[:, par * D: par * D + D])
                v.drain()
                v.engine_nop().then_inc(dve, 1)
            ctr.inc(dve, 1)
            fin_add[it] = ctr.val(dve)
            emit_ln_fast(nc, ctr, dve, act,
                         att[:, par * D: par * D + D],
                         x2t[:, par * D: par * D + D],
                         x1r[:, par * D: par * D + D],
                         mu[:, par: par + 1], ssq[:, par: par + 1],
                         rstd[:, par: par + 1], sqs, epsb,
                         mu[:, 2 + par: 3 + par])
            dvev = ctr.val(dve)
            st_prev = ctr.val(st)
            @_gps(nc)
            def _(g, it=it, par=par, dvev=dvev, st_prev=st_prev):
                g.wait_ge(dve, dvev)
                if it > 0:
                    g.wait_ge(st, st_prev)
                g.dma_start(out=x2s[it * 128:(it + 1) * 128, :],
                            in_=x2t[:, par * D: par * D + D]).then_inc(st, 16)
            ctr.inc(st, 16)
        @_gps(nc)
        def _(g):
            g.wait_ge(st, ctr.val(st))
    return nc


def build_stage_b():
    nc = bass.Bass("TRN2", target_bir_lowering=False, debug=False)
    NSLOT = 32 * CAP_B
    io = {}
    io["ntok"] = TPC
    io["nx1"] = TPC
    io["x1full"] = nc.dram_tensor("x1full", [TPC, D], F32, kind="ExternalInput").ap()
    io["bgidx"] = nc.dram_tensor("bgidx", [TPC, 1], I32, kind="ExternalInput").ap()
    io["bjunk"] = nc.dram_tensor("bjunk", [TPC, 1], F32, kind="ExternalInput").ap()
    io["rw_T"] = nc.dram_tensor("rw_T", [D, 32], F32, kind="ExternalInput").ap()
    io["rb_bc"] = nc.dram_tensor("rb_bc", [128, 32], F32, kind="ExternalInput").ap()
    io["iota32"] = nc.dram_tensor("iota32", [128, 32], F32, kind="ExternalInput").ap()
    io["iotac"] = nc.dram_tensor("iotac", [128, 1], F32, kind="ExternalInput").ap()
    io["lt128"] = nc.dram_tensor("lt128", [128, 128], F32, kind="ExternalInput").ap()
    io["ones_col"] = nc.dram_tensor("ones_col", [128, 1], F32, kind="ExternalInput").ap()
    io["ones_row"] = nc.dram_tensor("ones_row", [1, 128], F32, kind="ExternalInput").ap()
    io["ident"] = nc.dram_tensor("ident", [128, 128], F32, kind="ExternalInput").ap()
    io["w1_t"] = nc.dram_tensor("w1_t", [32, D, 256], F32, kind="ExternalInput").ap()
    io["b1c"] = nc.dram_tensor("b1c", [32, 128, 2], F32, kind="ExternalInput").ap()
    io["w2_t"] = nc.dram_tensor("w2_t", [32, 256, D], F32, kind="ExternalInput").ap()
    io["b2bc"] = nc.dram_tensor("b2bc", [32, 128, D], F32, kind="ExternalInput").ap()
    io["x2s"] = nc.dram_tensor("x2s", [TPC, D], F32, kind="ExternalOutput").ap()
    io["xstage"] = nc.dram_tensor("xstage", [NSLOT, D], F32, kind="Internal").ap()
    io["tokstage"] = nc.dram_tensor("tokstage", [NSLOT, 1], F32, kind="Internal").ap()
    io["ydram"] = nc.dram_tensor("ydram", [TPC, D], F32, kind="Internal").ap()
    ctr = Ctr()
    sems = {}
    from contextlib import ExitStack
    with ExitStack() as ses:
        for sname in ["lds", "ldq", "ldg", "sct", "yst", "zst", "gth", "pe", "dve", "act", "st"]:
            sems[sname] = ses.enter_context(nc.semaphore(sname))
        emit_stage_b(nc, ctr, sems, io)
    return nc




def run_stage_b(inp, x1_full):
    nc = build_stage_b()
    lt = np.tril(np.ones((128, 128), np.float32), -1)
    shared = {
        "rw_T": np.ascontiguousarray(inp["L0_ffn_router_w"].T),
        "rb_bc": np.ascontiguousarray(np.broadcast_to(inp["L0_ffn_router_b"], (128, 32))),
        "iota32": np.ascontiguousarray(np.broadcast_to(np.arange(32, dtype=np.float32), (128, 32))),
        "iotac": np.arange(128, dtype=np.float32).reshape(128, 1),
        "lt128": lt,
        "ones_col": np.ones((128, 1), np.float32),
        "ones_row": np.ones((1, 128), np.float32),
        "ident": np.eye(128, dtype=np.float32),
        "w1_t": np.ascontiguousarray(inp["L0_ffn_weights1"]),
        "b1c": np.ascontiguousarray(inp["L0_ffn_bias1"].reshape(32, 2, 128).transpose(0, 2, 1)),
        "w2_t": np.ascontiguousarray(inp["L0_ffn_weights2"]),
        "b2bc": np.ascontiguousarray(np.broadcast_to(inp["L0_ffn_bias2"][:, None, :], (32, 128, 1024))),
        }
    in_maps = []
    for c in range(NCORE):
        sl = x1_full[c * TPC:(c + 1) * TPC]
        in_maps.append(dict(shared, x1s=np.ascontiguousarray(sl),
                            x1sT=np.ascontiguousarray(sl.T)))
    res = run_bass_kernel_spmd(nc, in_maps, core_ids=list(range(NCORE)))
    return np.concatenate([np.asarray(r["x2s"]) for r in res.results], 0)




def emit_stage_d(nc, ctr, sems, io):
    _pref = io.get("prefix", "d_")
    (x3s, gw_T, gb_bc, iota8, iotac, lt128, ones_col, ones_row, ident,
     w1_t, w3_t, w2_t, sw1_t, sw3_t, sw2_t, x4s,
     xstage, tokstage, gstage, y1dram, y2dram, zdram) = (
        io["x3s"], io["gw_T"], io["gb_bc"], io["iota8"], io["iotac"],
        io["lt128"], io["ones_col"], io["ones_row"], io["ident"],
        io["w1_t"], io["w3_t"], io["w2_t"], io["sw1_t"], io["sw3_t"], io["sw2_t"],
        io["x4s"], io["xstage"], io["tokstage"], io["gstage"],
        io["y1dram"], io["y2dram"], io["zdram"])
    NSLOT = 16 * CAP_D
    NT = TPC // 128
    from contextlib import ExitStack
    with ExitStack() as es:
        E = es.enter_context
        gwt = E(nc.sbuf_tensor(_pref + "gwt", [128, 8 * 8], F32))
        gbt = E(nc.sbuf_tensor(_pref + "gbt", [128, 8], F32))
        io8 = E(nc.sbuf_tensor(_pref + "io8", [128, 8], F32))
        ioc = E(nc.sbuf_tensor(_pref + "ioc", [128, 1], F32))
        tcol = E(nc.sbuf_tensor(_pref + "tcol", [128, 1], F32))
        gcol = E(nc.sbuf_tensor(_pref + "gcol", [128, 1], F32))
        ltt = E(nc.sbuf_tensor(_pref + "ltt", [128, 128], F32))
        onc = E(nc.sbuf_tensor(_pref + "onc", [128, 1], F32))
        onr = E(nc.sbuf_tensor(_pref + "onr", [1, 128], F32))
        idn = E(nc.sbuf_tensor(_pref + "idn", [128, 128], F32))
        hug = E(nc.sbuf_tensor(_pref + "hug", [128, NSLOT // 128], F32))
        zt = E(nc.sbuf_tensor(_pref + "zt", [128, D], F32))
        xsl = E(nc.sbuf_tensor(_pref + "xsl", [128, 8 * 128], F32))
        x3T = E(nc.sbuf_tensor(_pref + "x3T", [128, 8 * TPC], BF16))
        x3r = E(nc.sbuf_tensor(_pref + "x3r", [128, 2 * D], F32))
        lg = E(nc.sbuf_tensor(_pref + "lg", [128, 8], F32))
        nmx = E(nc.sbuf_tensor(_pref + "nmx", [128, 1], F32))
        e8 = E(nc.sbuf_tensor(_pref + "e8", [128, 8], F32))
        sumv = E(nc.sbuf_tensor(_pref + "sumv", [128, 1], F32))
        rz1 = E(nc.sbuf_tensor(_pref + "rz1", [128, 1], F32))
        probs = E(nc.sbuf_tensor(_pref + "probs", [128, 8], F32))
        scb = E(nc.sbuf_tensor(_pref + "scb", [128, 8], F32))
        mx8 = E(nc.sbuf_tensor(_pref + "mx8", [128, 8], F32))
        ix8 = E(nc.sbuf_tensor(_pref + "ix8", [128, 8], U32))
        ef32 = E(nc.sbuf_tensor(_pref + "ef32", [128, 1], F32))
        eq8 = E(nc.sbuf_tensor(_pref + "eq8", [128, 8], F32))
        bb8 = E(nc.sbuf_tensor(_pref + "bb8", [128, 8], F32))
        tmp8 = E(nc.sbuf_tensor(_pref + "tmp8", [128, 8], F32))
        scr8 = E(nc.sbuf_tensor(_pref + "scr8", [128, 8], F32))
        basev = E(nc.sbuf_tensor(_pref + "basev", [1, 16], F32))
        rankv = E(nc.sbuf_tensor(_pref + "rankv", [128, 1], F32))
        posv = E(nc.sbuf_tensor(_pref + "posv", [128, 1], F32))
        ovf = E(nc.sbuf_tensor(_pref + "ovf", [128, 1], F32))
        posi = E(nc.sbuf_tensor(_pref + "posi", [128, 1], I32))
        mu = E(nc.sbuf_tensor(_pref + "mu", [128, 4], F32))
        ssq = E(nc.sbuf_tensor(_pref + "ssq", [128, 2], F32))
        rstd = E(nc.sbuf_tensor(_pref + "rstd", [128, 2], F32))
        epsb = E(nc.sbuf_tensor(_pref + "epsb", [128, 1], F32))
        att = E(nc.sbuf_tensor(_pref + "att", [128, 2 * D], F32))
        sqs = E(nc.sbuf_tensor(_pref + "sqs", [128, D], F32))
        x4t = E(nc.sbuf_tensor(_pref + "x4t", [128, 2 * D], F32))
        yr1 = E(nc.sbuf_tensor(_pref + "yr1", [128, 2 * D], F32))
        yr2 = E(nc.sbuf_tensor(_pref + "yr2", [128, 2 * D], F32))
        pL = E(nc.psum_tensor(_pref + "pL", [128, 128], F32))
        pH1 = E(nc.psum_tensor(_pref + "pH1", [128, 512], F32))
        pH3 = E(nc.psum_tensor(_pref + "pH3", [128, 512], F32))
        pL2 = E(nc.psum_tensor(_pref + "pL2", [128, 128], F32))
        pH1b = E(nc.psum_tensor(_pref + "pH1b", [128, 512], F32))
        pH3b = E(nc.psum_tensor(_pref + "pH3b", [128, 512], F32))
        pQ = E(nc.psum_tensor(_pref + "pQ", [128, 1024], F32))
        lds = sems["lds"]
        ldr0 = sems["ldr0"]
        ldr1 = sems["ldr1"]
        ldq = sems["ldq"]
        ldg = sems["ldg"]
        sct = sems["sct"]
        yst = sems["yst"]
        zst = sems["zst"]
        pe = sems["pe"]
        dve = sems["dve"]
        act = sems["act"]
        st = sems["st"]
        ldsh = sems["ldsh"]
        rb_slot = nc.gpsimd.to_reg(NSLOT)
        rb_tok = nc.gpsimd.to_reg(TPC)

        @_syn(nc)
        def _(s):
            s.dma_start(out=gwt[:].rearrange("p (k e) -> p k e", e=8),
                        in_=gw_T.rearrange("(k p) e -> p k e", p=128)).then_inc(lds, 16)
            s.dma_start(out=gbt[:], in_=gb_bc).then_inc(lds, 16)
            s.dma_start(out=io8[:], in_=iota8).then_inc(lds, 16)
            s.dma_start(out=ioc[:], in_=iotac).then_inc(lds, 16)
            s.dma_start(out=ltt[:], in_=lt128).then_inc(lds, 16)
            s.dma_start(out=onc[:], in_=ones_col).then_inc(lds, 16)
            s.dma_start(out=onr[:], in_=ones_row).then_inc(lds, 16)
            s.dma_start(out=idn[:], in_=ident).then_inc(lds, 16)
        ctr.inc(lds, 8 * 16)
        lds_all = ctr.val(lds)

        @_vec(nc)
        def _(v):
            v.memset(hug[:], 1.0e9)
            v.drain()
            v.memset(zt[:], 0.0)
            v.drain()
            v.memset(basev[:], 0.0)
            v.drain()
            v.memset(epsb[:], LN_EPS)
            v.drain()
            v.engine_nop().then_inc(dve, 1)
        ctr.inc(dve, 1)
        ms_dve = ctr.val(dve)

        @_syn(nc)
        def _(s):
            s.wait_ge(dve, ms_dve)
            s.dma_start(out=tokstage.rearrange("(b a) c -> b (a c)", b=128),
                        in_=hug[:]).then_inc(zst, 16)
            s.dma_start(out=gstage.rearrange("(b a) c -> b (a c)", b=128),
                        in_=zt[:, 0:NSLOT // 128]).then_inc(zst, 16)
            for i in range(NT):
                s.dma_start(out=y1dram[i * 128:(i + 1) * 128, :], in_=zt[:]).then_inc(zst, 16)
                s.dma_start(out=y2dram[i * 128:(i + 1) * 128, :], in_=zt[:]).then_inc(zst, 16)
            for i in range(NSLOT // 128):
                s.dma_start(out=xstage[i * 128:(i + 1) * 128, :], in_=zt[:]).then_inc(zst, 16)
        ctr.inc(zst, 16 * (2 + 2 * NT + NSLOT // 128))
        zst_all = ctr.val(zst)

        sct_rt0 = ctr.val(sct)
        for it in range(NT):
            sct_prev = sct_rt0 + 16 * 6 * it
            @_syn(nc)
            def _(s, it=it, sct_prev=sct_prev):
                if it > 0:
                    s.wait_ge(sct, sct_prev)
                s.dma_start(out=x3r[:, 0:D], in_=x3s[it * 128:(it + 1) * 128, :]).then_inc(ldq, 16)
            ctr.inc(ldq, 16)
            ldqv = ctr.val(ldq)
            for kd in range(8):
                dve_t = ctr.val(dve)
                @_ten(nc)
                def _(t, kd=kd, ldqv=ldqv, dve_t=dve_t):
                    if kd == 0:
                        t.wait_ge(lds, lds_all)
                        t.wait_ge(ldq, ldqv)
                    else:
                        t.wait_ge(dve, dve_t)
                    t.transpose(pL[:, 0:128], x3r[:, kd * 128:(kd + 1) * 128],
                                idn[:]).then_inc(pe, 1)
                ctr.inc(pe, 1)
                pev = ctr.val(pe)
                @_vec(nc)
                def _(v, kd=kd, it=it, pev=pev):
                    v.wait_ge(pe, pev)
                    v.tensor_copy(xsl[:, kd * 128:(kd + 1) * 128], pL[:, 0:128])
                    v.drain()
                    v.tensor_copy(x3T[:, kd * TPC + it * 128: kd * TPC + (it + 1) * 128],
                                  pL[:, 0:128])
                    v.drain()
                    v.engine_nop().then_inc(dve, 1)
                ctr.inc(dve, 1)
            dvev = ctr.val(dve)
            @_ten(nc)
            def _(t, dvev=dvev):
                t.wait_ge(dve, dvev)
                last = None
                for kd in range(8):
                    last = t.matmul(pL[:, 0:8], xsl[:, kd * 128:(kd + 1) * 128],
                                    gwt[:, kd * 8:(kd + 1) * 8],
                                    start=(kd == 0), stop=(kd == 7))
                last.then_inc(pe, 1)
            ctr.inc(pe, 1)
            pev = ctr.val(pe)
            @_vec(nc)
            def _(v, pev=pev):
                v.wait_ge(pe, pev)
                v.reduce_max(nmx[:], pL[:, 0:8], axis=AX.X, negate=True)
                v.drain()
                v.tensor_copy(lg[:], pL[:, 0:8])
                v.drain()
                v.engine_nop().then_inc(dve, 1)
            ctr.inc(dve, 1)
            dvev = ctr.val(dve)
            @_scl(nc)
            def _(s, dvev=dvev):
                s.wait_ge(dve, dvev)
                s.activation(e8[:], lg[:], AF.Exp, bias=nmx[:], accum_out=sumv[:]).then_inc(act, 1)
            ctr.inc(act, 1)
            actv = ctr.val(act)
            @_vec(nc)
            def _(v, actv=actv):
                v.wait_ge(act, actv)
                v.reciprocal(rz1[:], sumv[:])
                v.drain()
                v.tensor_scalar(probs[:], e8[:], rz1[:], None, ALU.mult)
                v.drain()
                v.tensor_add(scb[:], probs[:], gbt[:])
                v.drain()
                v.max(mx8[:], scb[:])
                v.drain()
                v.max_index(ix8[:], mx8[:], scb[:])
                v.drain()
                v.tensor_scalar(tcol[:], ioc[:], float(it * 128), None, ALU.add)
                v.drain()
                v.engine_nop().then_inc(dve, 1)
            ctr.inc(dve, 1)
            for k in range(2):
                dvev = ctr.val(dve)
                sct_k = ctr.val(sct)
                @_vec(nc)
                def _(v, k=k, sct_k=sct_k):
                    if k == 1:
                        v.wait_ge(sct, sct_k)
                    v.tensor_copy(ef32[:], ix8[:, k:k + 1])
                    v.drain()
                    v.tensor_scalar(eq8[:], io8[:], ef32[:], None, ALU.is_equal)
                    v.drain()
                    v.scalar_tensor_tensor(scr8[:], probs[:], 1.0, eq8[:], ALU.mult, ALU.mult,
                                           accum_out=gcol[:])
                    v.drain()
                    v.engine_nop().then_inc(dve, 1)
                ctr.inc(dve, 1)
                dvev = ctr.val(dve)
                @_ten(nc)
                def _(t, k=k, dvev=dvev):
                    t.wait_ge(dve, dvev)
                    t.matmul(pL[:, 16:24], ltt[:], eq8[:], start=True, stop=True)
                    t.matmul(pL[:, 32:40], onr[:], basev[0:1, k * 8:(k + 1) * 8],
                             start=True, stop=True)
                    t.matmul(pL[0:1, 48:56], onc[:], eq8[:], start=True, stop=True).then_inc(pe, 1)
                ctr.inc(pe, 1)
                pev = ctr.val(pe)
                @_scl(nc)
                def _(s, pev=pev):
                    s.wait_ge(pe, pev)
                    s.copy(bb8[:], pL[:, 32:40]).then_inc(act, 1)
                ctr.inc(act, 1)
                actv = ctr.val(act)
                @_vec(nc)
                def _(v, k=k, actv=actv):
                    v.wait_ge(act, actv)
                    v.tensor_add(tmp8[:], pL[:, 16:24], bb8[:])
                    v.drain()
                    v.scalar_tensor_tensor(scr8[:], tmp8[:], 1.0, eq8[:], ALU.mult, ALU.mult,
                                           accum_out=rankv[:])
                    v.drain()
                    v.tensor_scalar(posv[:], ef32[:], float(CAP_D), None, ALU.mult)
                    v.drain()
                    v.tensor_add(posv[:], posv[:], rankv[:])
                    v.drain()
                    v.tensor_scalar(posv[:], posv[:], float(k * 8 * CAP_D), None, ALU.add)
                    v.drain()
                    v.tensor_scalar(ovf[:], rankv[:], float(CAP_D), None, ALU.is_ge)
                    v.drain()
                    v.scalar_tensor_tensor(posv[:], ovf[:], 1.0e9, posv[:], ALU.mult, ALU.add)
                    v.drain()
                    v.tensor_copy(posi[:], posv[:])
                    v.drain()
                    v.tensor_add(basev[0:1, k * 8:(k + 1) * 8], basev[0:1, k * 8:(k + 1) * 8],
                                 pL[0:1, 48:56])
                    v.drain()
                    v.engine_nop().then_inc(dve, 1)
                ctr.inc(dve, 1)
                dvev = ctr.val(dve)
                @_gps(nc)
                def _(g, dvev=dvev):
                    g.wait_ge(zst, zst_all)
                    g.wait_ge(dve, dvev)
                    g.indirect_dma_start(
                        xstage, IndirectOffsetOnAxis(ap=posi[:], axis=0), x3r[:, 0:D], None,
                        bounds_check=rb_slot, oob_is_err=False).then_inc(sct, 16)
                    g.indirect_dma_start(
                        tokstage, IndirectOffsetOnAxis(ap=posi[:], axis=0), tcol[:], None,
                        bounds_check=rb_slot, oob_is_err=False).then_inc(sct, 16)
                    g.indirect_dma_start(
                        gstage, IndirectOffsetOnAxis(ap=posi[:], axis=0), gcol[:], None,
                        bounds_check=rb_slot, oob_is_err=False).then_inc(sct, 16)
                ctr.inc(sct, 48)
        sct_all = ctr.val(sct)

        es_sh = ExitStack()
        sw1s = es_sh.enter_context(nc.sbuf_tensor(_pref + "sw1s", [128, 8 * 2048], BF16))
        sw3s = es_sh.enter_context(nc.sbuf_tensor(_pref + "sw3s", [128, 8 * 2048], BF16))
        sw2s = es_sh.enter_context(nc.sbuf_tensor(_pref + "sw2s", [128, 16 * D], BF16))
        hssh = es_sh.enter_context(nc.sbuf_tensor(_pref + "hssh", [128, 2 * 512], F32))
        hgsh = es_sh.enter_context(nc.sbuf_tensor(_pref + "hgsh", [128, 16 * 512], BF16))
        @_gps(nc)
        def _(g):
            g.wait_ge(sct, sct_all)
            g.dma_start(out=sw1s[:].rearrange("p (k i) -> p k i", i=2048),
                        in_=sw1_t.rearrange("(k p) i -> p k i", p=128)).then_inc(ldg, 16)
            g.dma_start(out=sw3s[:].rearrange("p (k i) -> p k i", i=2048),
                        in_=sw3_t.rearrange("(k p) i -> p k i", p=128)).then_inc(ldg, 16)
            g.dma_start(out=sw2s[:].rearrange("p (k d) -> p k d", d=D),
                        in_=sw2_t.rearrange("(k p) d -> p k d", p=128)).then_inc(ldg, 16)
        ctr.inc(ldg, 48)
        ldg_sh = ctr.val(ldg)

        sh_mul = {}
        for it4 in range(4):
            for ic in range(16):
                pH1c = pH1 if ic % 2 == 0 else pH1b
                pH3c = pH3 if ic % 2 == 0 else pH3b
                hsl = hssh[:, (ic % 2) * 512: (ic % 2) * 512 + 512]
                dve_pre = ctr.val(dve)
                @_ten(nc)
                def _(t, ic=ic, it4=it4, dve_pre=dve_pre, pH1c=pH1c, pH3c=pH3c):
                    t.wait_ge(dve, dve_pre)
                    if ic == 0 and it4 == 0:
                        t.wait_ge(ldg, ldg_sh)
                    last = None
                    for kd in range(8):
                        t.matmul(pH1c[:, 0:512],
                                 sw1s[:, kd * 2048 + ic * 128: kd * 2048 + ic * 128 + 128],
                                 x3T[:, kd * TPC + it4 * 512: kd * TPC + (it4 + 1) * 512],
                                 start=(kd == 0), stop=(kd == 7))
                        last = t.matmul(pH3c[:, 0:512],
                                        sw3s[:, kd * 2048 + ic * 128: kd * 2048 + ic * 128 + 128],
                                        x3T[:, kd * TPC + it4 * 512: kd * TPC + (it4 + 1) * 512],
                                        start=(kd == 0), stop=(kd == 7))
                    last.then_inc(pe, 1)
                ctr.inc(pe, 1)
                pev = ctr.val(pe)
                sguard = sh_mul.get(ic - 2, 0)
                @_scl(nc)
                def _(s, pev=pev, pH1c=pH1c, hsl=hsl, sguard=sguard):
                    s.wait_ge(pe, pev)
                    if sguard > 0:
                        s.wait_ge(dve, sguard)
                    s.activation(hsl, pH1c[:, 0:512], AF.Silu).then_inc(act, 1)
                ctr.inc(act, 1)
                actv = ctr.val(act)
                @_vec(nc)
                def _(v, ic=ic, actv=actv, pH3c=pH3c, hsl=hsl):
                    v.wait_ge(act, actv)
                    v.tensor_mul(hgsh[:, ic * 512:(ic + 1) * 512], hsl,
                                 pH3c[:, 0:512]).then_inc(dve, 1)
                ctr.inc(dve, 1)
                sh_mul[ic] = ctr.val(dve)
            hg_done = ctr.val(dve)
            w2ev = []
            for tc in range(4):
                wpar = tc % 2
                bank_guard = w2ev[tc - 2] if tc >= 2 else 0
                @_ten(nc)
                def _(t, tc=tc, wpar=wpar, hg_done=hg_done, bank_guard=bank_guard):
                    if tc == 0:
                        t.wait_ge(dve, hg_done)
                    elif bank_guard > 0:
                        t.wait_ge(dve, bank_guard)
                    last = None
                    for dh in range(2):
                        if wpar == 0:
                            out = pQ[:, dh * 512: dh * 512 + 512]
                        else:
                            out = (pH1 if dh == 0 else pH3)[:, 0:512]
                        for ki in range(16):
                            last = t.matmul(out,
                                            hgsh[:, ki * 512 + tc * 128: ki * 512 + tc * 128 + 128],
                                            sw2s[:, ki * D + dh * 512: ki * D + dh * 512 + 512],
                                            start=(ki == 0), stop=(ki == 15))
                    last.then_inc(pe, 1)
                ctr.inc(pe, 1)
                pev = ctr.val(pe)
                st_att = ctr.val(st) - 16 if tc >= 2 or it4 > 0 else 0
                @_vec(nc)
                def _(v, tc=tc, wpar=wpar, pev=pev, st_att=st_att):
                    v.wait_ge(pe, pev)
                    if st_att > 0:
                        v.wait_ge(st, st_att)
                    zsl = att[:, (tc % 2) * D: (tc % 2) * D + D]
                    if wpar == 0:
                        v.tensor_copy(zsl, pQ[:, 0:1024])
                    else:
                        v.tensor_copy(zsl[:, 0:512], pH1[:, 0:512])
                        v.drain()
                        v.tensor_copy(zsl[:, 512:1024], pH3[:, 0:512])
                    v.drain()
                    v.engine_nop().then_inc(dve, 1)
                ctr.inc(dve, 1)
                w2ev.append(ctr.val(dve))
                dvev = ctr.val(dve)
                st_prev = ctr.val(st)
                @_gps(nc)
                def _(g, it4=it4, tc=tc, dvev=dvev, st_prev=st_prev):
                    g.wait_ge(dve, dvev)
                    if st_prev > 0:
                        g.wait_ge(st, st_prev)
                    g.dma_start(out=zdram[it4 * 512 + tc * 128: it4 * 512 + (tc + 1) * 128, :],
                                in_=att[:, (tc % 2) * D: (tc % 2) * D + D]).then_inc(st, 16)
                ctr.inc(st, 16)
        sh_st = ctr.val(st)
        pe_sh = ctr.val(pe)
        es_sh.close()

        w1sb = E(nc.sbuf_tensor(_pref + "w1sb", [128, 8 * 1024], BF16))
        w3sb = E(nc.sbuf_tensor(_pref + "w3sb", [128, 8 * 1024], BF16))
        w2sb = E(nc.sbuf_tensor(_pref + "w2sb", [128, 8 * D], BF16))
        xg4 = E(nc.sbuf_tensor(_pref + "xg4", [128, 8 * D], F32))
        xgT5 = E(nc.sbuf_tensor(_pref + "xgT5", [128, 8 * 512], BF16))
        hg5 = E(nc.sbuf_tensor(_pref + "hg5", [128, 8 * 512], BF16))
        hs5 = E(nc.sbuf_tensor(_pref + "hs5", [128, 2 * 512], F32))
        ysb = E(nc.sbuf_tensor(_pref + "ysb", [128, D], F32))
        tif = E(nc.sbuf_tensor(_pref + "tif", [128, 8], F32))
        gif = E(nc.sbuf_tensor(_pref + "gif", [128, 8], F32))
        ti32 = E(nc.sbuf_tensor(_pref + "ti32", [128, 1], I32))
        sub_tev = {}
        sub_w2e = {}
        st_shared_done = ctr.val(st)
        gsub = 0
        for e in range(8):
            dve_now = ctr.val(dve)
            st_now = ctr.val(st)
            @_gps(nc)
            def _(g, e=e, dve_now=dve_now, st_now=st_now):
                if e == 0:
                    g.wait_ge(st, st_now)
                else:
                    g.wait_ge(dve, dve_now)
                g.dma_start(out=w1sb[:].rearrange("p (k i) -> p k i", i=1024),
                            in_=w1_t[e].rearrange("(k p) i -> p k i", p=128)).then_inc(ldg, 16)
                g.dma_start(out=w3sb[:].rearrange("p (k i) -> p k i", i=1024),
                            in_=w3_t[e].rearrange("(k p) i -> p k i", p=128)).then_inc(ldg, 16)
                g.dma_start(out=w2sb[:].rearrange("p (k d) -> p k d", d=D),
                            in_=w2_t[e].rearrange("(k p) d -> p k d", p=128)).then_inc(ldg, 16)
            ctr.inc(ldg, 48)
            ldgv = ctr.val(ldg)
            for k in range(2):
                for sub in range(2):
                    par = gsub % 2
                    base = (k * 8 + e) * CAP_D + sub * 512
                    tguard = sub_tev.get(gsub - 2, 0)
                    wguard = sub_w2e.get(gsub - 2, 0)
                    @_syn(nc)
                    def _(s, base=base, par=par, tguard=tguard, wguard=wguard, gsub=gsub):
                        s.wait_ge(sct, sct_all)
                        if gsub < 2:
                            s.wait_ge(st, st_shared_done)
                        if tguard > 0:
                            s.wait_ge(dve, tguard)
                        if wguard > 0:
                            s.wait_ge(dve, wguard)
                        ldrx = ldr0 if par == 0 else ldr1
                        for tc in range(4):
                            s.dma_start(out=xg4[:, (par * 4 + tc) * D: (par * 4 + tc + 1) * D],
                                        in_=xstage[base + tc * 128: base + (tc + 1) * 128, :]
                                        ).then_inc(ldrx, 16)
                            s.dma_start(out=tif[:, par * 4 + tc: par * 4 + tc + 1],
                                        in_=tokstage[base + tc * 128: base + (tc + 1) * 128, :]
                                        ).then_inc(ldrx, 16)
                            s.dma_start(out=gif[:, par * 4 + tc: par * 4 + tc + 1],
                                        in_=gstage[base + tc * 128: base + (tc + 1) * 128, :]
                                        ).then_inc(ldrx, 16)
                    ctr.inc(ldr0 if par == 0 else ldr1, 16 * 12)
                    ldqv = ctr.val(ldr0 if par == 0 else ldr1)
                    ti = 0
                    ev_list = []
                    for tc in range(4):
                        for kd in range(8):
                            pLx = pL if ti % 2 == 0 else pL2
                            ev_pre = ev_list[ti - 2] if ti >= 2 else 0
                            @_ten(nc)
                            def _(t, tc=tc, kd=kd, par=par, pLx=pLx, ti=ti,
                                  ldqv=ldqv, ev_pre=ev_pre):
                                if ti == 0:
                                    t.wait_ge(ldr0 if par == 0 else ldr1, ldqv)
                                elif ev_pre > 0:
                                    t.wait_ge(dve, ev_pre)
                                t.transpose(pLx[:, 0:128],
                                            xg4[:, (par * 4 + tc) * D + kd * 128:
                                                (par * 4 + tc) * D + kd * 128 + 128],
                                            idn[:]).then_inc(pe, 1)
                            ctr.inc(pe, 1)
                            pev = ctr.val(pe)
                            @_vec(nc)
                            def _(v, tc=tc, kd=kd, pLx=pLx, pev=pev):
                                v.wait_ge(pe, pev)
                                v.tensor_copy(xgT5[:, kd * 512 + tc * 128: kd * 512 + tc * 128 + 128],
                                              pLx[:, 0:128]).then_inc(dve, 1)
                            ctr.inc(dve, 1)
                            ev_list.append(ctr.val(dve))
                            ti += 1
                    sub_tev[gsub] = ctr.val(dve)
                    tev = ctr.val(dve)
                    mul_dve = {}
                    for ic in range(8):
                        pH1c = pH1 if ic % 2 == 0 else pH1b
                        pH3c = pH3 if ic % 2 == 0 else pH3b
                        hsl = hs5[:, (ic % 2) * 512: (ic % 2) * 512 + 512]
                        dve_pre = ctr.val(dve)
                        @_ten(nc)
                        def _(t, ic=ic, tev=tev, dve_pre=dve_pre, ldgv=ldgv,
                              pH1c=pH1c, pH3c=pH3c):
                            if ic == 0:
                                t.wait_ge(dve, tev)
                                t.wait_ge(ldg, ldgv)
                            elif ic >= 2:
                                t.wait_ge(dve, dve_pre)
                            last = None
                            for kd in range(8):
                                t.matmul(pH1c[:, 0:512],
                                         w1sb[:, kd * 1024 + ic * 128: kd * 1024 + ic * 128 + 128],
                                         xgT5[:, kd * 512:(kd + 1) * 512],
                                         start=(kd == 0), stop=(kd == 7))
                                last = t.matmul(pH3c[:, 0:512],
                                                w3sb[:, kd * 1024 + ic * 128: kd * 1024 + ic * 128 + 128],
                                                xgT5[:, kd * 512:(kd + 1) * 512],
                                                start=(kd == 0), stop=(kd == 7))
                            last.then_inc(pe, 1)
                        ctr.inc(pe, 1)
                        pev = ctr.val(pe)
                        sguard = mul_dve.get(ic - 2, 0)
                        @_scl(nc)
                        def _(s, pev=pev, pH1c=pH1c, hsl=hsl, sguard=sguard):
                            s.wait_ge(pe, pev)
                            if sguard > 0:
                                s.wait_ge(dve, sguard)
                            s.activation(hsl, pH1c[:, 0:512], AF.Silu).then_inc(act, 1)
                        ctr.inc(act, 1)
                        actv = ctr.val(act)
                        @_vec(nc)
                        def _(v, ic=ic, actv=actv, pH3c=pH3c, hsl=hsl):
                            v.wait_ge(act, actv)
                            v.tensor_mul(hg5[:, ic * 512:(ic + 1) * 512], hsl,
                                         pH3c[:, 0:512]).then_inc(dve, 1)
                        ctr.inc(dve, 1)
                        mul_dve[ic] = ctr.val(dve)
                    hg_done = ctr.val(dve)
                    w2ev = []
                    for tc in range(4):
                        wpar = tc % 2
                        bank_guard = w2ev[tc - 2] if tc >= 2 else 0
                        @_ten(nc)
                        def _(t, tc=tc, wpar=wpar, hg_done=hg_done, bank_guard=bank_guard):
                            if tc == 0:
                                t.wait_ge(dve, hg_done)
                            elif bank_guard > 0:
                                t.wait_ge(dve, bank_guard)
                            last = None
                            for dh in range(2):
                                if wpar == 0:
                                    out = pQ[:, dh * 512: dh * 512 + 512]
                                else:
                                    out = (pH1 if dh == 0 else pH3)[:, 0:512]
                                for ki in range(8):
                                    last = t.matmul(out,
                                                    hg5[:, ki * 512 + tc * 128: ki * 512 + tc * 128 + 128],
                                                    w2sb[:, ki * D + dh * 512: ki * D + dh * 512 + 512],
                                                    start=(ki == 0), stop=(ki == 7))
                            last.then_inc(pe, 1)
                        ctr.inc(pe, 1)
                        pev = ctr.val(pe)
                        yst_prev = ctr.val(yst)
                        @_vec(nc)
                        def _(v, tc=tc, par=par, wpar=wpar, pev=pev, yst_prev=yst_prev):
                            v.wait_ge(pe, pev)
                            if yst_prev > 0:
                                v.wait_ge(yst, yst_prev)
                            gcolx = gif[:, par * 4 + tc: par * 4 + tc + 1]
                            if wpar == 0:
                                v.tensor_scalar(ysb[:], pQ[:, 0:1024], gcolx, None, ALU.mult)
                            else:
                                v.tensor_scalar(ysb[:, 0:512], pH1[:, 0:512], gcolx, None, ALU.mult)
                                v.drain()
                                v.tensor_scalar(ysb[:, 512:1024], pH3[:, 0:512], gcolx, None, ALU.mult)
                            v.drain()
                            v.tensor_copy(ti32[:], tif[:, par * 4 + tc: par * 4 + tc + 1])
                            v.drain()
                            v.engine_nop().then_inc(dve, 1)
                        ctr.inc(dve, 1)
                        w2ev.append(ctr.val(dve))
                        dvev = ctr.val(dve)
                        yst_p2 = ctr.val(yst)
                        @_gps(nc)
                        def _(g, k=k, dvev=dvev, yst_p2=yst_p2):
                            g.wait_ge(dve, dvev)
                            if yst_p2 > 0:
                                g.wait_ge(yst, yst_p2)
                            g.indirect_dma_start(
                                y1dram if k == 0 else y2dram,
                                IndirectOffsetOnAxis(ap=ti32[:], axis=0),
                                ysb[:], None,
                                bounds_check=rb_tok, oob_is_err=False).then_inc(yst, 16)
                        ctr.inc(yst, 16)
                    sub_w2e[gsub] = ctr.val(dve)
                    gsub += 1
        yst_all = ctr.val(yst)

        st_base_f = ctr.val(st)
        fin_dve = {}
        fin_act = {}
        for it in range(NT):
            par = it % 2
            ldq_prev = ctr.val(ldq)
            dguard = fin_dve.get(it - 2, 0)
            aguard = fin_act.get(it - 2, 0)
            @_syn(nc)
            def _(s, it=it, par=par, ldq_prev=ldq_prev, dguard=dguard, aguard=aguard):
                if it == 0:
                    s.wait_ge(yst, yst_all)
                else:
                    s.wait_ge(ldq, ldq_prev)
                if dguard > 0:
                    s.wait_ge(dve, dguard)
                if aguard > 0:
                    s.wait_ge(act, aguard)
                s.dma_start(out=yr1[:, par * D: par * D + D],
                            in_=y1dram[it * 128:(it + 1) * 128, :]).then_inc(ldq, 16)
                s.dma_start(out=yr2[:, par * D: par * D + D],
                            in_=y2dram[it * 128:(it + 1) * 128, :]).then_inc(ldq, 16)
                s.dma_start(out=x3r[:, par * D: par * D + D],
                            in_=x3s[it * 128:(it + 1) * 128, :]).then_inc(ldq, 16)
                s.dma_start(out=att[:, par * D: par * D + D],
                            in_=zdram[it * 128:(it + 1) * 128, :]).then_inc(ldq, 16)
            ctr.inc(ldq, 64)
            ldqv = ctr.val(ldq)
            st_w = st_base_f + (it - 1) * 16 if it >= 2 else 0
            @_vec(nc)
            def _(v, par=par, ldqv=ldqv, st_w=st_w):
                v.wait_ge(ldq, ldqv)
                if st_w > 0:
                    v.wait_ge(st, st_w)
                ap = att[:, par * D: par * D + D]
                v.tensor_add(ap, ap, yr1[:, par * D: par * D + D])
                v.drain()
                v.tensor_add(ap, ap, yr2[:, par * D: par * D + D])
                v.drain()
                v.tensor_add(ap, ap, x3r[:, par * D: par * D + D])
                v.drain()
                v.engine_nop().then_inc(dve, 1)
            ctr.inc(dve, 1)
            emit_ln_fast(nc, ctr, dve, act,
                         att[:, par * D: par * D + D],
                         x4t[:, par * D: par * D + D],
                         yr1[:, par * D: par * D + D],
                         mu[:, par: par + 1], ssq[:, par: par + 1],
                         rstd[:, par: par + 1], sqs, epsb,
                         mu[:, 2 + par: 3 + par])
            fin_dve[it] = ctr.val(dve)
            fin_act[it] = ctr.val(act)
            dvev = ctr.val(dve)
            st_prev = ctr.val(st)
            @_gps(nc)
            def _(g, it=it, par=par, dvev=dvev, st_prev=st_prev):
                g.wait_ge(dve, dvev)
                if it > 0:
                    g.wait_ge(st, st_prev)
                g.dma_start(out=x4s[it * 128:(it + 1) * 128, :],
                            in_=x4t[:, par * D: par * D + D]).then_inc(st, 16)
            ctr.inc(st, 16)
        @_gps(nc)
        def _(g):
            g.wait_ge(st, ctr.val(st))
    return nc


def build_stage_d():
    nc = bass.Bass("TRN2", target_bir_lowering=False, debug=False)
    NSLOT = 16 * CAP_D
    io = {}
    io["x3s"] = nc.dram_tensor("x3s", [TPC, D], F32, kind="ExternalInput").ap()
    io["gw_T"] = nc.dram_tensor("gw_T", [D, 8], F32, kind="ExternalInput").ap()
    io["gb_bc"] = nc.dram_tensor("gb_bc", [128, 8], F32, kind="ExternalInput").ap()
    io["iota8"] = nc.dram_tensor("iota8", [128, 8], F32, kind="ExternalInput").ap()
    io["iotac"] = nc.dram_tensor("iotac", [128, 1], F32, kind="ExternalInput").ap()
    io["lt128"] = nc.dram_tensor("lt128", [128, 128], F32, kind="ExternalInput").ap()
    io["ones_col"] = nc.dram_tensor("ones_col", [128, 1], F32, kind="ExternalInput").ap()
    io["ones_row"] = nc.dram_tensor("ones_row", [1, 128], F32, kind="ExternalInput").ap()
    io["ident"] = nc.dram_tensor("ident", [128, 128], F32, kind="ExternalInput").ap()
    io["w1_t"] = nc.dram_tensor("w1_t", [8, D, 1024], F32, kind="ExternalInput").ap()
    io["w3_t"] = nc.dram_tensor("w3_t", [8, D, 1024], F32, kind="ExternalInput").ap()
    io["w2_t"] = nc.dram_tensor("w2_t", [8, 1024, D], F32, kind="ExternalInput").ap()
    io["sw1_t"] = nc.dram_tensor("sw1_t", [D, 2048], F32, kind="ExternalInput").ap()
    io["sw3_t"] = nc.dram_tensor("sw3_t", [D, 2048], F32, kind="ExternalInput").ap()
    io["sw2_t"] = nc.dram_tensor("sw2_t", [2048, D], F32, kind="ExternalInput").ap()
    io["x4s"] = nc.dram_tensor("x4s", [TPC, D], F32, kind="ExternalOutput").ap()
    io["xstage"] = nc.dram_tensor("xstage", [NSLOT, D], F32, kind="Internal").ap()
    io["tokstage"] = nc.dram_tensor("tokstage", [NSLOT, 1], F32, kind="Internal").ap()
    io["gstage"] = nc.dram_tensor("gstage", [NSLOT, 1], F32, kind="Internal").ap()
    io["y1dram"] = nc.dram_tensor("y1dram", [TPC, D], F32, kind="Internal").ap()
    io["y2dram"] = nc.dram_tensor("y2dram", [TPC, D], F32, kind="Internal").ap()
    io["zdram"] = nc.dram_tensor("zdram", [TPC, D], F32, kind="Internal").ap()
    ctr = Ctr()
    sems = {}
    from contextlib import ExitStack
    with ExitStack() as ses:
        for sname in ["lds", "ldq", "ldg", "sct", "yst", "zst", "pe", "dve",
                      "act", "st", "ldsh", "ldr0", "ldr1"]:
            sems[sname] = ses.enter_context(nc.semaphore(sname))
        emit_stage_d(nc, ctr, sems, io)
    return nc


def run_stage_d(inp, x3_full):
    nc = build_stage_d()
    lt = np.tril(np.ones((128, 128), np.float32), -1)
    shared = {
        "gw_T": np.ascontiguousarray(inp["L1_ffn_gate_w"].T),
        "gb_bc": np.ascontiguousarray(np.broadcast_to(inp["L1_ffn_gate_bias"], (128, 8))),
        "iota8": np.ascontiguousarray(np.broadcast_to(np.arange(8, dtype=np.float32), (128, 8))),
        "iotac": np.arange(128, dtype=np.float32).reshape(128, 1),
        "lt128": lt,
        "ones_col": np.ones((128, 1), np.float32),
        "ones_row": np.ones((1, 128), np.float32),
        "ident": np.eye(128, dtype=np.float32),
        "w1_t": np.ascontiguousarray(inp["L1_ffn_w1"].transpose(0, 2, 1)),
        "w3_t": np.ascontiguousarray(inp["L1_ffn_w3"].transpose(0, 2, 1)),
        "w2_t": np.ascontiguousarray(inp["L1_ffn_w2"].transpose(0, 2, 1)),
        "sw1_t": np.ascontiguousarray(inp["L1_ffn_sw1"].T),
        "sw3_t": np.ascontiguousarray(inp["L1_ffn_sw3"].T),
        "sw2_t": np.ascontiguousarray(inp["L1_ffn_sw2"].T),
    }
    in_maps = []
    for c in range(NCORE):
        sl = x3_full[c * TPC:(c + 1) * TPC]
        in_maps.append(dict(shared, x3s=np.ascontiguousarray(sl)))
    res = run_bass_kernel_spmd(nc, in_maps, core_ids=list(range(NCORE)))
    return np.concatenate([np.asarray(r["x4s"]) for r in res.results], 0)




def emit_phase_barrier(nc, ctr, sems, phase_no):
    ph = sems["ph"]
    @_gps(nc)
    def _(g):
        g.wait_ge(sems["st"], ctr.val(sems["st"]))
        g.nop().then_inc(ph, 1)
    ctr.inc(ph, 1)
    tgt = ctr.val(ph)
    for eng in (nc.tensor, nc.vector, nc.scalar, nc.sync, nc.gpsimd):
        eng.wait_ge(ph, tgt)


def build_mega():
    nc = bass.Bass("TRN2", target_bir_lowering=False, debug=False)
    NTOK_B = TPC + 128
    NH = TPC + HALO
    NSLOT_B = 32 * CAP_B
    NSLOT_D = 16 * CAP_D

    def inp2(name, shape, dt=F32):
        return nc.dram_tensor(name, shape, dt, kind="ExternalInput").ap()
    def internal(name, shape, dt=F32):
        return nc.dram_tensor(name, shape, dt, kind="Internal").ap()

    ident = inp2("ident", [128, 128])
    iotac = inp2("iotac", [128, 1])
    lt128 = inp2("lt128", [128, 128])
    ones_col = inp2("ones_col", [128, 1])
    ones_row = inp2("ones_row", [1, 128])

    io_a = dict(
        fT=inp2("fT", [D, T]), f_tm=inp2("f_tm", [T, D]),
        wq_T=inp2("wq_T", [D, D]), w2col=inp2("w2col", [128, 8]),
        wqbcol=inp2("wqbcol", [128, 8]), maskM=inp2("maskM", [128, 3 * 256]),
        ones164=inp2("ones164", [1, 64]), onesr128=ones_row,
        w2b_in=inp2("w2b_in", [1, 1]), ident=ident,
        cc_init=inp2("cc_init", [128, UW]), cr_init=inp2("cr_init", [128, 65 * 65]),
        gbcA=inp2("gbcA", [128, D]), bbcA=inp2("bbcA", [128, D]),
        x1s=internal("x1full", [T, D]),
    )
    io_b = dict(
        ntok=NTOK_B, nx1=T,
        x1full=io_a["x1s"],
        bgidx=inp2("bgidx", [NTOK_B, 1], I32), bjunk=inp2("bjunk", [NTOK_B, 1]),
        rw_T=inp2("rw_T", [D, 32]), rb_bc=inp2("rb_bc", [128, 32]),
        iota32=inp2("iota32", [128, 32]), iotac=iotac, lt128=lt128,
        ones_col=ones_col, ones_row=ones_row, ident=ident,
        w1_t=inp2("b_w1t", [32, D, 256]), b1c=inp2("b_b1c", [32, 128, 2]),
        w2_t=inp2("b_w2t", [32, 256, D]), b2bc=inp2("b_b2bc", [32, 128, D]),
        x2s=internal("x2loc", [NTOK_B, D]),
        xstage=internal("b_xstage", [NSLOT_B, D]),
        tokstage=internal("b_tokstage", [NSLOT_B, 1]),
        ydram=internal("b_ydram", [NTOK_B, D]),
    )
    io_c = dict(
        x2full=io_b["x2s"], halo_off=96, ident=ident,
        A_T=inp2("A_T", [D, D]), B_T=inp2("B_T", [D, D]), C_T=inp2("C_T", [D, D]),
        gbc=inp2("gbc", [128, D]), bbc=inp2("bbc", [128, D]),
        x3s=internal("x3loc", [TPC, D]),
    )
    io_d = dict(
        x3s=io_c["x3s"],
        gw_T=inp2("gw_T", [D, 8]), gb_bc=inp2("gb_bc", [128, 8]),
        iota8=inp2("iota8", [128, 8]), iotac=iotac, lt128=lt128,
        ones_col=ones_col, ones_row=ones_row, ident=ident,
        w1_t=inp2("d_w1t", [8, D, 1024]), w3_t=inp2("d_w3t", [8, D, 1024]),
        w2_t=inp2("d_w2t", [8, 1024, D]),
        sw1_t=inp2("sw1_t", [D, 2048]), sw3_t=inp2("sw3_t", [D, 2048]),
        sw2_t=inp2("sw2_t", [2048, D]),
        x4s=nc.dram_tensor("x4s", [TPC, D], F32, kind="ExternalOutput").ap(),
        xstage=internal("d_xstage", [NSLOT_D, D]),
        tokstage=internal("d_tokstage", [NSLOT_D, 1]),
        gstage=internal("d_gstage", [NSLOT_D, 1]),
        y1dram=internal("d_y1dram", [TPC, D]),
        y2dram=internal("d_y2dram", [TPC, D]),
        zdram=internal("d_zdram", [TPC, D]),
    )

    ctr = Ctr()
    sems = {}
    from contextlib import ExitStack
    with ExitStack() as ses:
        for sname in ["ld", "lds", "ldx", "ldf0", "ldf1", "ldr0", "ldr1", "ldq",
                      "ldg", "ldsh", "sct", "yst", "zst", "gth", "pe", "dve",
                      "act", "st", "ph"]:
            sems[sname] = ses.enter_context(nc.semaphore(sname))
        emit_stage_a(nc, ctr, sems, io_a)
        emit_phase_barrier(nc, ctr, sems, 1)
        emit_stage_b(nc, ctr, sems, io_b)
        emit_phase_barrier(nc, ctr, sems, 2)
        emit_stage_c(nc, ctr, sems, io_c)
        emit_phase_barrier(nc, ctr, sems, 3)
        emit_stage_d(nc, ctr, sems, io_d)
    return nc


def mega_in_maps(inp):
    cc, cr, msk = make_stage_a_consts()
    lt = np.tril(np.ones((128, 128), np.float32), -1)
    f = inp["L0_attn_feature"]
    NTOK_B = TPC + 128
    shared = {
        "ident": np.eye(128, dtype=np.float32),
        "iotac": np.arange(128, dtype=np.float32).reshape(128, 1),
        "lt128": lt,
        "ones_col": np.ones((128, 1), np.float32),
        "ones_row": np.ones((1, 128), np.float32),
        "wq_T": np.ascontiguousarray(inp["L0_attn_wq_w"].T),
        "w2col": np.ascontiguousarray(inp["L0_attn_w2_w"].reshape(8, 128).T),
        "wqbcol": np.ascontiguousarray(inp["L0_attn_wq_b"].reshape(8, 128).T),
        "maskM": msk,
        "ones164": np.full((1, 64), np.float32(OMEGA), np.float32),
        "w2b_in": np.asarray(inp["L0_attn_w2_b"], dtype=np.float32).reshape(1, 1),
        "cc_init": cc, "cr_init": cr,
        "gbcA": np.ascontiguousarray(np.broadcast_to(inp["L0_attn_ln_g"], (128, D))),
        "bbcA": np.ascontiguousarray(np.broadcast_to(inp["L0_attn_ln_b"], (128, D))),
        "rw_T": np.ascontiguousarray(inp["L0_ffn_router_w"].T),
        "rb_bc": np.ascontiguousarray(np.broadcast_to(inp["L0_ffn_router_b"], (128, 32))),
        "iota32": np.ascontiguousarray(np.broadcast_to(np.arange(32, dtype=np.float32), (128, 32))),
        "b_w1t": np.ascontiguousarray(inp["L0_ffn_weights1"]),
        "b_b1c": np.ascontiguousarray(inp["L0_ffn_bias1"].reshape(32, 2, 128).transpose(0, 2, 1)),
        "b_w2t": np.ascontiguousarray(inp["L0_ffn_weights2"]),
        "b_b2bc": np.ascontiguousarray(np.broadcast_to(inp["L0_ffn_bias2"][:, None, :], (32, 128, D))),
        "A_T": np.ascontiguousarray(inp["L1_attn_A"].T),
        "B_T": np.ascontiguousarray(inp["L1_attn_B"].T),
        "C_T": np.ascontiguousarray(inp["L1_attn_C"].T),
        "gbc": np.ascontiguousarray(np.broadcast_to(inp["L1_attn_gamma"], (128, D))),
        "bbc": np.ascontiguousarray(np.broadcast_to(inp["L1_attn_beta"], (128, D))),
        "gw_T": np.ascontiguousarray(inp["L1_ffn_gate_w"].T),
        "gb_bc": np.ascontiguousarray(np.broadcast_to(inp["L1_ffn_gate_bias"], (128, 8))),
        "iota8": np.ascontiguousarray(np.broadcast_to(np.arange(8, dtype=np.float32), (128, 8))),
        "d_w1t": np.ascontiguousarray(inp["L1_ffn_w1"].transpose(0, 2, 1)),
        "d_w3t": np.ascontiguousarray(inp["L1_ffn_w3"].transpose(0, 2, 1)),
        "d_w2t": np.ascontiguousarray(inp["L1_ffn_w2"].transpose(0, 2, 1)),
        "sw1_t": np.ascontiguousarray(inp["L1_ffn_sw1"].T),
        "sw3_t": np.ascontiguousarray(inp["L1_ffn_sw3"].T),
        "sw2_t": np.ascontiguousarray(inp["L1_ffn_sw2"].T),
    }
    in_maps = []
    for c in range(NCORE):
        bb = c // 4
        start = (c % 4) * TPC - 128
        gi = start + np.arange(NTOK_B, dtype=np.int64)
        junk = (gi < 0)
        gidx = np.where(junk, 10 ** 9, gi).astype(np.int32).reshape(NTOK_B, 1)
        jk = np.where(junk, 1.0e9, 0.0).astype(np.float32).reshape(NTOK_B, 1)
        in_maps.append(dict(shared,
                            fT=np.ascontiguousarray(f[bb].T),
                            f_tm=np.ascontiguousarray(f[bb]),
                            bgidx=gidx, bjunk=jk))
    return in_maps


def kernel(**inputs):
    inp = {k: np.asarray(v) for k, v in inputs.items()}
    nc = build_mega()
    in_maps = mega_in_maps(inp)
    res = run_bass_kernel_spmd(nc, in_maps, core_ids=list(range(NCORE)))
    out = np.empty((B, T, D), np.float32)
    for c in range(NCORE):
        bb, s = c // 4, (c % 4) * TPC
        out[bb, s:s + TPC] = np.asarray(res.results[c]["x4s"])
    return out


# version 21
# speedup vs baseline: 1.0238x; 1.0238x over previous
import numpy as np
import concourse.bass as bass
import concourse.mybir as mybir
from concourse.bass_utils import run_bass_kernel_spmd
from concourse.bass import IndirectOffsetOnAxis

F32 = mybir.dt.float32
BF16 = mybir.dt.bfloat16
I32 = mybir.dt.int32
U32 = mybir.dt.uint32
AF = mybir.ActivationFunctionType
ALU = mybir.AluOpType
AX = mybir.AxisListType

B, T, D = 2, 8192, 1024
UW = 8320
W = 64
NSWEEP = 26
OMEGA = 1.2
KPIC = 20
HALO = 32
CAP_B = 192
CAP_D = 1024
LN_EPS = 1e-5
NCORE = 8
TPC = 2048


class Ctr:
    def __init__(self):
        self.v = {}
    def inc(self, sem, n=1):
        k = sem.name
        self.v[k] = self.v.get(k, 0) + n
        return self.v[k]
    def val(self, sem):
        return self.v.get(sem.name, 0)


def emit_ln_tokmajor(nc, ctr, dve, act, src, dst, mu, ssq, rstd, sqs, epsb, gt=None, bt=None):
    @_vec(nc)
    def _(v):
        v.reduce_sum(mu[:], src[:], axis=AX.X)
        v.drain()
        v.tensor_scalar_mul(mu[:], mu[:], 1.0 / D)
        v.drain()
        v.tensor_scalar(dst[:], src[:], mu[:], None, ALU.subtract).then_inc(dve, 1)
    ctr.inc(dve, 1)
    dvev = ctr.val(dve)
    @_scl(nc)
    def _(s):
        s.wait_ge(dve, dvev)
        s.activation(sqs[:], dst[:], AF.Square, accum_out=ssq[:])
        s.drain()
        s.activation(rstd[:], ssq[:], AF.Ln, bias=epsb[:], scale=1.0 / D)
        s.drain()
        s.activation(rstd[:], rstd[:], AF.Exp, scale=-0.5).then_inc(act, 1)
    ctr.inc(act, 1)
    actv = ctr.val(act)
    @_vec(nc)
    def _(v):
        v.wait_ge(act, actv)
        if gt is not None:
            v.scalar_tensor_tensor(dst[:], dst[:], rstd[:], gt[:], ALU.mult, ALU.mult)
            v.drain()
            v.tensor_add(dst[:], dst[:], bt[:])
        else:
            v.tensor_scalar(dst[:], dst[:], rstd[:], None, ALU.mult)
        v.drain()
        v.engine_nop().then_inc(dve, 1)
    ctr.inc(dve, 1)


def emit_ln_fast(nc, ctr, dve, act, src, dst, cen, mu, ssq, rstd, sqs, epsb, tmp1,
                 gt=None, bt=None):
    dvev0 = ctr.val(dve)
    @_scl(nc)
    def _(s):
        s.wait_ge(dve, dvev0)
        s.activation(sqs[:], src[:], AF.Square, accum_out=ssq[:]).then_inc(act, 1)
    ctr.inc(act, 1)
    @_vec(nc)
    def _(v):
        v.reduce_sum(mu[:], src[:], axis=AX.X)
        v.drain()
        v.tensor_scalar_mul(mu[:], mu[:], 1.0 / D)
        v.drain()
        v.tensor_mul(tmp1[:], mu[:], mu[:])
        v.drain()
        v.tensor_sub(tmp1[:], epsb[:], tmp1[:]).then_inc(dve, 1)
        v.drain()
        v.tensor_scalar(cen[:], src[:], mu[:], None, ALU.subtract).then_inc(dve, 1)
    ctr.inc(dve, 2)
    dvev1 = ctr.val(dve) - 1
    @_scl(nc)
    def _(s):
        s.drain()
        s.wait_ge(dve, dvev1)
        s.activation(rstd[:], ssq[:], AF.Ln, bias=tmp1[:], scale=1.0 / D)
        s.drain()
        s.activation(rstd[:], rstd[:], AF.Exp, scale=-0.5).then_inc(act, 1)
    ctr.inc(act, 1)
    actv = ctr.val(act)
    @_vec(nc)
    def _(v):
        v.wait_ge(act, actv)
        v.drain()
        if gt is not None:
            v.scalar_tensor_tensor(dst[:], cen[:], rstd[:], gt[:], ALU.mult, ALU.mult)
            v.drain()
            v.tensor_add(dst[:], dst[:], bt[:])
        else:
            v.tensor_scalar(dst[:], cen[:], rstd[:], None, ALU.mult)
        v.drain()
        v.engine_nop().then_inc(dve, 1)
    ctr.inc(dve, 1)


class _eng_ctx:
    def __init__(self, eng):
        self.eng = eng
    def __call__(self, f):
        f(self.eng)

def _vec(nc):
    return _eng_ctx(nc.vector)
def _scl(nc):
    return _eng_ctx(nc.scalar)
def _ten(nc):
    return _eng_ctx(nc.tensor)
def _gps(nc):
    return _eng_ctx(nc.gpsimd)
def _syn(nc):
    return _eng_ctx(nc.sync)


def emit_stage_c(nc, ctr, sems, io):
    _pref = io.get("prefix", "c_")
    (x2full, A_T, B_T, C_T, gbc, bbc, x3s, identC) = (
        io["x2full"], io["A_T"], io["B_T"], io["C_T"],
        io["gbc"], io["bbc"], io["x3s"], io["ident"])
    NH = TPC + HALO
    hoff = io["halo_off"]
    roff = hoff + HALO
    ZW = NH + 1
    TT = []
    t0 = 0
    while t0 < NH:
        TT.append((t0, min(512, NH - t0)))
        t0 += 512

    from contextlib import ExitStack
    with ExitStack() as es:
        E = es.enter_context
        ATt = E(nc.sbuf_tensor(_pref + "ATt", [128, 8 * D], BF16))
        CTt = E(nc.sbuf_tensor(_pref + "CTt", [128, 8 * D], BF16))
        xb = E(nc.sbuf_tensor(_pref + "xb", [128, 8 * NH], BF16))
        z0 = E(nc.sbuf_tensor(_pref + "z0", [128, 8 * ZW], BF16))
        z1 = E(nc.sbuf_tensor(_pref + "z1", [128, 8 * ZW], BF16))
        scr = E(nc.sbuf_tensor(_pref + "scr", [128, 4 * 512], F32))
        idn_c = E(nc.sbuf_tensor(_pref + "idn_c", [128, 128], F32))
        mu = E(nc.sbuf_tensor(_pref + "mu", [128, 1], F32))
        ssq = E(nc.sbuf_tensor(_pref + "ssq", [128, 1], F32))
        rstd = E(nc.sbuf_tensor(_pref + "rstd", [128, 1], F32))
        epsb = E(nc.sbuf_tensor(_pref + "epsb", [128, 1], F32))
        pa = E(nc.psum_tensor(_pref + "pa", [128, 512], F32))
        pb = E(nc.psum_tensor(_pref + "pb", [128, 512], F32))
        pc = E(nc.psum_tensor(_pref + "pc", [128, 512], F32))
        pd = E(nc.psum_tensor(_pref + "pd", [128, 512], F32))
        ld = sems["ld"]
        ldx = sems["ldx"]
        ldr0 = sems["ldr0"]
        ldr1 = sems["ldr1"]
        pe = sems["pe"]
        dve = sems["dve"]
        act = sems["act"]
        st = sems["st"]
        zbuf = [z0, z1]
        psum = [pa, pb]

        es_p1 = ExitStack()
        x2T = es_p1.enter_context(nc.sbuf_tensor(_pref + "x2T", [128, 8 * NH], BF16))
        BTt = es_p1.enter_context(nc.sbuf_tensor(_pref + "BTt", [128, 8 * D], BF16))

        @_gps(nc)
        def _(g):
            g.dma_start(out=ATt[:].rearrange("p (k e) -> p k e", e=D),
                        in_=A_T.rearrange("(k p) e -> p k e", p=128)).then_inc(ld, 16)
            g.dma_start(out=CTt[:].rearrange("p (k e) -> p k e", e=D),
                        in_=C_T.rearrange("(k p) e -> p k e", p=128)).then_inc(ld, 16)
            g.dma_start(out=BTt[:].rearrange("p (k e) -> p k e", e=D),
                        in_=B_T.rearrange("(k p) e -> p k e", p=128)).then_inc(ld, 16)
            g.dma_start(out=idn_c[:], in_=identC).then_inc(ld, 16)
        ctr.inc(ld, 64)
        ld_all = ctr.val(ld)

        x2q = es_p1.enter_context(nc.sbuf_tensor(_pref + "x2q", [128, D], F32))
        nchk = (NH + 127) // 128
        for kchk in range(nchk):
            rows = min(128, NH - kchk * 128)
            dve_now = ctr.val(dve)
            @_syn(nc)
            def _(s, kchk=kchk, rows=rows, dve_now=dve_now):
                if kchk > 0:
                    s.wait_ge(dve, dve_now)
                s.dma_start(out=x2q[0:rows, :],
                            in_=x2full[hoff + kchk * 128: hoff + kchk * 128 + rows, :]
                            ).then_inc(ldx, 16)
            ctr.inc(ldx, 16)
            ldxv = ctr.val(ldx)
            for kd in range(8):
                dve_t = ctr.val(dve)
                @_ten(nc)
                def _(t, kchk=kchk, rows=rows, kd=kd, ldxv=ldxv, dve_t=dve_t):
                    if kd == 0:
                        t.wait_ge(ldx, ldxv)
                        if kchk == 0:
                            t.wait_ge(ld, ld_all)
                    else:
                        t.wait_ge(dve, dve_t)
                    t.transpose(pa[:, 0:rows], x2q[0:rows, kd * 128:(kd + 1) * 128],
                                idn_c[0:rows, 0:rows]).then_inc(pe, 1)
                ctr.inc(pe, 1)
                pev = ctr.val(pe)
                @_vec(nc)
                def _(v, kchk=kchk, rows=rows, kd=kd, pev=pev):
                    v.wait_ge(pe, pev)
                    v.tensor_copy(x2T[:, kd * NH + kchk * 128: kd * NH + kchk * 128 + rows],
                                  pa[:, 0:rows]).then_inc(dve, 1)
                ctr.inc(dve, 1)

        @_vec(nc)
        def _(v):
            v.memset(z0[:], 0.0)
            v.drain()
            v.memset(z1[:], 0.0)
            v.drain()
            v.memset(epsb[:], LN_EPS)
            v.drain()
            v.engine_nop().then_inc(dve, 1)
        ctr.inc(dve, 1)
        dve_z = ctr.val(dve)

        dve_tp = ctr.val(dve)
        @_ten(nc)
        def _(t):
            t.wait_ge(ld, ld_all)
            t.wait_ge(dve, dve_tp)
        ti = 0
        for (tt0, tw) in TT:
            for ke in range(8):
                par = ti % 2
                @_ten(nc)
                def _(t, tt0=tt0, tw=tw, ke=ke, par=par, ti=ti):
                    if ti >= 2:
                        t.wait_ge(dve, dve_z + ti - 1)
                    last = None
                    for kd in range(8):
                        last = t.matmul(psum[par][:, 0:tw],
                                        BTt[:, kd * D + ke * 128: kd * D + ke * 128 + 128],
                                        x2T[:, kd * NH + tt0: kd * NH + tt0 + tw],
                                        start=(kd == 0), stop=(kd == 7))
                    last.then_inc(pe, 1)
                ctr.inc(pe, 1)
                pev = ctr.val(pe)
                @_vec(nc)
                def _(v, tt0=tt0, tw=tw, ke=ke, par=par, pev=pev):
                    v.wait_ge(pe, pev)
                    v.tensor_copy(xb[:, ke * NH + tt0: ke * NH + tt0 + tw],
                                  psum[par][:, 0:tw]).then_inc(dve, 1)
                ctr.inc(dve, 1)
                ti += 1

        pe_phase1 = ctr.val(pe)
        es_p1.close()

        psum4 = [pa, pb, pc, pd]
        TILES = [(m, tt0, tw) for (tt0, tw) in TT for m in range(8)]
        ntile = len(TILES)
        dve_pic0 = ctr.val(dve)
        act_pic0 = ctr.val(act)
        dve_base0 = ctr.val(dve)
        for sw in range(KPIC):
            rd, wr = sw % 2, (sw + 1) % 2
            for i, (m, tt0, tw) in enumerate(TILES):
                g = sw * ntile + i
                par = g % 4
                ct = i // 8
                wav = act_pic0 + (sw - 1) * ntile + (ct + 1) * 8 if sw > 0 else 0
                @_ten(nc)
                def _(t, m=m, tt0=tt0, tw=tw, par=par, g=g, sw=sw, i=i, rd=rd, wav=wav):
                    if sw == 0 and i == 0:
                        t.wait_ge(dve, dve_base0)
                    if wav > 0 and (i % 8 == 0 or i == 0):
                        t.wait_ge(act, wav)
                    if g >= 4:
                        t.wait_ge(dve, dve_pic0 + g - 3)
                    last = None
                    for kd in range(8):
                        last = t.matmul(psum4[par][:, 0:tw],
                                        ATt[:, kd * D + m * 128: kd * D + m * 128 + 128],
                                        zbuf[rd][:, kd * ZW + tt0: kd * ZW + tt0 + tw],
                                        start=(kd == 0), stop=(kd == 7))
                    last.then_inc(pe, 1)
                ctr.inc(pe, 1)
                pev = ctr.val(pe)
                act_guard = act_pic0 + g - 3 if g >= 4 else 0
                @_vec(nc)
                def _(v, m=m, tt0=tt0, tw=tw, par=par, pev=pev, act_guard=act_guard):
                    v.wait_ge(pe, pev)
                    if act_guard > 0:
                        v.wait_ge(act, act_guard)
                    v.tensor_add(scr[:, par * 512: par * 512 + tw],
                                 psum4[par][:, 0:tw],
                                 xb[:, m * NH + tt0: m * NH + tt0 + tw]).then_inc(dve, 1)
                ctr.inc(dve, 1)
                dvev = ctr.val(dve)
                @_scl(nc)
                def _(s, m=m, tt0=tt0, tw=tw, par=par, dvev=dvev, wr=wr):
                    s.wait_ge(dve, dvev)
                    s.activation(zbuf[wr][:, m * ZW + 1 + tt0: m * ZW + 1 + tt0 + tw],
                                 scr[:, par * 512: par * 512 + tw],
                                 AF.Tanh).then_inc(act, 1)
                ctr.inc(act, 1)
        pic_lastbase = act_pic0 + (KPIC - 1) * ntile
        pic_done = ctr.val(act)
        pic_dve_done = ctr.val(dve)
        zfin = zbuf[KPIC % 2]

        gt = E(nc.sbuf_tensor(_pref + "gt", [128, D], F32))
        bt = E(nc.sbuf_tensor(_pref + "bt", [128, D], F32))
        x2r = E(nc.sbuf_tensor(_pref + "x2r", [128, 2 * D], F32))
        att = E(nc.sbuf_tensor(_pref + "att", [128, 2 * D], F32))
        sqs = E(nc.sbuf_tensor(_pref + "sqs", [128, 2 * D], F32))
        x3t = E(nc.sbuf_tensor(_pref + "x3t", [128, 2 * D], F32))
        mu2 = E(nc.sbuf_tensor(_pref + "mu2", [128, 4], F32))
        ssq2 = E(nc.sbuf_tensor(_pref + "ssq2", [128, 2], F32))
        rstd2 = E(nc.sbuf_tensor(_pref + "rstd2", [128, 2], F32))
        pep = [E(nc.psum_tensor(_pref + "pe%d" % j, [128, 512], F32)) for j in range(4)]
        @_syn(nc)
        def _(sync):
            sync.wait_ge(pe, pe_phase1)
            sync.dma_start(out=gt[:], in_=gbc).then_inc(ldx, 16)
            sync.dma_start(out=bt[:], in_=bbc).then_inc(ldx, 16)
        ctr.inc(ldx, 32)
        ldx_gb = ctr.val(ldx)
        nt = TPC // 128
        tile_act_end = {}
        tile_add_dve = {}
        st_base_c = ctr.val(st)
        for it in range(nt):
            tb = HALO + it * 128
            par = it % 2
            ct_hi = (tb + 127) // 512
            wav = pic_lastbase + min((ct_hi + 1) * 8, ntile)
            st_guard = st_base_c + (it - 1) * 16 if it >= 2 else 0
            @_syn(nc)
            def _(sync, it=it, par=par, st_guard=st_guard, wav=wav):
                if st_guard > 0:
                    sync.wait_ge(st, st_guard)
                else:
                    sync.wait_ge(act, wav)
                ldr = ldr0 if par == 0 else ldr1
                sync.dma_start(out=x2r[:, par * D: par * D + D],
                               in_=x2full[roff + it * 128: roff + (it + 1) * 128, :]).then_inc(ldr, 16)
            ctr.inc(ldr0 if par == 0 else ldr1, 16)
            ldv = ctr.val(ldr0 if par == 0 else ldr1)
            pguard = tile_add_dve.get(it - 2, 0)
            for eh in range(2):
                @_ten(nc)
                def _(t, tb=tb, eh=eh, par=par, wav=wav, pguard=pguard, it=it):
                    if eh == 0:
                        t.wait_ge(act, wav)
                        if it == 0:
                            t.wait_ge(dve, pic_dve_done)
                        if pguard > 0:
                            t.wait_ge(dve, pguard)
                    last = None
                    for m in range(8):
                        last = t.matmul(pep[2 * par + eh][:, 0:512],
                                        zfin[:, m * ZW + 1 + tb: m * ZW + 1 + tb + 128],
                                        CTt[:, m * D + eh * 512: m * D + eh * 512 + 512],
                                        start=(m == 0), stop=(m == 7))
                    last.then_inc(pe, 1)
                ctr.inc(pe, 1)
            pev = ctr.val(pe)
            aguard = tile_act_end.get(it - 2, 0)
            @_vec(nc)
            def _(v, it=it, par=par, pev=pev, ldv=ldv, aguard=aguard):
                v.wait_ge(pe, pev)
                v.wait_ge(ldx, ldx_gb)
                v.wait_ge(ldr0 if par == 0 else ldr1, ldv)
                if aguard > 0:
                    v.wait_ge(act, aguard)
                if it < 2:
                    v.wait_ge(act, pic_done)
                x2t = x2r[:, par * D: par * D + D]
                v.tensor_add(att[:, par * D: par * D + 512], pep[2 * par][:, 0:512], x2t[:, 0:512])
                v.drain()
                v.tensor_add(att[:, par * D + 512: par * D + 1024], pep[2 * par + 1][:, 0:512],
                             x2t[:, 512:1024])
                v.drain()
                v.engine_nop().then_inc(dve, 1)
            ctr.inc(dve, 1)
            tile_add_dve[it] = ctr.val(dve)
            attp = att[:, par * D: par * D + D]
            x3tp = x3t[:, par * D: par * D + D]
            sqsp = sqs[:, par * D: par * D + D]
            cen1 = scr[:, par * D: par * D + D]
            cen2 = att[:, (1 - par) * D: (1 - par) * D + D]
            mup = mu2[:, par: par + 1]
            tmp1p = mu2[:, 2 + par: 3 + par]
            ssqp = ssq2[:, par: par + 1]
            rstdp = rstd2[:, par: par + 1]
            emit_ln_fast(nc, ctr, dve, act, attp, attp, cen1, mup, ssqp, rstdp, sqsp,
                         epsb, tmp1p, gt, bt)
            dvev = ctr.val(dve)
            @_vec(nc)
            def _(v, par=par, dvev=dvev, attp=attp):
                v.wait_ge(dve, dvev)
                x2t = x2r[:, par * D: par * D + D]
                v.tensor_add(attp, attp, x2t)
                v.drain()
                v.engine_nop().then_inc(dve, 1)
            ctr.inc(dve, 1)
            emit_ln_fast(nc, ctr, dve, act, attp, x3tp, cen2, mup, ssqp, rstdp, sqsp,
                         epsb, tmp1p)
            tile_act_end[it] = ctr.val(act)
            dvev = ctr.val(dve)
            st_prev = ctr.val(st)
            @_gps(nc)
            def _(g, it=it, dvev=dvev, x3tp=x3tp, st_prev=st_prev):
                g.wait_ge(dve, dvev)
                if it > 0:
                    g.wait_ge(st, st_prev)
                g.dma_start(out=x3s[it * 128:(it + 1) * 128, :], in_=x3tp).then_inc(st, 16)
            ctr.inc(st, 16)
        @_gps(nc)
        def _(g):
            g.wait_ge(st, ctr.val(st))
    return nc


def build_stage_c():
    nc = bass.Bass("TRN2", target_bir_lowering=False, debug=False)
    NH = TPC + HALO
    io = {}
    io["x2full"] = nc.dram_tensor("x2full", [NH, D], F32, kind="ExternalInput").ap()
    io["halo_off"] = 0
    io["ident"] = nc.dram_tensor("ident", [128, 128], F32, kind="ExternalInput").ap()
    io["A_T"] = nc.dram_tensor("A_T", [D, D], F32, kind="ExternalInput").ap()
    io["B_T"] = nc.dram_tensor("B_T", [D, D], F32, kind="ExternalInput").ap()
    io["C_T"] = nc.dram_tensor("C_T", [D, D], F32, kind="ExternalInput").ap()
    io["gbc"] = nc.dram_tensor("gbc", [128, D], F32, kind="ExternalInput").ap()
    io["bbc"] = nc.dram_tensor("bbc", [128, D], F32, kind="ExternalInput").ap()
    io["x3s"] = nc.dram_tensor("x3s", [TPC, D], F32, kind="ExternalOutput").ap()
    ctr = Ctr()
    sems = {}
    from contextlib import ExitStack
    with ExitStack() as ses:
        for sname in ["ld", "ldx", "ldr0", "ldr1", "pe", "dve", "act", "st"]:
            sems[sname] = ses.enter_context(nc.semaphore(sname))
        emit_stage_c(nc, ctr, sems, io)
    return nc


def run_stage_c(inp, x2_full):
    nc = build_stage_c()
    xf = x2_full
    gamma, beta = inp["L1_attn_gamma"], inp["L1_attn_beta"]
    shared = {
        "ident": np.eye(128, dtype=np.float32),
        "A_T": np.ascontiguousarray(inp["L1_attn_A"].T),
        "B_T": np.ascontiguousarray(inp["L1_attn_B"].T),
        "C_T": np.ascontiguousarray(inp["L1_attn_C"].T),
        "gbc": np.ascontiguousarray(np.broadcast_to(gamma, (128, D))),
        "bbc": np.ascontiguousarray(np.broadcast_to(beta, (128, D))),
    }
    in_maps = []
    for c in range(NCORE):
        lo = c * TPC
        sl = xf[lo:lo + TPC]
        if c % 4 == 0:
            halo = np.zeros((HALO, D), np.float32)
        else:
            halo = xf[lo - HALO:lo]
        x2h = np.concatenate([halo, sl], 0)
        in_maps.append(dict(shared, x2full=np.ascontiguousarray(x2h)))
    res = run_bass_kernel_spmd(nc, in_maps, core_ids=list(range(NCORE)))
    return np.concatenate([np.asarray(r["x3s"]) for r in res.results], 0)




def emit_stage_a(nc, ctr, sems, io):
    _pref = io.get("prefix", "a_")
    (fT, f_tm, wq_T, w2col, wqbcol, maskM, ones164, onesr128, w2b_in, ident,
     cc_init, cr_init, gbcA, bbcA, x1s) = (
        io["fT"], io["f_tm"], io["wq_T"], io["w2col"], io["wqbcol"], io["maskM"],
        io["ones164"], io["onesr128"], io["w2b_in"], io["ident"],
        io["cc_init"], io["cr_init"], io["gbcA"], io["bbcA"], io["x1s"])
    from contextlib import ExitStack
    with ExitStack() as es:
        E = es.enter_context
        cc0 = E(nc.sbuf_tensor(_pref + "cc0", [64, UW], F32))
        cc1 = E(nc.sbuf_tensor(_pref + "cc1", [64, UW], F32))
        cr0 = E(nc.sbuf_tensor(_pref + "cr0", [128, 65 * 65], F32))
        cr1 = E(nc.sbuf_tensor(_pref + "cr1", [128, 65 * 65], F32))
        Psb = E(nc.sbuf_tensor(_pref + "Psb", [64, T], F32))
        F0sb = E(nc.sbuf_tensor(_pref + "F0sb", [64, D], F32))
        F0T = E(nc.sbuf_tensor(_pref + "F0T", [128, 8 * 64], F32))
        Msb = E(nc.sbuf_tensor(_pref + "Msb", [128, 8 * 64], F32))
        w2sb = E(nc.sbuf_tensor(_pref + "w2sb", [128, 8], F32))
        wqbsb = E(nc.sbuf_tensor(_pref + "wqbsb", [128, 8], F32))
        rsb = E(nc.sbuf_tensor(_pref + "rsb", [64, 1], F32))
        msk = E(nc.sbuf_tensor(_pref + "msk", [128, 3 * 256], F32))
        o164 = E(nc.sbuf_tensor(_pref + "o164", [1, 64], F32))
        or128 = E(nc.sbuf_tensor(_pref + "or128", [1, 128], F32))
        w2bsb = E(nc.sbuf_tensor(_pref + "w2bsb", [1, 1], F32))
        w2bb = E(nc.sbuf_tensor(_pref + "w2bb", [128, 1], F32))
        ebv = E(nc.sbuf_tensor(_pref + "ebv", [1, 1], F32))
        idn = E(nc.sbuf_tensor(_pref + "idn", [128, 128], F32))
        esb = [E(nc.sbuf_tensor(_pref + "esb0", [128, 3 * 256], F32)),
               E(nc.sbuf_tensor(_pref + "esb1", [128, 3 * 256], F32))]
        zr = [E(nc.sbuf_tensor(_pref + "zr0", [1, 256], F32)),
              E(nc.sbuf_tensor(_pref + "zr1", [1, 256], F32))]
        rzb = [E(nc.sbuf_tensor(_pref + "rzb0", [64, 256], F32)),
               E(nc.sbuf_tensor(_pref + "rzb1", [64, 256], F32))]
        tmpc = [E(nc.sbuf_tensor(_pref + "tmpc0", [64, 256], F32)),
                E(nc.sbuf_tensor(_pref + "tmpc1", [64, 256], F32))]
        lnw = E(nc.sbuf_tensor(_pref + "lnw", [1, 1], F32))
        gtA = E(nc.sbuf_tensor(_pref + "gtA", [128, D], F32))
        btA = E(nc.sbuf_tensor(_pref + "btA", [128, D], F32))
        mu = E(nc.sbuf_tensor(_pref + "mu", [128, 8], F32))
        ssq = E(nc.sbuf_tensor(_pref + "ssq", [128, 4], F32))
        rstd = E(nc.sbuf_tensor(_pref + "rstd", [128, 4], F32))
        epsb = E(nc.sbuf_tensor(_pref + "epsb", [128, 1], F32))
        pS = [E(nc.psum_tensor(_pref + "pS0", [128, 3 * 256], F32)),
              E(nc.psum_tensor(_pref + "pS1", [128, 3 * 256], F32))]
        pM = [E(nc.psum_tensor(_pref + "pM0", [128, 512], F32)),
              E(nc.psum_tensor(_pref + "pM1", [128, 512], F32))]
        pV = E(nc.psum_tensor(_pref + "pV", [128, 1024], F32))
        ld = sems["ld"]
        lds = sems["lds"]
        ldf0 = sems["ldf0"]
        ldf1 = sems["ldf1"]
        ldr0 = sems["ldr0"]
        ldr1 = sems["ldr1"]
        pe = sems["pe"]
        dve = sems["dve"]
        act = sems["act"]
        st = sems["st"]
        ccb = [cc0, cc1]
        crb = [cr0, cr1]

        @_syn(nc)
        def _(s):
            s.dma_start(out=w2sb[:], in_=w2col).then_inc(lds, 16)
            s.dma_start(out=wqbsb[:], in_=wqbcol).then_inc(lds, 16)
            s.dma_start(out=msk[:], in_=maskM).then_inc(lds, 16)
            s.dma_start(out=o164[:], in_=ones164).then_inc(lds, 16)
            s.dma_start(out=or128[:], in_=onesr128).then_inc(lds, 16)
            s.dma_start(out=w2bsb[:], in_=w2b_in).then_inc(lds, 16)
            s.dma_start(out=idn[:], in_=ident).then_inc(lds, 16)
            s.dma_start(out=F0sb[:], in_=f_tm[0:W, :]).then_inc(lds, 16)
            s.dma_start(out=cc0[:], in_=cc_init).then_inc(lds, 16)
            s.dma_start(out=cc1[:], in_=cc_init).then_inc(lds, 16)
            s.dma_start(out=cr0[:], in_=cr_init).then_inc(lds, 16)
            s.dma_start(out=cr1[:], in_=cr_init).then_inc(lds, 16)
            s.dma_start(out=gtA[:], in_=gbcA).then_inc(lds, 16)
            s.dma_start(out=btA[:], in_=bbcA).then_inc(lds, 16)
        ctr.inc(lds, 14 * 16)
        lds_all = ctr.val(lds)
        @_gps(nc)
        def _(g):
            g.dma_start(out=F0T[:].rearrange("p (k j) -> p k j", j=64),
                        in_=fT.rearrange("(k p) t -> p k t", p=128)[:, :, 0:64]).then_inc(ld, 16)
        ctr.inc(ld, 16)
        ld_all = ctr.val(ld)

        @_vec(nc)
        def _(v):
            v.wait_ge(lds, lds_all)
            v.wait_ge(ld, ld_all)
            for k in range(8):
                v.tensor_scalar(F0T[:, k * 64:(k + 1) * 64], F0T[:, k * 64:(k + 1) * 64],
                                w2sb[:, k:k + 1], None, ALU.mult)
                v.drain()
            v.memset(epsb[:], LN_EPS)
            v.drain()
            v.memset(lnw[:], 0.0)
            v.drain()
            v.engine_nop().then_inc(dve, 1)
        ctr.inc(dve, 1)
        prep_dve = ctr.val(dve)

        @_ten(nc)
        def _(t):
            t.wait_ge(dve, prep_dve)
            last = None
            for k in range(8):
                last = t.matmul(pM[0][0:64, 0:1], F0T[:, k * 64:(k + 1) * 64],
                                wqbsb[:, k:k + 1], start=(k == 0), stop=(k == 7))
            last.then_inc(pe, 1)
        ctr.inc(pe, 1)
        pev = ctr.val(pe)
        @_vec(nc)
        def _(v, pev=pev):
            v.wait_ge(pe, pev)
            v.tensor_copy(rsb[:], pM[0][0:64, 0:1]).then_inc(dve, 1)
        ctr.inc(dve, 1)
        r_dve = ctr.val(dve)

        @_ten(nc)
        def _(t):
            t.matmul(pM[0][:, 4:5], or128[:], w2bsb[:], start=True, stop=True).then_inc(pe, 1)
        ctr.inc(pe, 1)
        pev = ctr.val(pe)
        @_vec(nc)
        def _(v, pev=pev):
            v.wait_ge(pe, pev)
            v.tensor_copy(w2bb[:], pM[0][:, 4:5]).then_inc(dve, 1)
        ctr.inc(dve, 1)
        @_scl(nc)
        def _(s):
            s.wait_ge(dve, ctr.val(dve))
            s.activation(ebv[:], w2bsb[:], AF.Exp).then_inc(act, 1)
        ctr.inc(act, 1)

        es_mp = ExitStack()
        fsl = es_mp.enter_context(nc.sbuf_tensor(_pref + "fsl", [128, 8 * 512], F32))
        for m in range(8):
            par = 0
            dve_now = ctr.val(dve)
            @_syn(nc)
            def _(s, m=m, par=par, dve_now=dve_now):
                if m >= 1:
                    s.wait_ge(dve, dve_now)
                s.dma_start(out=fsl[:, par * 4096: par * 4096 + 1024].rearrange("p (k c) -> p k c", c=128),
                            in_=wq_T.rearrange("(k p) d -> p k d", p=128)[:, :, m * 128:(m + 1) * 128]
                            ).then_inc(ldf0 if par == 0 else ldf1, 16)
            ctr.inc(ldf0 if par == 0 else ldf1, 16)
            ldfv = ctr.val(ldf0 if par == 0 else ldf1)
            @_ten(nc)
            def _(t, m=m, par=par, ldfv=ldfv):
                t.wait_ge(ldf0 if par == 0 else ldf1, ldfv)
                last = None
                for kq in range(8):
                    last = t.matmul(pS[0][:, 0:64],
                                    fsl[:, par * 4096 + kq * 128: par * 4096 + kq * 128 + 128],
                                    F0T[:, kq * 64:(kq + 1) * 64],
                                    start=(kq == 0), stop=(kq == 7))
                last.then_inc(pe, 1)
            ctr.inc(pe, 1)
            pev = ctr.val(pe)
            @_vec(nc)
            def _(v, m=m, pev=pev):
                v.wait_ge(pe, pev)
                v.tensor_copy(Msb[:, m * 64:(m + 1) * 64], pS[0][:, 0:64]).then_inc(dve, 1)
            ctr.inc(dve, 1)
        m_done_dve = ctr.val(dve)

        for sidx in range(16):
            par = 0
            @_syn(nc)
            def _(s, sidx=sidx, par=par):
                s.wait_ge(dve, m_done_dve + sidx)
                s.dma_start(out=fsl[:, par * 4096: (par + 1) * 4096].rearrange("p (k c) -> p k c", c=512),
                            in_=fT.rearrange("(k p) t -> p k t", p=128)[:, :, sidx * 512:(sidx + 1) * 512]
                            ).then_inc(ldf0 if par == 0 else ldf1, 16)
            ctr.inc(ldf0 if par == 0 else ldf1, 16)
            ldfv = ctr.val(ldf0 if par == 0 else ldf1)
            @_ten(nc)
            def _(t, sidx=sidx, par=par, ldfv=ldfv):
                t.wait_ge(ldf0 if par == 0 else ldf1, ldfv)
                last = None
                for k in range(8):
                    last = t.matmul(pV[0:64, 0:512], Msb[:, k * 64:(k + 1) * 64],
                                    fsl[:, par * 4096 + k * 512: par * 4096 + (k + 1) * 512],
                                    start=(k == 0), stop=(k == 7))
                last.then_inc(pe, 1)
            ctr.inc(pe, 1)
            pev = ctr.val(pe)
            @_vec(nc)
            def _(v, sidx=sidx, pev=pev):
                v.wait_ge(pe, pev)
                v.tensor_scalar(Psb[:, sidx * 512:(sidx + 1) * 512], pV[0:64, 0:512],
                                rsb[:], None, ALU.add).then_inc(dve, 1)
            ctr.inc(dve, 1)
        p_done_dve = ctr.val(dve)
        es_mp.close()
        fr = E(nc.sbuf_tensor(_pref + "fr", [128, 2 * D], F32))
        th = E(nc.sbuf_tensor(_pref + "th", [128, 2 * D], F32))
        att = E(nc.sbuf_tensor(_pref + "att", [128, 4 * D], F32))
        sqs = E(nc.sbuf_tensor(_pref + "sqs", [128, D], F32))
        x1t = E(nc.sbuf_tensor(_pref + "x1t", [128, 2 * D], F32))

        t_tiles = []
        for g in range(16):
            blk = []
            base = 64 + g * 512
            for (o, wdt) in ((0, 256), (256, 256)):
                ta = base + o
                tw = min(wdt, T - ta)
                if tw > 0:
                    blk.append((ta, tw))
            t_tiles.append(blk)

        @_ten(nc)
        def _(t):
            t.wait_ge(dve, p_done_dve)

        for g in range(16):
            dve_blk = ctr.val(dve)
            for sw in range(NSWEEP):
                rd, wr = sw % 2, (sw + 1) % 2
                tiles = t_tiles[g]
                nt_ = len(tiles)
                pevs = []
                for i, (ta, tw) in enumerate(tiles):
                    ua = ta + 64
                    @_ten(nc)
                    def _(t, ta=ta, tw=tw, ua=ua, rd=rd, i=i, sw=sw, dve_blk=dve_blk):
                        if sw == 0 and i == 0:
                            t.wait_ge(dve, dve_blk)
                        last = None
                        for c in range(3):
                            u0 = ua - 128 + c * 128
                            last = t.matmul(pS[i][:, c * 256: c * 256 + tw],
                                            ccb[rd][:, u0:u0 + 128],
                                            Psb[:, ta:ta + tw],
                                            start=True, stop=True)
                        last.then_inc(pe, 1)
                    ctr.inc(pe, 1)
                    pevs.append(ctr.val(pe))
                actvs = []
                for i, (ta, tw) in enumerate(tiles):
                    @_scl(nc)
                    def _(s, i=i, pev=pevs[i]):
                        s.wait_ge(pe, pev)
                        s.activation(esb[i][:], pS[i][:, 0:768], AF.Exp,
                                     bias=w2bb[:, 0:1]).then_inc(act, 1)
                    ctr.inc(act, 1)
                    actvs.append(ctr.val(act))
                dvevs = []
                for i, (ta, tw) in enumerate(tiles):
                    @_vec(nc)
                    def _(v, i=i, actv=actvs[i]):
                        v.wait_ge(act, actv)
                        v.tensor_mul(esb[i][:], esb[i][:], msk[:]).then_inc(dve, 1)
                    ctr.inc(dve, 1)
                    dvevs.append(ctr.val(dve))
                pevs = []
                for i, (ta, tw) in enumerate(tiles):
                    ua = ta + 64
                    @_ten(nc)
                    def _(t, ua=ua, tw=tw, rd=rd, i=i, dvev=dvevs[i]):
                        t.wait_ge(dve, dvev)
                        last = None
                        for c in range(3):
                            cch = (ua - 128) // 128 + c
                            last = t.matmul(pM[i][0:65, 0:tw],
                                            crb[rd][:, cch * 65: cch * 65 + 65],
                                            esb[i][:, c * 256: c * 256 + tw],
                                            start=(c == 0), stop=(c == 2))
                        last.then_inc(pe, 1)
                    ctr.inc(pe, 1)
                    pevs.append(ctr.val(pe))
                actvs = []
                for i, (ta, tw) in enumerate(tiles):
                    @_scl(nc)
                    def _(s, tw=tw, i=i, pev=pevs[i]):
                        s.wait_ge(pe, pev)
                        s.activation(zr[i][:, 0:tw], pM[i][64:65, 0:tw], AF.Ln,
                                     bias=ebv[0:1, 0:1])
                        s.drain()
                        s.activation(zr[i][:, 0:tw], zr[i][:, 0:tw], AF.Exp,
                                     bias=lnw[0:1, 0:1], scale=-1.0).then_inc(act, 1)
                    ctr.inc(act, 1)
                    actvs.append(ctr.val(act))
                pevs = []
                for i, (ta, tw) in enumerate(tiles):
                    @_ten(nc)
                    def _(t, tw=tw, i=i, actv=actvs[i]):
                        t.wait_ge(act, actv)
                        t.matmul(pM[i][0:64, 256:256 + tw], o164[:], zr[i][:, 0:tw],
                                 start=True, stop=True).then_inc(pe, 1)
                    ctr.inc(pe, 1)
                    pevs.append(ctr.val(pe))
                actvs = []
                for i, (ta, tw) in enumerate(tiles):
                    @_scl(nc)
                    def _(s, tw=tw, i=i, pev=pevs[i]):
                        s.wait_ge(pe, pev)
                        s.copy(rzb[i][:, 0:tw], pM[i][0:64, 256:256 + tw]).then_inc(act, 1)
                    ctr.inc(act, 1)
                    actvs.append(ctr.val(act))
                dvevs = []
                for i, (ta, tw) in enumerate(tiles):
                    ua = ta + 64
                    @_vec(nc)
                    def _(v, ua=ua, tw=tw, wr=wr, rd=rd, i=i, actv=actvs[i]):
                        v.wait_ge(act, actv)
                        v.tensor_mul(tmpc[i][:, 0:tw], pM[i][0:64, 0:tw],
                                     rzb[i][:, 0:tw])
                        v.drain()
                        v.scalar_tensor_tensor(ccb[wr][:, ua:ua + tw],
                                               ccb[rd][:, ua:ua + tw],
                                               1.0 - float(np.float32(OMEGA)),
                                               tmpc[i][:, 0:tw],
                                               ALU.mult, ALU.add).then_inc(dve, 1)
                    ctr.inc(dve, 1)
                    dvevs.append(ctr.val(dve))
                for i, (ta, tw) in enumerate(tiles):
                    ua = ta + 64
                    for sub in range((tw + 127) // 128):
                        swd = min(128, tw - sub * 128)
                        dve_pre = ctr.val(dve)
                        @_ten(nc)
                        def _(t, ua=ua, sub=sub, swd=swd, wr=wr, i=i,
                              dvev=dvevs[i], dve_pre=dve_pre):
                            t.wait_ge(dve, dvev if sub == 0 else dve_pre)
                            t.transpose(pM[i][0:swd, 448:512],
                                        ccb[wr][:, ua + sub * 128: ua + sub * 128 + swd],
                                        idn[0:64, 0:64]).then_inc(pe, 1)
                        ctr.inc(pe, 1)
                        pev = ctr.val(pe)
                        @_vec(nc)
                        def _(v, ua=ua, sub=sub, swd=swd, wr=wr, i=i, pev=pev):
                            v.wait_ge(pe, pev)
                            cch = ua // 128 + sub
                            v.tensor_copy(crb[wr][0:swd, cch * 65: cch * 65 + 64],
                                          pM[i][0:swd, 448:512]).then_inc(dve, 1)
                        ctr.inc(dve, 1)
            last_par = NSWEEP % 2
            u0b = 64 + g * 512 + 64
            u1b = min(u0b + 512, UW)
            dvev = ctr.val(dve)
            pev_all = ctr.val(pe)
            @_vec(nc)
            def _(v, u0b=u0b, u1b=u1b, last_par=last_par, dvev=dvev, pev_all=pev_all):
                v.wait_ge(dve, dvev)
                v.wait_ge(pe, pev_all)
                v.tensor_copy(ccb[1 - last_par][:, u0b:u1b], ccb[last_par][:, u0b:u1b])
                v.drain()
                c0, c1 = u0b // 128, (u1b + 127) // 128
                v.tensor_copy(crb[1 - last_par][:, c0 * 65: c1 * 65],
                              crb[last_par][:, c0 * 65: c1 * 65])
                v.drain()
                v.engine_nop().then_inc(dve, 1)
            ctr.inc(dve, 1)
        scan_dve = ctr.val(dve)
        ccf = ccb[NSWEEP % 2]

        @_ten(nc)
        def _(t):
            t.wait_ge(dve, scan_dve)
        tile_tanh_act = {}
        tile_mul_dve = {}
        tile_fadd_dve = {}
        tile_act_end = {}
        st_base_a = ctr.val(st)
        for it in range(64):
            tb = it * 128
            par = it % 2
            par4 = it % 4
            fguard = tile_fadd_dve.get(it - 2, 0)
            @_syn(nc)
            def _(s, it=it, par=par, fguard=fguard):
                if fguard > 0:
                    s.wait_ge(dve, fguard)
                elif it == 0:
                    s.wait_ge(dve, p_done_dve)
                ldr = ldr0 if par == 0 else ldr1
                s.dma_start(out=fr[:, par * D: par * D + D],
                            in_=f_tm[it * 128:(it + 1) * 128, :]).then_inc(ldr, 16)
            ctr.inc(ldr0 if par == 0 else ldr1, 16)
            ldrv = ctr.val(ldr0 if par == 0 else ldr1)
            vguard = tile_tanh_act.get(it - 2, 0)
            @_ten(nc)
            def _(t, tb=tb, par=par, vguard=vguard):
                if vguard > 0:
                    t.wait_ge(act, vguard)
                for vh in range(2):
                    pdst = pV[:, vh * 512:(vh + 1) * 512] if par == 0 \
                        else pS[vh][:, 0:512]
                    t.matmul(pdst, ccf[:, tb + 64: tb + 64 + 128],
                             F0sb[:, vh * 512:(vh + 1) * 512],
                             start=True, stop=True).then_inc(pe, 1)
            ctr.inc(pe, 2)
            pev = ctr.val(pe)
            tguard = tile_mul_dve.get(it - 2, 0)
            @_scl(nc)
            def _(s, par=par, pev=pev, tguard=tguard):
                s.wait_ge(pe, pev)
                if tguard > 0:
                    s.wait_ge(dve, tguard)
                if par == 0:
                    s.activation(th[:, 0:D], pV[:, 0:1024], AF.Tanh).then_inc(act, 1)
                else:
                    s.activation(th[:, D: D + 512], pS[0][:, 0:512], AF.Tanh)
                    s.drain()
                    s.activation(th[:, D + 512: 2 * D], pS[1][:, 0:512],
                                 AF.Tanh).then_inc(act, 1)
            ctr.inc(act, 1)
            tile_tanh_act[it] = ctr.val(act)
            actv = ctr.val(act)
            aguard = tile_act_end.get(it - 4, 0)
            @_vec(nc)
            def _(v, par=par, par4=par4, actv=actv, ldrv=ldrv, aguard=aguard):
                v.wait_ge(act, actv)
                v.wait_ge(ldr0 if par == 0 else ldr1, ldrv)
                if aguard > 0:
                    v.wait_ge(act, aguard)
                frt = fr[:, par * D: par * D + D]
                thp = th[:, par * D: par * D + D]
                v.scalar_tensor_tensor(att[:, par4 * D: par4 * D + D], thp, 1.0,
                                       frt, ALU.add, ALU.mult)
                v.drain()
                v.engine_nop().then_inc(dve, 1)
            ctr.inc(dve, 1)
            tile_mul_dve[it] = ctr.val(dve)
            attp = att[:, par4 * D: par4 * D + D]
            x1tp = x1t[:, par * D: par * D + D]
            cen1 = att[:, ((par4 + 1) % 4) * D: ((par4 + 1) % 4) * D + D]
            cen2 = att[:, ((par4 + 2) % 4) * D: ((par4 + 2) % 4) * D + D]
            mup = mu[:, par4: par4 + 1]
            tmp1p = mu[:, 4 + par4: 5 + par4]
            ssqp = ssq[:, par4: par4 + 1]
            rstdp = rstd[:, par4: par4 + 1]
            emit_ln_fast(nc, ctr, dve, act, attp, attp, cen1, mup, ssqp, rstdp, sqs,
                         epsb, tmp1p, gtA, btA)
            dvev = ctr.val(dve)
            st_w = st_base_a + (it - 1) * 16 if it >= 2 else 0
            @_vec(nc)
            def _(v, par=par, dvev=dvev, attp=attp, st_w=st_w):
                v.wait_ge(dve, dvev)
                if st_w > 0:
                    v.wait_ge(st, st_w)
                frt = fr[:, par * D: par * D + D]
                v.tensor_add(attp, attp, frt)
                v.drain()
                v.engine_nop().then_inc(dve, 1)
            ctr.inc(dve, 1)
            tile_fadd_dve[it] = ctr.val(dve)
            emit_ln_fast(nc, ctr, dve, act, attp, x1tp, cen2, mup, ssqp, rstdp, sqs,
                         epsb, tmp1p)
            tile_act_end[it] = ctr.val(act)
            dvev = ctr.val(dve)
            st_prev = ctr.val(st)
            @_gps(nc)
            def _(gg, it=it, dvev=dvev, x1tp=x1tp, st_prev=st_prev):
                gg.wait_ge(dve, dvev)
                if it > 0:
                    gg.wait_ge(st, st_prev)
                gg.dma_start(out=x1s[it * 128:(it + 1) * 128, :], in_=x1tp).then_inc(st, 16)
            ctr.inc(st, 16)
        @_gps(nc)
        def _(g):
            g.wait_ge(st, ctr.val(st))
    return nc


def build_stage_a():
    nc = bass.Bass("TRN2", target_bir_lowering=False, debug=False)
    io = {}
    io["fT"] = nc.dram_tensor("fT", [D, T], F32, kind="ExternalInput").ap()
    io["f_tm"] = nc.dram_tensor("f_tm", [T, D], F32, kind="ExternalInput").ap()
    io["wq_T"] = nc.dram_tensor("wq_T", [D, D], F32, kind="ExternalInput").ap()
    io["w2col"] = nc.dram_tensor("w2col", [128, 8], F32, kind="ExternalInput").ap()
    io["wqbcol"] = nc.dram_tensor("wqbcol", [128, 8], F32, kind="ExternalInput").ap()
    io["maskM"] = nc.dram_tensor("maskM", [128, 3 * 256], F32, kind="ExternalInput").ap()
    io["ones164"] = nc.dram_tensor("ones164", [1, 64], F32, kind="ExternalInput").ap()
    io["onesr128"] = nc.dram_tensor("onesr128", [1, 128], F32, kind="ExternalInput").ap()
    io["w2b_in"] = nc.dram_tensor("w2b_in", [1, 1], F32, kind="ExternalInput").ap()
    io["ident"] = nc.dram_tensor("ident", [128, 128], F32, kind="ExternalInput").ap()
    io["cc_init"] = nc.dram_tensor("cc_init", [64, UW], F32, kind="ExternalInput").ap()
    io["cr_init"] = nc.dram_tensor("cr_init", [128, 65 * 65], F32, kind="ExternalInput").ap()
    io["gbcA"] = nc.dram_tensor("gbcA", [128, D], F32, kind="ExternalInput").ap()
    io["bbcA"] = nc.dram_tensor("bbcA", [128, D], F32, kind="ExternalInput").ap()
    io["x1s"] = nc.dram_tensor("x1s", [T, D], F32, kind="ExternalOutput").ap()
    ctr = Ctr()
    sems = {}
    from contextlib import ExitStack
    with ExitStack() as ses:
        for sname in ["ld", "lds", "ldf0", "ldf1", "ldr0", "ldr1", "pe", "dve", "act", "st"]:
            sems[sname] = ses.enter_context(nc.semaphore(sname))
        emit_stage_a(nc, ctr, sems, io)
    return nc


def make_stage_a_consts():
    UWl = UW
    cc = np.zeros((64, UWl), np.float32)
    cc[:, 64:128] = np.eye(64, dtype=np.float32)
    cr = np.zeros((128, 65 * 65), np.float32)
    for ch in range(65):
        u0 = ch * 128
        for p in range(128):
            u = u0 + p
            if 64 <= u < 128:
                cr[p, ch * 65 + (u - 64)] = 1.0
        cr[:, ch * 65 + 64] = 1.0
    msk = np.zeros((128, 3 * 256), np.float32)
    for c in range(3):
        for tr in range(256):
            for taur in range(128):
                dtt = tr - (taur + c * 128 - 128)
                if 0 < dtt <= 64:
                    msk[taur, c * 256 + tr] = 1.0
    return cc, cr, msk


def run_stage_a(inp):
    nc = build_stage_a()
    cc, cr, msk = make_stage_a_consts()
    f = inp["L0_attn_feature"]
    wq = inp["L0_attn_wq_w"]
    w2 = inp["L0_attn_w2_w"]
    wqb = inp["L0_attn_wq_b"]
    g_, b_ = inp["L0_attn_ln_g"], inp["L0_attn_ln_b"]
    in_maps = []
    for c in range(NCORE):
        bb = c % 2
        in_maps.append({
            "fT": np.ascontiguousarray(f[bb].T),
            "f_tm": np.ascontiguousarray(f[bb]),
            "wq_T": np.ascontiguousarray(wq.T),
            "w2col": np.ascontiguousarray(w2.reshape(8, 128).T),
            "wqbcol": np.ascontiguousarray(wqb.reshape(8, 128).T),
            "maskM": msk,
            "ones164": np.full((1, 64), np.float32(OMEGA), np.float32),
            "onesr128": np.ones((1, 128), np.float32),
            "w2b_in": np.asarray(inp["L0_attn_w2_b"], dtype=np.float32).reshape(1, 1),
            "ident": np.eye(128, dtype=np.float32),
            "cc_init": cc,
            "cr_init": cr,
            "gbcA": np.ascontiguousarray(np.broadcast_to(g_, (128, D))),
            "bbcA": np.ascontiguousarray(np.broadcast_to(b_, (128, D))),
        })
    res = run_bass_kernel_spmd(nc, in_maps, core_ids=list(range(NCORE)))
    x1 = np.stack([np.asarray(res.results[0]["x1s"]), np.asarray(res.results[1]["x1s"])], 0)
    return x1.reshape(-1, D)




def emit_stage_b(nc, ctr, sems, io):
    _pref = io.get("prefix", "b_")
    (x1full, bgidx, bjunk, rw_T, rb_bc, iota32, iotac, lt128, ones_col, ones_row,
     ident, w1_t, b1c, w2_t, b2bc, x2s, xstage, tokstage, ydram) = (
        io["x1full"], io["bgidx"], io["bjunk"], io["rw_T"], io["rb_bc"], io["iota32"],
        io["iotac"], io["lt128"], io["ones_col"], io["ones_row"], io["ident"],
        io["w1_t"], io["b1c"], io["w2_t"], io["b2bc"], io["x2s"],
        io["xstage"], io["tokstage"], io["ydram"])
    NSLOT = 32 * CAP_B
    NTOK = io["ntok"]
    NX1 = io["nx1"]
    NT = NTOK // 128
    from contextlib import ExitStack
    with ExitStack() as es:
        E = es.enter_context
        rwt = E(nc.sbuf_tensor(_pref + "rwt", [128, 8 * 32], F32))
        rbt = E(nc.sbuf_tensor(_pref + "rbt", [128, 32], F32))
        io32 = E(nc.sbuf_tensor(_pref + "io32", [128, 32], F32))
        ioc = E(nc.sbuf_tensor(_pref + "ioc", [128, 1], F32))
        tcol = E(nc.sbuf_tensor(_pref + "tcol", [128, 1], F32))
        ltt = E(nc.sbuf_tensor(_pref + "ltt", [128, 128], F32))
        onc = E(nc.sbuf_tensor(_pref + "onc", [128, 1], F32))
        onr = E(nc.sbuf_tensor(_pref + "onr", [1, 128], F32))
        idn = E(nc.sbuf_tensor(_pref + "idn", [128, 128], F32))
        hug = E(nc.sbuf_tensor(_pref + "hug", [128, NSLOT // 128], F32))
        zt = E(nc.sbuf_tensor(_pref + "zt", [128, D], F32))
        xsl = E(nc.sbuf_tensor(_pref + "xsl", [128, 8 * 128], F32))
        idxt = E(nc.sbuf_tensor(_pref + "idxt", [128, 2], I32))
        jkt = E(nc.sbuf_tensor(_pref + "jkt", [128, 1], F32))
        x1r = E(nc.sbuf_tensor(_pref + "x1r", [128, 2 * D], F32))
        lg = E(nc.sbuf_tensor(_pref + "lg", [128, 32], F32))
        mx8 = E(nc.sbuf_tensor(_pref + "mx8", [128, 8], F32))
        ix8 = E(nc.sbuf_tensor(_pref + "ix8", [128, 8], U32))
        ef32 = E(nc.sbuf_tensor(_pref + "ef32", [128, 1], F32))
        eq = E(nc.sbuf_tensor(_pref + "eq", [128, 32], F32))
        bb32 = E(nc.sbuf_tensor(_pref + "bb32", [128, 32], F32))
        tmp32 = E(nc.sbuf_tensor(_pref + "tmp32", [128, 32], F32))
        basev = E(nc.sbuf_tensor(_pref + "basev", [1, 32], F32))
        rankv = E(nc.sbuf_tensor(_pref + "rankv", [128, 1], F32))
        posv = E(nc.sbuf_tensor(_pref + "posv", [128, 1], F32))
        ovf = E(nc.sbuf_tensor(_pref + "ovf", [128, 1], F32))
        posi = E(nc.sbuf_tensor(_pref + "posi", [128, 1], I32))
        scr32 = E(nc.sbuf_tensor(_pref + "scr32", [128, 32], F32))
        xg = E(nc.sbuf_tensor(_pref + "xg", [128, D], F32))
        xgT = E(nc.sbuf_tensor(_pref + "xgT", [128, 8 * 128], BF16))
        w1sb = E(nc.sbuf_tensor(_pref + "w1sb", [128, 8 * 256], BF16))
        w2sb = E(nc.sbuf_tensor(_pref + "w2sb", [128, 2 * D], BF16))
        b1sb = E(nc.sbuf_tensor(_pref + "b1sb", [128, 2], F32))
        b2sb = E(nc.sbuf_tensor(_pref + "b2sb", [128, D], F32))
        hg = E(nc.sbuf_tensor(_pref + "hg", [128, 2 * 128], BF16))
        ysb = E(nc.sbuf_tensor(_pref + "ysb", [128, D], F32))
        tif = E(nc.sbuf_tensor(_pref + "tif", [128, 1], F32))
        ti32 = E(nc.sbuf_tensor(_pref + "ti32", [128, 1], I32))
        att = E(nc.sbuf_tensor(_pref + "att", [128, 2 * D], F32))
        sqs = E(nc.sbuf_tensor(_pref + "sqs", [128, D], F32))
        x2t = E(nc.sbuf_tensor(_pref + "x2t", [128, 2 * D], F32))
        yr = E(nc.sbuf_tensor(_pref + "yr", [128, 2 * D], F32))
        mu = E(nc.sbuf_tensor(_pref + "mu", [128, 4], F32))
        ssq = E(nc.sbuf_tensor(_pref + "ssq", [128, 2], F32))
        rstd = E(nc.sbuf_tensor(_pref + "rstd", [128, 2], F32))
        epsb = E(nc.sbuf_tensor(_pref + "epsb", [128, 1], F32))
        pL = E(nc.psum_tensor(_pref + "pL", [128, 128], F32))
        pQ = E(nc.psum_tensor(_pref + "pQ", [128, 1024], F32))
        pH = E(nc.psum_tensor(_pref + "pH", [128, 256], F32))
        lds = sems["lds"]
        ldq = sems["ldq"]
        ldg = sems["ldg"]
        sct = sems["sct"]
        yst = sems["yst"]
        zst = sems["zst"]
        gth = sems["gth"]
        pe = sems["pe"]
        dve = sems["dve"]
        act = sems["act"]
        st = sems["st"]
        rb_slot = nc.gpsimd.to_reg(NSLOT)
        rb_tok = nc.gpsimd.to_reg(NTOK)
        rb_x1 = nc.gpsimd.to_reg(NX1)

        @_syn(nc)
        def _(s):
            s.dma_start(out=rwt[:].rearrange("p (k e) -> p k e", e=32),
                        in_=rw_T.rearrange("(k p) e -> p k e", p=128)).then_inc(lds, 16)
            s.dma_start(out=rbt[:], in_=rb_bc).then_inc(lds, 16)
            s.dma_start(out=io32[:], in_=iota32).then_inc(lds, 16)
            s.dma_start(out=ioc[:], in_=iotac).then_inc(lds, 16)
            s.dma_start(out=ltt[:], in_=lt128).then_inc(lds, 16)
            s.dma_start(out=onc[:], in_=ones_col).then_inc(lds, 16)
            s.dma_start(out=onr[:], in_=ones_row).then_inc(lds, 16)
            s.dma_start(out=idn[:], in_=ident).then_inc(lds, 16)
        ctr.inc(lds, 8 * 16)
        lds_all = ctr.val(lds)

        @_vec(nc)
        def _(v):
            v.memset(hug[:], 1.0e9)
            v.drain()
            v.memset(zt[:], 0.0)
            v.drain()
            v.memset(basev[:], 0.0)
            v.drain()
            v.memset(epsb[:], LN_EPS)
            v.drain()
            v.engine_nop().then_inc(dve, 1)
        ctr.inc(dve, 1)
        ms_dve = ctr.val(dve)

        @_syn(nc)
        def _(s):
            s.wait_ge(dve, ms_dve)
            s.dma_start(out=tokstage.rearrange("(b a) c -> b (a c)", b=128),
                        in_=hug[:]).then_inc(zst, 16)
            for i in range(NT):
                s.dma_start(out=ydram[i * 128:(i + 1) * 128, :], in_=zt[:]).then_inc(zst, 16)
            for i in range(NSLOT // 128):
                s.dma_start(out=xstage[i * 128:(i + 1) * 128, :], in_=zt[:]).then_inc(zst, 16)
        ctr.inc(zst, 16 * (1 + NT + NSLOT // 128))
        zst_all = ctr.val(zst)

        sct_rt0 = ctr.val(sct)
        for it in range(NT):
            sct_prev = sct_rt0 + 32 * it
            sct_base = ctr.val(sct)
            @_syn(nc)
            def _(s, it=it, sct_prev=sct_prev):
                if it > 0:
                    s.wait_ge(sct, sct_prev)
                s.dma_start(out=idxt[:, 0:1], in_=bgidx[it * 128:(it + 1) * 128, :]).then_inc(ldq, 16)
                s.dma_start(out=jkt[:], in_=bjunk[it * 128:(it + 1) * 128, :]).then_inc(ldq, 16)
            ctr.inc(ldq, 32)
            ldqv = ctr.val(ldq)
            dve_m = ctr.val(dve)
            @_vec(nc)
            def _(v, ldqv=ldqv, it=it, sct_prev=sct_prev):
                v.wait_ge(ldq, ldqv)
                if it > 0:
                    v.wait_ge(sct, sct_prev)
                v.memset(x1r[:, 0:D], 0.0)
                v.drain()
                v.engine_nop().then_inc(dve, 1)
            ctr.inc(dve, 1)
            dvev = ctr.val(dve)
            @_gps(nc)
            def _(g, dvev=dvev):
                g.wait_ge(dve, dvev)
                g.indirect_dma_start(x1r[:, 0:D], None, x1full,
                                     IndirectOffsetOnAxis(ap=idxt[:, 0:1], axis=0),
                                     bounds_check=rb_x1, oob_is_err=False).then_inc(gth, 16)
            ctr.inc(gth, 16)
            gthv = ctr.val(gth)
            for kd in range(8):
                dve_t = ctr.val(dve)
                @_ten(nc)
                def _(t, kd=kd, gthv=gthv, dve_t=dve_t):
                    if kd == 0:
                        t.wait_ge(lds, lds_all)
                        t.wait_ge(gth, gthv)
                    else:
                        t.wait_ge(dve, dve_t)
                    t.transpose(pL[:, 0:128], x1r[:, kd * 128:(kd + 1) * 128],
                                idn[:]).then_inc(pe, 1)
                ctr.inc(pe, 1)
                pev = ctr.val(pe)
                @_vec(nc)
                def _(v, kd=kd, pev=pev):
                    v.wait_ge(pe, pev)
                    v.tensor_copy(xsl[:, kd * 128:(kd + 1) * 128], pL[:, 0:128]).then_inc(dve, 1)
                ctr.inc(dve, 1)
            dvev = ctr.val(dve)
            @_ten(nc)
            def _(t, dvev=dvev):
                t.wait_ge(dve, dvev)
                last = None
                for kd in range(8):
                    last = t.matmul(pL[:, 0:32], xsl[:, kd * 128:(kd + 1) * 128],
                                    rwt[:, kd * 32:(kd + 1) * 32],
                                    start=(kd == 0), stop=(kd == 7))
                last.then_inc(pe, 1)
            ctr.inc(pe, 1)
            pev = ctr.val(pe)
            @_vec(nc)
            def _(v, it=it, pev=pev):
                v.wait_ge(pe, pev)
                v.tensor_add(lg[:], pL[:, 0:32], rbt[:])
                v.drain()
                v.max(mx8[:], lg[:])
                v.drain()
                v.max_index(ix8[:], mx8[:], lg[:])
                v.drain()
                v.tensor_copy(ef32[:], ix8[:, 0:1])
                v.drain()
                v.tensor_scalar(eq[:], io32[:], ef32[:], None, ALU.is_equal)
                v.drain()
                v.tensor_scalar(tcol[:], ioc[:], float(it * 128), None, ALU.add)
                v.drain()
                v.engine_nop().then_inc(dve, 1)
            ctr.inc(dve, 1)
            dvev = ctr.val(dve)
            @_ten(nc)
            def _(t, dvev=dvev):
                t.wait_ge(dve, dvev)
                t.matmul(pL[:, 32:64], ltt[:], eq[:], start=True, stop=True)
                t.matmul(pL[:, 64:96], onr[:], basev[:], start=True, stop=True)
                t.matmul(pL[0:1, 96:128], onc[:], eq[:], start=True, stop=True).then_inc(pe, 1)
            ctr.inc(pe, 1)
            pev = ctr.val(pe)
            @_scl(nc)
            def _(s, pev=pev):
                s.wait_ge(pe, pev)
                s.copy(bb32[:], pL[:, 64:96]).then_inc(act, 1)
            ctr.inc(act, 1)
            actv = ctr.val(act)
            @_vec(nc)
            def _(v, actv=actv):
                v.wait_ge(act, actv)
                v.tensor_add(tmp32[:], pL[:, 32:64], bb32[:])
                v.drain()
                v.scalar_tensor_tensor(scr32[:], tmp32[:], 1.0, eq[:], ALU.mult, ALU.mult,
                                       accum_out=rankv[:])
                v.drain()
                v.tensor_scalar(posv[:], ef32[:], float(CAP_B), None, ALU.mult)
                v.drain()
                v.tensor_add(posv[:], posv[:], rankv[:])
                v.drain()
                v.tensor_add(posv[:], posv[:], jkt[:])
                v.drain()
                v.tensor_scalar(ovf[:], rankv[:], float(CAP_B), None, ALU.is_ge)
                v.drain()
                v.scalar_tensor_tensor(posv[:], ovf[:], 1.0e9, posv[:], ALU.mult, ALU.add)
                v.drain()
                v.tensor_copy(posi[:], posv[:])
                v.drain()
                v.tensor_add(basev[:], basev[:], pL[0:1, 96:128])
                v.drain()
                v.engine_nop().then_inc(dve, 1)
            ctr.inc(dve, 1)
            dvev = ctr.val(dve)
            @_gps(nc)
            def _(g, dvev=dvev):
                g.wait_ge(zst, zst_all)
                g.wait_ge(dve, dvev)
                g.indirect_dma_start(
                    xstage, IndirectOffsetOnAxis(ap=posi[:], axis=0), x1r[:, 0:D], None,
                    bounds_check=rb_slot, oob_is_err=False,
                ).then_inc(sct, 16)
                g.indirect_dma_start(
                    tokstage, IndirectOffsetOnAxis(ap=posi[:], axis=0), tcol[:], None,
                    bounds_check=rb_slot, oob_is_err=False,
                ).then_inc(sct, 16)
            ctr.inc(sct, 32)
        sct_all = ctr.val(sct)

        NSUB = (CAP_B + 127) // 128
        for e in range(32):
            dve_now = ctr.val(dve)
            @_gps(nc)
            def _(g, e=e, dve_now=dve_now):
                if e == 0:
                    g.wait_ge(sct, sct_all)
                else:
                    g.wait_ge(dve, dve_now)
                g.dma_start(out=w1sb[:].rearrange("p (k h) -> p k h", h=256),
                            in_=w1_t[e].rearrange("(k p) h -> p k h", p=128)).then_inc(ldg, 16)
                g.dma_start(out=w2sb[:].rearrange("p (k d) -> p k d", d=D),
                            in_=w2_t[e].rearrange("(k p) d -> p k d", p=128)).then_inc(ldg, 16)
            ctr.inc(ldg, 32)
            ldgv = ctr.val(ldg)
            @_syn(nc)
            def _(s, e=e, dve_now=dve_now):
                s.wait_ge(sct, sct_all)
                if e > 0:
                    s.wait_ge(dve, dve_now)
                s.dma_start(out=b1sb[:], in_=b1c[e]).then_inc(ldq, 16)
                s.dma_start(out=b2sb[:], in_=b2bc[e]).then_inc(ldq, 16)
            ctr.inc(ldq, 32)
            for sub in range(NSUB):
                stt = min(128, CAP_B - sub * 128)
                base_slot = e * CAP_B + sub * 128
                dve_sub = ctr.val(dve)
                @_syn(nc)
                def _(s, base_slot=base_slot, stt=stt, sub=sub, dve_sub=dve_sub, edve=dve_now):
                    s.wait_ge(sct, sct_all)
                    if sub > 0:
                        s.wait_ge(dve, dve_sub)
                    elif edve > 0:
                        s.wait_ge(dve, edve)
                    s.dma_start(out=xg[0:stt, :], in_=xstage[base_slot:base_slot + stt, :]).then_inc(ldq, 16)
                    s.dma_start(out=tif[0:stt, :], in_=tokstage[base_slot:base_slot + stt, :]).then_inc(ldq, 16)
                ctr.inc(ldq, 32)
                ldqv = ctr.val(ldq)
                for kd in range(8):
                    dve_t = ctr.val(dve)
                    @_ten(nc)
                    def _(t, stt=stt, kd=kd, ldqv=ldqv, dve_t=dve_t):
                        if kd == 0:
                            t.wait_ge(ldq, ldqv)
                        else:
                            t.wait_ge(dve, dve_t)
                        t.transpose(pL[0:128, 0:stt], xg[0:stt, kd * 128:(kd + 1) * 128],
                                    idn[0:stt, 0:stt]).then_inc(pe, 1)
                    ctr.inc(pe, 1)
                    pev = ctr.val(pe)
                    @_vec(nc)
                    def _(v, stt=stt, kd=kd, pev=pev):
                        v.wait_ge(pe, pev)
                        v.tensor_copy(xgT[:, kd * 128: kd * 128 + stt],
                                      pL[0:128, 0:stt]).then_inc(dve, 1)
                    ctr.inc(dve, 1)
                dvev = ctr.val(dve)
                @_ten(nc)
                def _(t, stt=stt, dvev=dvev, ldgv=ldgv):
                    t.wait_ge(dve, dvev)
                    t.wait_ge(ldg, ldgv)
                    last = None
                    for hc in range(2):
                        for kd in range(8):
                            last = t.matmul(pH[:, hc * 128: hc * 128 + stt],
                                            w1sb[:, kd * 256 + hc * 128: kd * 256 + hc * 128 + 128],
                                            xgT[:, kd * 128: kd * 128 + stt],
                                            start=(kd == 0), stop=(kd == 7))
                    last.then_inc(pe, 1)
                ctr.inc(pe, 1)
                pev = ctr.val(pe)
                @_scl(nc)
                def _(s, stt=stt, pev=pev):
                    s.wait_ge(pe, pev)
                    s.activation(hg[:, 0:stt], pH[:, 0:stt], AF.Relu, bias=b1sb[:, 0:1])
                    s.drain()
                    s.activation(hg[:, 128:128 + stt], pH[:, 128:128 + stt], AF.Relu,
                                 bias=b1sb[:, 1:2]).then_inc(act, 1)
                ctr.inc(act, 1)
                actv = ctr.val(act)
                @_ten(nc)
                def _(t, stt=stt, actv=actv):
                    t.wait_ge(act, actv)
                    last = None
                    for dh in range(2):
                        for kh in range(2):
                            last = t.matmul(pQ[0:stt, dh * 512: dh * 512 + 512],
                                            hg[:, kh * 128: kh * 128 + stt],
                                            w2sb[:, kh * D + dh * 512: kh * D + dh * 512 + 512],
                                            start=(kh == 0), stop=(kh == 1))
                    last.then_inc(pe, 1)
                ctr.inc(pe, 1)
                pev = ctr.val(pe)
                yst_prev = ctr.val(yst)
                @_vec(nc)
                def _(v, stt=stt, pev=pev, yst_prev=yst_prev):
                    v.wait_ge(pe, pev)
                    if yst_prev > 0:
                        v.wait_ge(yst, yst_prev)
                    v.scalar_tensor_tensor(ysb[0:stt, :], pQ[0:stt, :], 1.0,
                                           b2sb[0:stt, :], ALU.mult, ALU.add)
                    v.drain()
                    v.tensor_scalar_max(ysb[0:stt, :], ysb[0:stt, :], 0.0)
                    v.drain()
                    v.tensor_copy(ti32[0:stt, :], tif[0:stt, :])
                    v.drain()
                    v.engine_nop().then_inc(dve, 1)
                ctr.inc(dve, 1)
                dvev = ctr.val(dve)
                @_gps(nc)
                def _(g, stt=stt, dvev=dvev):
                    g.wait_ge(dve, dvev)
                    g.indirect_dma_start(
                        ydram, IndirectOffsetOnAxis(ap=ti32[0:stt, :], axis=0),
                        ysb[0:stt, :], None,
                        bounds_check=rb_tok, oob_is_err=False,
                    ).then_inc(yst, 16)
                ctr.inc(yst, 16)
        yst_all = ctr.val(yst)

        st_base_f = ctr.val(st)
        fin_gth = {}
        fin_add = {}
        for it in range(NT):
            par = it % 2
            ldq_prev = ctr.val(ldq)
            gguard = fin_gth.get(it - 2, 0)
            aguard2 = fin_add.get(it - 2, 0)
            @_syn(nc)
            def _(s, it=it, par=par, ldq_prev=ldq_prev, gguard=gguard, aguard2=aguard2):
                if it == 0:
                    s.wait_ge(yst, yst_all)
                else:
                    s.wait_ge(ldq, ldq_prev)
                if gguard > 0:
                    s.wait_ge(gth, gguard)
                if aguard2 > 0:
                    s.wait_ge(dve, aguard2)
                s.dma_start(out=yr[:, par * D: par * D + D],
                            in_=ydram[it * 128:(it + 1) * 128, :]).then_inc(ldq, 16)
                s.dma_start(out=idxt[:, par: par + 1],
                            in_=bgidx[it * 128:(it + 1) * 128, :]).then_inc(ldq, 16)
            ctr.inc(ldq, 32)
            ldqv = ctr.val(ldq)
            @_vec(nc)
            def _(v, par=par, ldqv=ldqv):
                v.wait_ge(ldq, ldqv)
                v.memset(x1r[:, par * D: par * D + D], 0.0)
                v.drain()
                v.engine_nop().then_inc(dve, 1)
            ctr.inc(dve, 1)
            dvev = ctr.val(dve)
            gth_prev = ctr.val(gth)
            @_gps(nc)
            def _(g, par=par, dvev=dvev, gth_prev=gth_prev, it=it):
                g.wait_ge(dve, dvev)
                if it > 0:
                    g.wait_ge(gth, gth_prev)
                g.indirect_dma_start(x1r[:, par * D: par * D + D], None, x1full,
                                     IndirectOffsetOnAxis(ap=idxt[:, par: par + 1], axis=0),
                                     bounds_check=rb_x1, oob_is_err=False).then_inc(gth, 16)
            ctr.inc(gth, 16)
            gthv = ctr.val(gth)
            fin_gth[it] = gthv
            st_w = st_base_f + (it - 1) * 16 if it >= 2 else 0
            @_vec(nc)
            def _(v, par=par, gthv=gthv, st_w=st_w):
                v.wait_ge(gth, gthv)
                if st_w > 0:
                    v.wait_ge(st, st_w)
                v.tensor_add(att[:, par * D: par * D + D],
                             x1r[:, par * D: par * D + D],
                             yr[:, par * D: par * D + D])
                v.drain()
                v.engine_nop().then_inc(dve, 1)
            ctr.inc(dve, 1)
            fin_add[it] = ctr.val(dve)
            emit_ln_fast(nc, ctr, dve, act,
                         att[:, par * D: par * D + D],
                         x2t[:, par * D: par * D + D],
                         x1r[:, par * D: par * D + D],
                         mu[:, par: par + 1], ssq[:, par: par + 1],
                         rstd[:, par: par + 1], sqs, epsb,
                         mu[:, 2 + par: 3 + par])
            dvev = ctr.val(dve)
            st_prev = ctr.val(st)
            @_gps(nc)
            def _(g, it=it, par=par, dvev=dvev, st_prev=st_prev):
                g.wait_ge(dve, dvev)
                if it > 0:
                    g.wait_ge(st, st_prev)
                g.dma_start(out=x2s[it * 128:(it + 1) * 128, :],
                            in_=x2t[:, par * D: par * D + D]).then_inc(st, 16)
            ctr.inc(st, 16)
        @_gps(nc)
        def _(g):
            g.wait_ge(st, ctr.val(st))
    return nc


def build_stage_b():
    nc = bass.Bass("TRN2", target_bir_lowering=False, debug=False)
    NSLOT = 32 * CAP_B
    io = {}
    io["ntok"] = TPC
    io["nx1"] = TPC
    io["x1full"] = nc.dram_tensor("x1full", [TPC, D], F32, kind="ExternalInput").ap()
    io["bgidx"] = nc.dram_tensor("bgidx", [TPC, 1], I32, kind="ExternalInput").ap()
    io["bjunk"] = nc.dram_tensor("bjunk", [TPC, 1], F32, kind="ExternalInput").ap()
    io["rw_T"] = nc.dram_tensor("rw_T", [D, 32], F32, kind="ExternalInput").ap()
    io["rb_bc"] = nc.dram_tensor("rb_bc", [128, 32], F32, kind="ExternalInput").ap()
    io["iota32"] = nc.dram_tensor("iota32", [128, 32], F32, kind="ExternalInput").ap()
    io["iotac"] = nc.dram_tensor("iotac", [128, 1], F32, kind="ExternalInput").ap()
    io["lt128"] = nc.dram_tensor("lt128", [128, 128], F32, kind="ExternalInput").ap()
    io["ones_col"] = nc.dram_tensor("ones_col", [128, 1], F32, kind="ExternalInput").ap()
    io["ones_row"] = nc.dram_tensor("ones_row", [1, 128], F32, kind="ExternalInput").ap()
    io["ident"] = nc.dram_tensor("ident", [128, 128], F32, kind="ExternalInput").ap()
    io["w1_t"] = nc.dram_tensor("w1_t", [32, D, 256], F32, kind="ExternalInput").ap()
    io["b1c"] = nc.dram_tensor("b1c", [32, 128, 2], F32, kind="ExternalInput").ap()
    io["w2_t"] = nc.dram_tensor("w2_t", [32, 256, D], F32, kind="ExternalInput").ap()
    io["b2bc"] = nc.dram_tensor("b2bc", [32, 128, D], F32, kind="ExternalInput").ap()
    io["x2s"] = nc.dram_tensor("x2s", [TPC, D], F32, kind="ExternalOutput").ap()
    io["xstage"] = nc.dram_tensor("xstage", [NSLOT, D], F32, kind="Internal").ap()
    io["tokstage"] = nc.dram_tensor("tokstage", [NSLOT, 1], F32, kind="Internal").ap()
    io["ydram"] = nc.dram_tensor("ydram", [TPC, D], F32, kind="Internal").ap()
    ctr = Ctr()
    sems = {}
    from contextlib import ExitStack
    with ExitStack() as ses:
        for sname in ["lds", "ldq", "ldg", "sct", "yst", "zst", "gth", "pe", "dve", "act", "st"]:
            sems[sname] = ses.enter_context(nc.semaphore(sname))
        emit_stage_b(nc, ctr, sems, io)
    return nc




def run_stage_b(inp, x1_full):
    nc = build_stage_b()
    lt = np.tril(np.ones((128, 128), np.float32), -1)
    shared = {
        "rw_T": np.ascontiguousarray(inp["L0_ffn_router_w"].T),
        "rb_bc": np.ascontiguousarray(np.broadcast_to(inp["L0_ffn_router_b"], (128, 32))),
        "iota32": np.ascontiguousarray(np.broadcast_to(np.arange(32, dtype=np.float32), (128, 32))),
        "iotac": np.arange(128, dtype=np.float32).reshape(128, 1),
        "lt128": lt,
        "ones_col": np.ones((128, 1), np.float32),
        "ones_row": np.ones((1, 128), np.float32),
        "ident": np.eye(128, dtype=np.float32),
        "w1_t": np.ascontiguousarray(inp["L0_ffn_weights1"]),
        "b1c": np.ascontiguousarray(inp["L0_ffn_bias1"].reshape(32, 2, 128).transpose(0, 2, 1)),
        "w2_t": np.ascontiguousarray(inp["L0_ffn_weights2"]),
        "b2bc": np.ascontiguousarray(np.broadcast_to(inp["L0_ffn_bias2"][:, None, :], (32, 128, 1024))),
        }
    in_maps = []
    for c in range(NCORE):
        sl = x1_full[c * TPC:(c + 1) * TPC]
        in_maps.append(dict(shared, x1s=np.ascontiguousarray(sl),
                            x1sT=np.ascontiguousarray(sl.T)))
    res = run_bass_kernel_spmd(nc, in_maps, core_ids=list(range(NCORE)))
    return np.concatenate([np.asarray(r["x2s"]) for r in res.results], 0)




def emit_stage_d(nc, ctr, sems, io):
    _pref = io.get("prefix", "d_")
    (x3s, gw_T, gb_bc, iota8, iotac, lt128, ones_col, ones_row, ident,
     w1_t, w3_t, w2_t, sw1_t, sw3_t, sw2_t, x4s,
     xstage, tokstage, gstage, y1dram, y2dram, zdram) = (
        io["x3s"], io["gw_T"], io["gb_bc"], io["iota8"], io["iotac"],
        io["lt128"], io["ones_col"], io["ones_row"], io["ident"],
        io["w1_t"], io["w3_t"], io["w2_t"], io["sw1_t"], io["sw3_t"], io["sw2_t"],
        io["x4s"], io["xstage"], io["tokstage"], io["gstage"],
        io["y1dram"], io["y2dram"], io["zdram"])
    NSLOT = 16 * CAP_D
    NT = TPC // 128
    from contextlib import ExitStack
    with ExitStack() as es:
        E = es.enter_context
        gwt = E(nc.sbuf_tensor(_pref + "gwt", [128, 8 * 8], F32))
        gbt = E(nc.sbuf_tensor(_pref + "gbt", [128, 8], F32))
        io8 = E(nc.sbuf_tensor(_pref + "io8", [128, 8], F32))
        ioc = E(nc.sbuf_tensor(_pref + "ioc", [128, 1], F32))
        tcol = E(nc.sbuf_tensor(_pref + "tcol", [128, 1], F32))
        gcol = E(nc.sbuf_tensor(_pref + "gcol", [128, 1], F32))
        ltt = E(nc.sbuf_tensor(_pref + "ltt", [128, 128], F32))
        onc = E(nc.sbuf_tensor(_pref + "onc", [128, 1], F32))
        onr = E(nc.sbuf_tensor(_pref + "onr", [1, 128], F32))
        idn = E(nc.sbuf_tensor(_pref + "idn", [128, 128], F32))
        hug = E(nc.sbuf_tensor(_pref + "hug", [128, NSLOT // 128], F32))
        zt = E(nc.sbuf_tensor(_pref + "zt", [128, D], F32))
        xsl = E(nc.sbuf_tensor(_pref + "xsl", [128, 8 * 128], F32))
        x3T = E(nc.sbuf_tensor(_pref + "x3T", [128, 8 * TPC], BF16))
        x3r = E(nc.sbuf_tensor(_pref + "x3r", [128, 2 * D], F32))
        lg = E(nc.sbuf_tensor(_pref + "lg", [128, 8], F32))
        nmx = E(nc.sbuf_tensor(_pref + "nmx", [128, 1], F32))
        e8 = E(nc.sbuf_tensor(_pref + "e8", [128, 8], F32))
        sumv = E(nc.sbuf_tensor(_pref + "sumv", [128, 1], F32))
        rz1 = E(nc.sbuf_tensor(_pref + "rz1", [128, 1], F32))
        probs = E(nc.sbuf_tensor(_pref + "probs", [128, 8], F32))
        scb = E(nc.sbuf_tensor(_pref + "scb", [128, 8], F32))
        mx8 = E(nc.sbuf_tensor(_pref + "mx8", [128, 8], F32))
        ix8 = E(nc.sbuf_tensor(_pref + "ix8", [128, 8], U32))
        ef32 = E(nc.sbuf_tensor(_pref + "ef32", [128, 1], F32))
        eq8 = E(nc.sbuf_tensor(_pref + "eq8", [128, 8], F32))
        bb8 = E(nc.sbuf_tensor(_pref + "bb8", [128, 8], F32))
        tmp8 = E(nc.sbuf_tensor(_pref + "tmp8", [128, 8], F32))
        scr8 = E(nc.sbuf_tensor(_pref + "scr8", [128, 8], F32))
        basev = E(nc.sbuf_tensor(_pref + "basev", [1, 16], F32))
        rankv = E(nc.sbuf_tensor(_pref + "rankv", [128, 1], F32))
        posv = E(nc.sbuf_tensor(_pref + "posv", [128, 1], F32))
        ovf = E(nc.sbuf_tensor(_pref + "ovf", [128, 1], F32))
        posi = E(nc.sbuf_tensor(_pref + "posi", [128, 1], I32))
        mu = E(nc.sbuf_tensor(_pref + "mu", [128, 4], F32))
        ssq = E(nc.sbuf_tensor(_pref + "ssq", [128, 2], F32))
        rstd = E(nc.sbuf_tensor(_pref + "rstd", [128, 2], F32))
        epsb = E(nc.sbuf_tensor(_pref + "epsb", [128, 1], F32))
        att = E(nc.sbuf_tensor(_pref + "att", [128, 2 * D], F32))
        sqs = E(nc.sbuf_tensor(_pref + "sqs", [128, D], F32))
        x4t = E(nc.sbuf_tensor(_pref + "x4t", [128, 2 * D], F32))
        yr1 = E(nc.sbuf_tensor(_pref + "yr1", [128, 2 * D], F32))
        yr2 = E(nc.sbuf_tensor(_pref + "yr2", [128, 2 * D], F32))
        pL = E(nc.psum_tensor(_pref + "pL", [128, 128], F32))
        pH1 = E(nc.psum_tensor(_pref + "pH1", [128, 512], F32))
        pH3 = E(nc.psum_tensor(_pref + "pH3", [128, 512], F32))
        pL2 = E(nc.psum_tensor(_pref + "pL2", [128, 128], F32))
        pH1b = E(nc.psum_tensor(_pref + "pH1b", [128, 512], F32))
        pH3b = E(nc.psum_tensor(_pref + "pH3b", [128, 512], F32))
        pQ = E(nc.psum_tensor(_pref + "pQ", [128, 1024], F32))
        lds = sems["lds"]
        ldr0 = sems["ldr0"]
        ldr1 = sems["ldr1"]
        ldsh = sems["ldsh"]
        ldq = sems["ldq"]
        ldg = sems["ldg"]
        sct = sems["sct"]
        yst = sems["yst"]
        zst = sems["zst"]
        pe = sems["pe"]
        dve = sems["dve"]
        act = sems["act"]
        st = sems["st"]
        ldsh = sems["ldsh"]
        rb_slot = nc.gpsimd.to_reg(NSLOT)
        rb_tok = nc.gpsimd.to_reg(TPC)

        @_syn(nc)
        def _(s):
            s.dma_start(out=gwt[:].rearrange("p (k e) -> p k e", e=8),
                        in_=gw_T.rearrange("(k p) e -> p k e", p=128)).then_inc(lds, 16)
            s.dma_start(out=gbt[:], in_=gb_bc).then_inc(lds, 16)
            s.dma_start(out=io8[:], in_=iota8).then_inc(lds, 16)
            s.dma_start(out=ioc[:], in_=iotac).then_inc(lds, 16)
            s.dma_start(out=ltt[:], in_=lt128).then_inc(lds, 16)
            s.dma_start(out=onc[:], in_=ones_col).then_inc(lds, 16)
            s.dma_start(out=onr[:], in_=ones_row).then_inc(lds, 16)
            s.dma_start(out=idn[:], in_=ident).then_inc(lds, 16)
        ctr.inc(lds, 8 * 16)
        lds_all = ctr.val(lds)

        @_vec(nc)
        def _(v):
            v.memset(hug[:], 1.0e9)
            v.drain()
            v.memset(zt[:], 0.0)
            v.drain()
            v.memset(basev[:], 0.0)
            v.drain()
            v.memset(epsb[:], LN_EPS)
            v.drain()
            v.engine_nop().then_inc(dve, 1)
        ctr.inc(dve, 1)
        ms_dve = ctr.val(dve)

        @_syn(nc)
        def _(s):
            s.wait_ge(dve, ms_dve)
            s.dma_start(out=tokstage.rearrange("(b a) c -> b (a c)", b=128),
                        in_=hug[:]).then_inc(zst, 16)
            s.dma_start(out=gstage.rearrange("(b a) c -> b (a c)", b=128),
                        in_=zt[:, 0:NSLOT // 128]).then_inc(zst, 16)
            for i in range(NT):
                s.dma_start(out=y1dram[i * 128:(i + 1) * 128, :], in_=zt[:]).then_inc(zst, 16)
                s.dma_start(out=y2dram[i * 128:(i + 1) * 128, :], in_=zt[:]).then_inc(zst, 16)
            for i in range(NSLOT // 128):
                s.dma_start(out=xstage[i * 128:(i + 1) * 128, :], in_=zt[:]).then_inc(zst, 16)
        ctr.inc(zst, 16 * (2 + 2 * NT + NSLOT // 128))
        zst_all = ctr.val(zst)

        sct_rt0 = ctr.val(sct)
        for it in range(NT):
            sct_prev = sct_rt0 + 16 * 6 * it
            @_syn(nc)
            def _(s, it=it, sct_prev=sct_prev):
                if it > 0:
                    s.wait_ge(sct, sct_prev)
                s.dma_start(out=x3r[:, 0:D], in_=x3s[it * 128:(it + 1) * 128, :]).then_inc(ldq, 16)
            ctr.inc(ldq, 16)
            ldqv = ctr.val(ldq)
            for kd in range(8):
                dve_t = ctr.val(dve)
                @_ten(nc)
                def _(t, kd=kd, ldqv=ldqv, dve_t=dve_t):
                    if kd == 0:
                        t.wait_ge(lds, lds_all)
                        t.wait_ge(ldq, ldqv)
                    else:
                        t.wait_ge(dve, dve_t)
                    t.transpose(pL[:, 0:128], x3r[:, kd * 128:(kd + 1) * 128],
                                idn[:]).then_inc(pe, 1)
                ctr.inc(pe, 1)
                pev = ctr.val(pe)
                @_vec(nc)
                def _(v, kd=kd, it=it, pev=pev):
                    v.wait_ge(pe, pev)
                    v.tensor_copy(xsl[:, kd * 128:(kd + 1) * 128], pL[:, 0:128])
                    v.drain()
                    v.tensor_copy(x3T[:, kd * TPC + it * 128: kd * TPC + (it + 1) * 128],
                                  pL[:, 0:128])
                    v.drain()
                    v.engine_nop().then_inc(dve, 1)
                ctr.inc(dve, 1)
            dvev = ctr.val(dve)
            @_ten(nc)
            def _(t, dvev=dvev):
                t.wait_ge(dve, dvev)
                last = None
                for kd in range(8):
                    last = t.matmul(pL[:, 0:8], xsl[:, kd * 128:(kd + 1) * 128],
                                    gwt[:, kd * 8:(kd + 1) * 8],
                                    start=(kd == 0), stop=(kd == 7))
                last.then_inc(pe, 1)
            ctr.inc(pe, 1)
            pev = ctr.val(pe)
            @_vec(nc)
            def _(v, pev=pev):
                v.wait_ge(pe, pev)
                v.reduce_max(nmx[:], pL[:, 0:8], axis=AX.X, negate=True)
                v.drain()
                v.tensor_copy(lg[:], pL[:, 0:8])
                v.drain()
                v.engine_nop().then_inc(dve, 1)
            ctr.inc(dve, 1)
            dvev = ctr.val(dve)
            @_scl(nc)
            def _(s, dvev=dvev):
                s.wait_ge(dve, dvev)
                s.activation(e8[:], lg[:], AF.Exp, bias=nmx[:], accum_out=sumv[:]).then_inc(act, 1)
            ctr.inc(act, 1)
            actv = ctr.val(act)
            @_vec(nc)
            def _(v, actv=actv):
                v.wait_ge(act, actv)
                v.reciprocal(rz1[:], sumv[:])
                v.drain()
                v.tensor_scalar(probs[:], e8[:], rz1[:], None, ALU.mult)
                v.drain()
                v.tensor_add(scb[:], probs[:], gbt[:])
                v.drain()
                v.max(mx8[:], scb[:])
                v.drain()
                v.max_index(ix8[:], mx8[:], scb[:])
                v.drain()
                v.tensor_scalar(tcol[:], ioc[:], float(it * 128), None, ALU.add)
                v.drain()
                v.engine_nop().then_inc(dve, 1)
            ctr.inc(dve, 1)
            for k in range(2):
                dvev = ctr.val(dve)
                sct_k = ctr.val(sct)
                @_vec(nc)
                def _(v, k=k, sct_k=sct_k):
                    if k == 1:
                        v.wait_ge(sct, sct_k)
                    v.tensor_copy(ef32[:], ix8[:, k:k + 1])
                    v.drain()
                    v.tensor_scalar(eq8[:], io8[:], ef32[:], None, ALU.is_equal)
                    v.drain()
                    v.scalar_tensor_tensor(scr8[:], probs[:], 1.0, eq8[:], ALU.mult, ALU.mult,
                                           accum_out=gcol[:])
                    v.drain()
                    v.engine_nop().then_inc(dve, 1)
                ctr.inc(dve, 1)
                dvev = ctr.val(dve)
                @_ten(nc)
                def _(t, k=k, dvev=dvev):
                    t.wait_ge(dve, dvev)
                    t.matmul(pL[:, 16:24], ltt[:], eq8[:], start=True, stop=True)
                    t.matmul(pL[:, 32:40], onr[:], basev[0:1, k * 8:(k + 1) * 8],
                             start=True, stop=True)
                    t.matmul(pL[0:1, 48:56], onc[:], eq8[:], start=True, stop=True).then_inc(pe, 1)
                ctr.inc(pe, 1)
                pev = ctr.val(pe)
                @_scl(nc)
                def _(s, pev=pev):
                    s.wait_ge(pe, pev)
                    s.copy(bb8[:], pL[:, 32:40]).then_inc(act, 1)
                ctr.inc(act, 1)
                actv = ctr.val(act)
                @_vec(nc)
                def _(v, k=k, actv=actv):
                    v.wait_ge(act, actv)
                    v.tensor_add(tmp8[:], pL[:, 16:24], bb8[:])
                    v.drain()
                    v.scalar_tensor_tensor(scr8[:], tmp8[:], 1.0, eq8[:], ALU.mult, ALU.mult,
                                           accum_out=rankv[:])
                    v.drain()
                    v.tensor_scalar(posv[:], ef32[:], float(CAP_D), None, ALU.mult)
                    v.drain()
                    v.tensor_add(posv[:], posv[:], rankv[:])
                    v.drain()
                    v.tensor_scalar(posv[:], posv[:], float(k * 8 * CAP_D), None, ALU.add)
                    v.drain()
                    v.tensor_scalar(ovf[:], rankv[:], float(CAP_D), None, ALU.is_ge)
                    v.drain()
                    v.scalar_tensor_tensor(posv[:], ovf[:], 1.0e9, posv[:], ALU.mult, ALU.add)
                    v.drain()
                    v.tensor_copy(posi[:], posv[:])
                    v.drain()
                    v.tensor_add(basev[0:1, k * 8:(k + 1) * 8], basev[0:1, k * 8:(k + 1) * 8],
                                 pL[0:1, 48:56])
                    v.drain()
                    v.engine_nop().then_inc(dve, 1)
                ctr.inc(dve, 1)
                dvev = ctr.val(dve)
                @_gps(nc)
                def _(g, dvev=dvev):
                    g.wait_ge(zst, zst_all)
                    g.wait_ge(dve, dvev)
                    g.indirect_dma_start(
                        xstage, IndirectOffsetOnAxis(ap=posi[:], axis=0), x3r[:, 0:D], None,
                        bounds_check=rb_slot, oob_is_err=False).then_inc(sct, 16)
                    g.indirect_dma_start(
                        tokstage, IndirectOffsetOnAxis(ap=posi[:], axis=0), tcol[:], None,
                        bounds_check=rb_slot, oob_is_err=False).then_inc(sct, 16)
                    g.indirect_dma_start(
                        gstage, IndirectOffsetOnAxis(ap=posi[:], axis=0), gcol[:], None,
                        bounds_check=rb_slot, oob_is_err=False).then_inc(sct, 16)
                ctr.inc(sct, 48)
        sct_all = ctr.val(sct)

        es_sh = ExitStack()
        sw1s = es_sh.enter_context(nc.sbuf_tensor(_pref + "sw1s", [128, 8 * 2048], BF16))
        sw3s = es_sh.enter_context(nc.sbuf_tensor(_pref + "sw3s", [128, 8 * 2048], BF16))
        sw2s = es_sh.enter_context(nc.sbuf_tensor(_pref + "sw2s", [128, 16 * D], BF16))
        hssh = es_sh.enter_context(nc.sbuf_tensor(_pref + "hssh", [128, 2 * 512], F32))
        hgsh = es_sh.enter_context(nc.sbuf_tensor(_pref + "hgsh", [128, 16 * 512], BF16))
        @_gps(nc)
        def _(g):
            g.wait_ge(sct, sct_all)
            g.dma_start(out=sw1s[:].rearrange("p (k i) -> p k i", i=2048),
                        in_=sw1_t.rearrange("(k p) i -> p k i", p=128)).then_inc(ldg, 16)
            g.dma_start(out=sw3s[:].rearrange("p (k i) -> p k i", i=2048),
                        in_=sw3_t.rearrange("(k p) i -> p k i", p=128)).then_inc(ldg, 16)
            g.dma_start(out=sw2s[:].rearrange("p (k d) -> p k d", d=D),
                        in_=sw2_t.rearrange("(k p) d -> p k d", p=128)).then_inc(ldg, 16)
        ctr.inc(ldg, 48)
        ldg_sh = ctr.val(ldg)

        sh_mul = {}
        for it4 in range(4):
            for ic in range(16):
                pH1c = pH1 if ic % 2 == 0 else pH1b
                pH3c = pH3 if ic % 2 == 0 else pH3b
                hsl = hssh[:, (ic % 2) * 512: (ic % 2) * 512 + 512]
                dve_pre = ctr.val(dve)
                @_ten(nc)
                def _(t, ic=ic, it4=it4, dve_pre=dve_pre, pH1c=pH1c, pH3c=pH3c):
                    t.wait_ge(dve, dve_pre)
                    if ic == 0 and it4 == 0:
                        t.wait_ge(ldg, ldg_sh)
                    last = None
                    for kd in range(8):
                        t.matmul(pH1c[:, 0:512],
                                 sw1s[:, kd * 2048 + ic * 128: kd * 2048 + ic * 128 + 128],
                                 x3T[:, kd * TPC + it4 * 512: kd * TPC + (it4 + 1) * 512],
                                 start=(kd == 0), stop=(kd == 7))
                        last = t.matmul(pH3c[:, 0:512],
                                        sw3s[:, kd * 2048 + ic * 128: kd * 2048 + ic * 128 + 128],
                                        x3T[:, kd * TPC + it4 * 512: kd * TPC + (it4 + 1) * 512],
                                        start=(kd == 0), stop=(kd == 7))
                    last.then_inc(pe, 1)
                ctr.inc(pe, 1)
                pev = ctr.val(pe)
                sguard = sh_mul.get(ic - 2, 0)
                @_scl(nc)
                def _(s, pev=pev, pH1c=pH1c, hsl=hsl, sguard=sguard):
                    s.wait_ge(pe, pev)
                    if sguard > 0:
                        s.wait_ge(dve, sguard)
                    s.activation(hsl, pH1c[:, 0:512], AF.Silu).then_inc(act, 1)
                ctr.inc(act, 1)
                actv = ctr.val(act)
                @_vec(nc)
                def _(v, ic=ic, actv=actv, pH3c=pH3c, hsl=hsl):
                    v.wait_ge(act, actv)
                    v.tensor_mul(hgsh[:, ic * 512:(ic + 1) * 512], hsl,
                                 pH3c[:, 0:512]).then_inc(dve, 1)
                ctr.inc(dve, 1)
                sh_mul[ic] = ctr.val(dve)
            hg_done = ctr.val(dve)
            w2ev = []
            for tc in range(4):
                wpar = tc % 2
                bank_guard = w2ev[tc - 2] if tc >= 2 else 0
                @_ten(nc)
                def _(t, tc=tc, wpar=wpar, hg_done=hg_done, bank_guard=bank_guard):
                    if tc == 0:
                        t.wait_ge(dve, hg_done)
                    elif bank_guard > 0:
                        t.wait_ge(dve, bank_guard)
                    last = None
                    for dh in range(2):
                        if wpar == 0:
                            out = pQ[:, dh * 512: dh * 512 + 512]
                        else:
                            out = (pH1 if dh == 0 else pH3)[:, 0:512]
                        for ki in range(16):
                            last = t.matmul(out,
                                            hgsh[:, ki * 512 + tc * 128: ki * 512 + tc * 128 + 128],
                                            sw2s[:, ki * D + dh * 512: ki * D + dh * 512 + 512],
                                            start=(ki == 0), stop=(ki == 15))
                    last.then_inc(pe, 1)
                ctr.inc(pe, 1)
                pev = ctr.val(pe)
                st_att = ctr.val(st) - 16 if tc >= 2 or it4 > 0 else 0
                @_vec(nc)
                def _(v, tc=tc, wpar=wpar, pev=pev, st_att=st_att):
                    v.wait_ge(pe, pev)
                    if st_att > 0:
                        v.wait_ge(st, st_att)
                    zsl = att[:, (tc % 2) * D: (tc % 2) * D + D]
                    if wpar == 0:
                        v.tensor_copy(zsl, pQ[:, 0:1024])
                    else:
                        v.tensor_copy(zsl[:, 0:512], pH1[:, 0:512])
                        v.drain()
                        v.tensor_copy(zsl[:, 512:1024], pH3[:, 0:512])
                    v.drain()
                    v.engine_nop().then_inc(dve, 1)
                ctr.inc(dve, 1)
                w2ev.append(ctr.val(dve))
                dvev = ctr.val(dve)
                st_prev = ctr.val(st)
                @_gps(nc)
                def _(g, it4=it4, tc=tc, dvev=dvev, st_prev=st_prev):
                    g.wait_ge(dve, dvev)
                    if st_prev > 0:
                        g.wait_ge(st, st_prev)
                    g.dma_start(out=zdram[it4 * 512 + tc * 128: it4 * 512 + (tc + 1) * 128, :],
                                in_=att[:, (tc % 2) * D: (tc % 2) * D + D]).then_inc(st, 16)
                ctr.inc(st, 16)
        sh_st = ctr.val(st)
        pe_sh = ctr.val(pe)
        es_sh.close()

        w1sb = E(nc.sbuf_tensor(_pref + "w1sb", [128, 2 * 8 * 1024], BF16))
        w3sb = E(nc.sbuf_tensor(_pref + "w3sb", [128, 2 * 8 * 1024], BF16))
        w2sb = E(nc.sbuf_tensor(_pref + "w2sb", [128, 8 * D], BF16))
        xgs = E(nc.sbuf_tensor(_pref + "xgs", [128, 2 * D], F32))
        xgT5 = E(nc.sbuf_tensor(_pref + "xgT5", [128, 8 * 512], BF16))
        hg5 = E(nc.sbuf_tensor(_pref + "hg5", [128, 8 * 512], BF16))
        hs5 = E(nc.sbuf_tensor(_pref + "hs5", [128, 2 * 512], F32))
        ysb = E(nc.sbuf_tensor(_pref + "ysb", [128, D], F32))
        tif = E(nc.sbuf_tensor(_pref + "tif", [128, 8], F32))
        gif = E(nc.sbuf_tensor(_pref + "gif", [128, 8], F32))
        ti32 = E(nc.sbuf_tensor(_pref + "ti32", [128, 1], I32))
        sub_tev = {}
        sub_w2e = {}
        st_shared_done = ctr.val(st)
        gsub = 0
        exp_pe = {}
        gtc = 0
        for e in range(8):
            ep = e % 2
            ldw = ldg if ep == 0 else ldsh
            st_now = ctr.val(st)
            wguard_e = exp_pe.get(e - 2, 0)
            @_gps(nc)
            def _(g, e=e, ep=ep, st_now=st_now, wguard_e=wguard_e, ldw=ldw):
                if e == 0:
                    g.wait_ge(st, st_now)
                if wguard_e > 0:
                    g.wait_ge(pe, wguard_e)
                g.dma_start(out=w1sb[:, ep * 8192: (ep + 1) * 8192].rearrange("p (k i) -> p k i", i=1024),
                            in_=w1_t[e].rearrange("(k p) i -> p k i", p=128)).then_inc(ldw, 16)
                g.dma_start(out=w3sb[:, ep * 8192: (ep + 1) * 8192].rearrange("p (k i) -> p k i", i=1024),
                            in_=w3_t[e].rearrange("(k p) i -> p k i", p=128)).then_inc(ldw, 16)
            ctr.inc(ldw, 32)
            w2guard = exp_pe.get(e - 1, 0)
            @_gps(nc)
            def _(g, e=e, w2guard=w2guard, ldw=ldw):
                if w2guard > 0:
                    g.wait_ge(pe, w2guard)
                g.dma_start(out=w2sb[:].rearrange("p (k d) -> p k d", d=D),
                            in_=w2_t[e].rearrange("(k p) d -> p k d", p=128)).then_inc(ldw, 16)
            ctr.inc(ldw, 16)
            ldgv = ctr.val(ldw)
            wof1 = ep * 8192
            for k in range(2):
                for sub in range(2):
                    par = gsub % 2
                    base = (k * 8 + e) * CAP_D + sub * 512
                    wguard = sub_w2e.get(gsub - 2, 0)
                    tc_ev = {}
                    ti = 0
                    for tc in range(4):
                        tp = gtc % 2
                        ldrp = ldr0 if tp == 0 else ldr1
                        prev_tr = tc_ev.get(ti - 9, sub_tev.get(gsub - 1, 0))
                        @_syn(nc)
                        def _(s, base=base, par=par, tc=tc, tp=tp, ldrp=ldrp,
                              prev_tr=prev_tr, wguard=wguard, gsub=gsub):
                            s.wait_ge(sct, sct_all)
                            if gsub < 2 and tc == 0:
                                s.wait_ge(st, st_shared_done)
                            if prev_tr > 0:
                                s.wait_ge(dve, prev_tr)
                            if wguard > 0 and tc == 0:
                                s.wait_ge(dve, wguard)
                            s.dma_start(out=xgs[:, tp * D: (tp + 1) * D],
                                        in_=xstage[base + tc * 128: base + (tc + 1) * 128, :]
                                        ).then_inc(ldrp, 16)
                            s.dma_start(out=tif[:, par * 4 + tc: par * 4 + tc + 1],
                                        in_=tokstage[base + tc * 128: base + (tc + 1) * 128, :]
                                        ).then_inc(ldrp, 16)
                            s.dma_start(out=gif[:, par * 4 + tc: par * 4 + tc + 1],
                                        in_=gstage[base + tc * 128: base + (tc + 1) * 128, :]
                                        ).then_inc(ldrp, 16)
                        ctr.inc(ldrp, 48)
                        ldpv = ctr.val(ldrp)
                        for kd in range(8):
                            pLx = pL if ti % 2 == 0 else pL2
                            ev_pre = tc_ev.get(ti - 2, 0)
                            @_ten(nc)
                            def _(t, tc=tc, kd=kd, tp=tp, pLx=pLx, ti=ti,
                                  ldpv=ldpv, ev_pre=ev_pre, ldrp=ldrp):
                                if kd == 0:
                                    t.wait_ge(ldrp, ldpv)
                                if ev_pre > 0:
                                    t.wait_ge(dve, ev_pre)
                                t.transpose(pLx[:, 0:128],
                                            xgs[:, tp * D + kd * 128: tp * D + kd * 128 + 128],
                                            idn[:]).then_inc(pe, 1)
                            ctr.inc(pe, 1)
                            pev = ctr.val(pe)
                            @_vec(nc)
                            def _(v, tc=tc, kd=kd, pLx=pLx, pev=pev):
                                v.wait_ge(pe, pev)
                                v.tensor_copy(xgT5[:, kd * 512 + tc * 128: kd * 512 + tc * 128 + 128],
                                              pLx[:, 0:128]).then_inc(dve, 1)
                            ctr.inc(dve, 1)
                            tc_ev[ti] = ctr.val(dve)
                            ti += 1
                        gtc += 1
                    sub_tev[gsub] = ctr.val(dve)
                    tev = ctr.val(dve)
                    mul_dve = {}
                    for ic in range(8):
                        pH1c = pH1 if ic % 2 == 0 else pH1b
                        pH3c = pH3 if ic % 2 == 0 else pH3b
                        hsl = hs5[:, (ic % 2) * 512: (ic % 2) * 512 + 512]
                        dve_pre = ctr.val(dve)
                        @_ten(nc)
                        def _(t, ic=ic, tev=tev, dve_pre=dve_pre, ldgv=ldgv,
                              pH1c=pH1c, pH3c=pH3c, wof1=wof1, ep=ep):
                            if ic == 0:
                                t.wait_ge(dve, tev)
                                t.wait_ge(ldg if ep == 0 else ldsh, ldgv)
                            elif ic >= 2:
                                t.wait_ge(dve, dve_pre)
                            last = None
                            for kd in range(8):
                                t.matmul(pH1c[:, 0:512],
                                         w1sb[:, wof1 + kd * 1024 + ic * 128: wof1 + kd * 1024 + ic * 128 + 128],
                                         xgT5[:, kd * 512:(kd + 1) * 512],
                                         start=(kd == 0), stop=(kd == 7))
                                last = t.matmul(pH3c[:, 0:512],
                                                w3sb[:, wof1 + kd * 1024 + ic * 128: wof1 + kd * 1024 + ic * 128 + 128],
                                                xgT5[:, kd * 512:(kd + 1) * 512],
                                                start=(kd == 0), stop=(kd == 7))
                            last.then_inc(pe, 1)
                        ctr.inc(pe, 1)
                        pev = ctr.val(pe)
                        sguard = mul_dve.get(ic - 2, 0)
                        @_scl(nc)
                        def _(s, pev=pev, pH1c=pH1c, hsl=hsl, sguard=sguard):
                            s.wait_ge(pe, pev)
                            if sguard > 0:
                                s.wait_ge(dve, sguard)
                            s.activation(hsl, pH1c[:, 0:512], AF.Silu).then_inc(act, 1)
                        ctr.inc(act, 1)
                        actv = ctr.val(act)
                        @_vec(nc)
                        def _(v, ic=ic, actv=actv, pH3c=pH3c, hsl=hsl):
                            v.wait_ge(act, actv)
                            v.tensor_mul(hg5[:, ic * 512:(ic + 1) * 512], hsl,
                                         pH3c[:, 0:512]).then_inc(dve, 1)
                        ctr.inc(dve, 1)
                        mul_dve[ic] = ctr.val(dve)
                    hg_done = ctr.val(dve)
                    w2ev = []
                    for tc in range(4):
                        wpar = tc % 2
                        bank_guard = w2ev[tc - 2] if tc >= 2 else 0
                        @_ten(nc)
                        def _(t, tc=tc, wpar=wpar, hg_done=hg_done, bank_guard=bank_guard):
                            if tc == 0:
                                t.wait_ge(dve, hg_done)
                            elif bank_guard > 0:
                                t.wait_ge(dve, bank_guard)
                            last = None
                            for dh in range(2):
                                if wpar == 0:
                                    out = pQ[:, dh * 512: dh * 512 + 512]
                                else:
                                    out = (pH1 if dh == 0 else pH3)[:, 0:512]
                                for ki in range(8):
                                    last = t.matmul(out,
                                                    hg5[:, ki * 512 + tc * 128: ki * 512 + tc * 128 + 128],
                                                    w2sb[:, ki * D + dh * 512: ki * D + dh * 512 + 512],
                                                    start=(ki == 0), stop=(ki == 7))
                            last.then_inc(pe, 1)
                        ctr.inc(pe, 1)
                        pev = ctr.val(pe)
                        yst_prev = ctr.val(yst)
                        @_vec(nc)
                        def _(v, tc=tc, par=par, wpar=wpar, pev=pev, yst_prev=yst_prev):
                            v.wait_ge(pe, pev)
                            if yst_prev > 0:
                                v.wait_ge(yst, yst_prev)
                            gcolx = gif[:, par * 4 + tc: par * 4 + tc + 1]
                            if wpar == 0:
                                v.tensor_scalar(ysb[:], pQ[:, 0:1024], gcolx, None, ALU.mult)
                            else:
                                v.tensor_scalar(ysb[:, 0:512], pH1[:, 0:512], gcolx, None, ALU.mult)
                                v.drain()
                                v.tensor_scalar(ysb[:, 512:1024], pH3[:, 0:512], gcolx, None, ALU.mult)
                            v.drain()
                            v.tensor_copy(ti32[:], tif[:, par * 4 + tc: par * 4 + tc + 1])
                            v.drain()
                            v.engine_nop().then_inc(dve, 1)
                        ctr.inc(dve, 1)
                        w2ev.append(ctr.val(dve))
                        dvev = ctr.val(dve)
                        yst_p2 = ctr.val(yst)
                        @_gps(nc)
                        def _(g, k=k, dvev=dvev, yst_p2=yst_p2):
                            g.wait_ge(dve, dvev)
                            if yst_p2 > 0:
                                g.wait_ge(yst, yst_p2)
                            g.indirect_dma_start(
                                y1dram if k == 0 else y2dram,
                                IndirectOffsetOnAxis(ap=ti32[:], axis=0),
                                ysb[:], None,
                                bounds_check=rb_tok, oob_is_err=False).then_inc(yst, 16)
                        ctr.inc(yst, 16)
                    sub_w2e[gsub] = ctr.val(dve)
                    gsub += 1
            exp_pe[e] = ctr.val(pe)
        yst_all = ctr.val(yst)

        st_base_f = ctr.val(st)
        fin_dve = {}
        fin_act = {}
        for it in range(NT):
            par = it % 2
            ldq_prev = ctr.val(ldq)
            dguard = fin_dve.get(it - 2, 0)
            aguard = fin_act.get(it - 2, 0)
            @_syn(nc)
            def _(s, it=it, par=par, ldq_prev=ldq_prev, dguard=dguard, aguard=aguard):
                if it == 0:
                    s.wait_ge(yst, yst_all)
                else:
                    s.wait_ge(ldq, ldq_prev)
                if dguard > 0:
                    s.wait_ge(dve, dguard)
                if aguard > 0:
                    s.wait_ge(act, aguard)
                s.dma_start(out=yr1[:, par * D: par * D + D],
                            in_=y1dram[it * 128:(it + 1) * 128, :]).then_inc(ldq, 16)
                s.dma_start(out=yr2[:, par * D: par * D + D],
                            in_=y2dram[it * 128:(it + 1) * 128, :]).then_inc(ldq, 16)
                s.dma_start(out=x3r[:, par * D: par * D + D],
                            in_=x3s[it * 128:(it + 1) * 128, :]).then_inc(ldq, 16)
                s.dma_start(out=att[:, par * D: par * D + D],
                            in_=zdram[it * 128:(it + 1) * 128, :]).then_inc(ldq, 16)
            ctr.inc(ldq, 64)
            ldqv = ctr.val(ldq)
            st_w = st_base_f + (it - 1) * 16 if it >= 2 else 0
            @_vec(nc)
            def _(v, par=par, ldqv=ldqv, st_w=st_w):
                v.wait_ge(ldq, ldqv)
                if st_w > 0:
                    v.wait_ge(st, st_w)
                ap = att[:, par * D: par * D + D]
                v.tensor_add(ap, ap, yr1[:, par * D: par * D + D])
                v.drain()
                v.tensor_add(ap, ap, yr2[:, par * D: par * D + D])
                v.drain()
                v.tensor_add(ap, ap, x3r[:, par * D: par * D + D])
                v.drain()
                v.engine_nop().then_inc(dve, 1)
            ctr.inc(dve, 1)
            emit_ln_fast(nc, ctr, dve, act,
                         att[:, par * D: par * D + D],
                         x4t[:, par * D: par * D + D],
                         yr1[:, par * D: par * D + D],
                         mu[:, par: par + 1], ssq[:, par: par + 1],
                         rstd[:, par: par + 1], sqs, epsb,
                         mu[:, 2 + par: 3 + par])
            fin_dve[it] = ctr.val(dve)
            fin_act[it] = ctr.val(act)
            dvev = ctr.val(dve)
            st_prev = ctr.val(st)
            @_gps(nc)
            def _(g, it=it, par=par, dvev=dvev, st_prev=st_prev):
                g.wait_ge(dve, dvev)
                if it > 0:
                    g.wait_ge(st, st_prev)
                g.dma_start(out=x4s[it * 128:(it + 1) * 128, :],
                            in_=x4t[:, par * D: par * D + D]).then_inc(st, 16)
            ctr.inc(st, 16)
        @_gps(nc)
        def _(g):
            g.wait_ge(st, ctr.val(st))
    return nc


def build_stage_d():
    nc = bass.Bass("TRN2", target_bir_lowering=False, debug=False)
    NSLOT = 16 * CAP_D
    io = {}
    io["x3s"] = nc.dram_tensor("x3s", [TPC, D], F32, kind="ExternalInput").ap()
    io["gw_T"] = nc.dram_tensor("gw_T", [D, 8], F32, kind="ExternalInput").ap()
    io["gb_bc"] = nc.dram_tensor("gb_bc", [128, 8], F32, kind="ExternalInput").ap()
    io["iota8"] = nc.dram_tensor("iota8", [128, 8], F32, kind="ExternalInput").ap()
    io["iotac"] = nc.dram_tensor("iotac", [128, 1], F32, kind="ExternalInput").ap()
    io["lt128"] = nc.dram_tensor("lt128", [128, 128], F32, kind="ExternalInput").ap()
    io["ones_col"] = nc.dram_tensor("ones_col", [128, 1], F32, kind="ExternalInput").ap()
    io["ones_row"] = nc.dram_tensor("ones_row", [1, 128], F32, kind="ExternalInput").ap()
    io["ident"] = nc.dram_tensor("ident", [128, 128], F32, kind="ExternalInput").ap()
    io["w1_t"] = nc.dram_tensor("w1_t", [8, D, 1024], F32, kind="ExternalInput").ap()
    io["w3_t"] = nc.dram_tensor("w3_t", [8, D, 1024], F32, kind="ExternalInput").ap()
    io["w2_t"] = nc.dram_tensor("w2_t", [8, 1024, D], F32, kind="ExternalInput").ap()
    io["sw1_t"] = nc.dram_tensor("sw1_t", [D, 2048], F32, kind="ExternalInput").ap()
    io["sw3_t"] = nc.dram_tensor("sw3_t", [D, 2048], F32, kind="ExternalInput").ap()
    io["sw2_t"] = nc.dram_tensor("sw2_t", [2048, D], F32, kind="ExternalInput").ap()
    io["x4s"] = nc.dram_tensor("x4s", [TPC, D], F32, kind="ExternalOutput").ap()
    io["xstage"] = nc.dram_tensor("xstage", [NSLOT, D], F32, kind="Internal").ap()
    io["tokstage"] = nc.dram_tensor("tokstage", [NSLOT, 1], F32, kind="Internal").ap()
    io["gstage"] = nc.dram_tensor("gstage", [NSLOT, 1], F32, kind="Internal").ap()
    io["y1dram"] = nc.dram_tensor("y1dram", [TPC, D], F32, kind="Internal").ap()
    io["y2dram"] = nc.dram_tensor("y2dram", [TPC, D], F32, kind="Internal").ap()
    io["zdram"] = nc.dram_tensor("zdram", [TPC, D], F32, kind="Internal").ap()
    ctr = Ctr()
    sems = {}
    from contextlib import ExitStack
    with ExitStack() as ses:
        for sname in ["lds", "ldq", "ldg", "sct", "yst", "zst", "pe", "dve",
                      "act", "st", "ldsh", "ldr0", "ldr1"]:
            sems[sname] = ses.enter_context(nc.semaphore(sname))
        emit_stage_d(nc, ctr, sems, io)
    return nc


def run_stage_d(inp, x3_full):
    nc = build_stage_d()
    lt = np.tril(np.ones((128, 128), np.float32), -1)
    shared = {
        "gw_T": np.ascontiguousarray(inp["L1_ffn_gate_w"].T),
        "gb_bc": np.ascontiguousarray(np.broadcast_to(inp["L1_ffn_gate_bias"], (128, 8))),
        "iota8": np.ascontiguousarray(np.broadcast_to(np.arange(8, dtype=np.float32), (128, 8))),
        "iotac": np.arange(128, dtype=np.float32).reshape(128, 1),
        "lt128": lt,
        "ones_col": np.ones((128, 1), np.float32),
        "ones_row": np.ones((1, 128), np.float32),
        "ident": np.eye(128, dtype=np.float32),
        "w1_t": np.ascontiguousarray(inp["L1_ffn_w1"].transpose(0, 2, 1)),
        "w3_t": np.ascontiguousarray(inp["L1_ffn_w3"].transpose(0, 2, 1)),
        "w2_t": np.ascontiguousarray(inp["L1_ffn_w2"].transpose(0, 2, 1)),
        "sw1_t": np.ascontiguousarray(inp["L1_ffn_sw1"].T),
        "sw3_t": np.ascontiguousarray(inp["L1_ffn_sw3"].T),
        "sw2_t": np.ascontiguousarray(inp["L1_ffn_sw2"].T),
    }
    in_maps = []
    for c in range(NCORE):
        sl = x3_full[c * TPC:(c + 1) * TPC]
        in_maps.append(dict(shared, x3s=np.ascontiguousarray(sl)))
    res = run_bass_kernel_spmd(nc, in_maps, core_ids=list(range(NCORE)))
    return np.concatenate([np.asarray(r["x4s"]) for r in res.results], 0)




def emit_phase_barrier(nc, ctr, sems, phase_no):
    ph = sems["ph"]
    @_gps(nc)
    def _(g):
        g.wait_ge(sems["st"], ctr.val(sems["st"]))
        g.nop().then_inc(ph, 1)
    ctr.inc(ph, 1)
    tgt = ctr.val(ph)
    for eng in (nc.tensor, nc.vector, nc.scalar, nc.sync, nc.gpsimd):
        eng.wait_ge(ph, tgt)


def build_mega():
    nc = bass.Bass("TRN2", target_bir_lowering=False, debug=False)
    NTOK_B = TPC + 128
    NH = TPC + HALO
    NSLOT_B = 32 * CAP_B
    NSLOT_D = 16 * CAP_D

    def inp2(name, shape, dt=F32):
        return nc.dram_tensor(name, shape, dt, kind="ExternalInput").ap()
    def internal(name, shape, dt=F32):
        return nc.dram_tensor(name, shape, dt, kind="Internal").ap()

    ident = inp2("ident", [128, 128])
    iotac = inp2("iotac", [128, 1])
    lt128 = inp2("lt128", [128, 128])
    ones_col = inp2("ones_col", [128, 1])
    ones_row = inp2("ones_row", [1, 128])

    io_a = dict(
        fT=inp2("fT", [D, T]), f_tm=inp2("f_tm", [T, D]),
        wq_T=inp2("wq_T", [D, D]), w2col=inp2("w2col", [128, 8]),
        wqbcol=inp2("wqbcol", [128, 8]), maskM=inp2("maskM", [128, 3 * 256]),
        ones164=inp2("ones164", [1, 64]), onesr128=ones_row,
        w2b_in=inp2("w2b_in", [1, 1]), ident=ident,
        cc_init=inp2("cc_init", [64, UW]), cr_init=inp2("cr_init", [128, 65 * 65]),
        gbcA=inp2("gbcA", [128, D]), bbcA=inp2("bbcA", [128, D]),
        x1s=internal("x1full", [T, D]),
    )
    io_b = dict(
        ntok=NTOK_B, nx1=T,
        x1full=io_a["x1s"],
        bgidx=inp2("bgidx", [NTOK_B, 1], I32), bjunk=inp2("bjunk", [NTOK_B, 1]),
        rw_T=inp2("rw_T", [D, 32]), rb_bc=inp2("rb_bc", [128, 32]),
        iota32=inp2("iota32", [128, 32]), iotac=iotac, lt128=lt128,
        ones_col=ones_col, ones_row=ones_row, ident=ident,
        w1_t=inp2("b_w1t", [32, D, 256]), b1c=inp2("b_b1c", [32, 128, 2]),
        w2_t=inp2("b_w2t", [32, 256, D]), b2bc=inp2("b_b2bc", [32, 128, D]),
        x2s=internal("x2loc", [NTOK_B, D]),
        xstage=internal("b_xstage", [NSLOT_B, D]),
        tokstage=internal("b_tokstage", [NSLOT_B, 1]),
        ydram=internal("b_ydram", [NTOK_B, D]),
    )
    io_c = dict(
        x2full=io_b["x2s"], halo_off=96, ident=ident,
        A_T=inp2("A_T", [D, D]), B_T=inp2("B_T", [D, D]), C_T=inp2("C_T", [D, D]),
        gbc=inp2("gbc", [128, D]), bbc=inp2("bbc", [128, D]),
        x3s=internal("x3loc", [TPC, D]),
    )
    io_d = dict(
        x3s=io_c["x3s"],
        gw_T=inp2("gw_T", [D, 8]), gb_bc=inp2("gb_bc", [128, 8]),
        iota8=inp2("iota8", [128, 8]), iotac=iotac, lt128=lt128,
        ones_col=ones_col, ones_row=ones_row, ident=ident,
        w1_t=inp2("d_w1t", [8, D, 1024]), w3_t=inp2("d_w3t", [8, D, 1024]),
        w2_t=inp2("d_w2t", [8, 1024, D]),
        sw1_t=inp2("sw1_t", [D, 2048]), sw3_t=inp2("sw3_t", [D, 2048]),
        sw2_t=inp2("sw2_t", [2048, D]),
        x4s=nc.dram_tensor("x4s", [TPC, D], F32, kind="ExternalOutput").ap(),
        xstage=internal("d_xstage", [NSLOT_D, D]),
        tokstage=internal("d_tokstage", [NSLOT_D, 1]),
        gstage=internal("d_gstage", [NSLOT_D, 1]),
        y1dram=internal("d_y1dram", [TPC, D]),
        y2dram=internal("d_y2dram", [TPC, D]),
        zdram=internal("d_zdram", [TPC, D]),
    )

    ctr = Ctr()
    sems = {}
    from contextlib import ExitStack
    with ExitStack() as ses:
        for sname in ["ld", "lds", "ldx", "ldf0", "ldf1", "ldr0", "ldr1", "ldq",
                      "ldg", "ldsh", "sct", "yst", "zst", "gth", "pe", "dve",
                      "act", "st", "ph"]:
            sems[sname] = ses.enter_context(nc.semaphore(sname))
        emit_stage_a(nc, ctr, sems, io_a)
        emit_phase_barrier(nc, ctr, sems, 1)
        emit_stage_b(nc, ctr, sems, io_b)
        emit_phase_barrier(nc, ctr, sems, 2)
        emit_stage_c(nc, ctr, sems, io_c)
        emit_phase_barrier(nc, ctr, sems, 3)
        emit_stage_d(nc, ctr, sems, io_d)
    return nc


def mega_in_maps(inp):
    cc, cr, msk = make_stage_a_consts()
    lt = np.tril(np.ones((128, 128), np.float32), -1)
    f = inp["L0_attn_feature"]
    NTOK_B = TPC + 128
    shared = {
        "ident": np.eye(128, dtype=np.float32),
        "iotac": np.arange(128, dtype=np.float32).reshape(128, 1),
        "lt128": lt,
        "ones_col": np.ones((128, 1), np.float32),
        "ones_row": np.ones((1, 128), np.float32),
        "wq_T": np.ascontiguousarray(inp["L0_attn_wq_w"].T),
        "w2col": np.ascontiguousarray(inp["L0_attn_w2_w"].reshape(8, 128).T),
        "wqbcol": np.ascontiguousarray(inp["L0_attn_wq_b"].reshape(8, 128).T),
        "maskM": msk,
        "ones164": np.full((1, 64), np.float32(OMEGA), np.float32),
        "w2b_in": np.asarray(inp["L0_attn_w2_b"], dtype=np.float32).reshape(1, 1),
        "cc_init": cc, "cr_init": cr,
        "gbcA": np.ascontiguousarray(np.broadcast_to(inp["L0_attn_ln_g"], (128, D))),
        "bbcA": np.ascontiguousarray(np.broadcast_to(inp["L0_attn_ln_b"], (128, D))),
        "rw_T": np.ascontiguousarray(inp["L0_ffn_router_w"].T),
        "rb_bc": np.ascontiguousarray(np.broadcast_to(inp["L0_ffn_router_b"], (128, 32))),
        "iota32": np.ascontiguousarray(np.broadcast_to(np.arange(32, dtype=np.float32), (128, 32))),
        "b_w1t": np.ascontiguousarray(inp["L0_ffn_weights1"]),
        "b_b1c": np.ascontiguousarray(inp["L0_ffn_bias1"].reshape(32, 2, 128).transpose(0, 2, 1)),
        "b_w2t": np.ascontiguousarray(inp["L0_ffn_weights2"]),
        "b_b2bc": np.ascontiguousarray(np.broadcast_to(inp["L0_ffn_bias2"][:, None, :], (32, 128, D))),
        "A_T": np.ascontiguousarray(inp["L1_attn_A"].T),
        "B_T": np.ascontiguousarray(inp["L1_attn_B"].T),
        "C_T": np.ascontiguousarray(inp["L1_attn_C"].T),
        "gbc": np.ascontiguousarray(np.broadcast_to(inp["L1_attn_gamma"], (128, D))),
        "bbc": np.ascontiguousarray(np.broadcast_to(inp["L1_attn_beta"], (128, D))),
        "gw_T": np.ascontiguousarray(inp["L1_ffn_gate_w"].T),
        "gb_bc": np.ascontiguousarray(np.broadcast_to(inp["L1_ffn_gate_bias"], (128, 8))),
        "iota8": np.ascontiguousarray(np.broadcast_to(np.arange(8, dtype=np.float32), (128, 8))),
        "d_w1t": np.ascontiguousarray(inp["L1_ffn_w1"].transpose(0, 2, 1)),
        "d_w3t": np.ascontiguousarray(inp["L1_ffn_w3"].transpose(0, 2, 1)),
        "d_w2t": np.ascontiguousarray(inp["L1_ffn_w2"].transpose(0, 2, 1)),
        "sw1_t": np.ascontiguousarray(inp["L1_ffn_sw1"].T),
        "sw3_t": np.ascontiguousarray(inp["L1_ffn_sw3"].T),
        "sw2_t": np.ascontiguousarray(inp["L1_ffn_sw2"].T),
    }
    in_maps = []
    for c in range(NCORE):
        bb = c // 4
        start = (c % 4) * TPC - 128
        gi = start + np.arange(NTOK_B, dtype=np.int64)
        junk = (gi < 0)
        gidx = np.where(junk, 10 ** 9, gi).astype(np.int32).reshape(NTOK_B, 1)
        jk = np.where(junk, 1.0e9, 0.0).astype(np.float32).reshape(NTOK_B, 1)
        in_maps.append(dict(shared,
                            fT=np.ascontiguousarray(f[bb].T),
                            f_tm=np.ascontiguousarray(f[bb]),
                            bgidx=gidx, bjunk=jk))
    return in_maps


def kernel(**inputs):
    inp = {k: np.asarray(v) for k, v in inputs.items()}
    nc = build_mega()
    in_maps = mega_in_maps(inp)
    res = run_bass_kernel_spmd(nc, in_maps, core_ids=list(range(NCORE)))
    out = np.empty((B, T, D), np.float32)
    for c in range(NCORE):
        bb, s = c // 4, (c % 4) * TPC
        out[bb, s:s + TPC] = np.asarray(res.results[c]["x4s"])
    return out


# version 22
# speedup vs baseline: 1.0710x; 1.0462x over previous
import numpy as np
import concourse.bass as bass
import concourse.mybir as mybir
from concourse.bass_utils import run_bass_kernel_spmd
from concourse.bass import IndirectOffsetOnAxis

F32 = mybir.dt.float32
BF16 = mybir.dt.bfloat16
I32 = mybir.dt.int32
U32 = mybir.dt.uint32
AF = mybir.ActivationFunctionType
ALU = mybir.AluOpType
AX = mybir.AxisListType

B, T, D = 2, 8192, 1024
UW = 8320
W = 64
NSWEEP = 25
OMEGA = 1.2
KPIC = 20
HALO = 32
CAP_B = 192
CAP_D = 1024
LN_EPS = 1e-5
NCORE = 8
TPC = 2048


class Ctr:
    def __init__(self):
        self.v = {}
    def inc(self, sem, n=1):
        k = sem.name
        self.v[k] = self.v.get(k, 0) + n
        return self.v[k]
    def val(self, sem):
        return self.v.get(sem.name, 0)


def emit_ln_tokmajor(nc, ctr, dve, act, src, dst, mu, ssq, rstd, sqs, epsb, gt=None, bt=None):
    @_vec(nc)
    def _(v):
        v.reduce_sum(mu[:], src[:], axis=AX.X)
        v.drain()
        v.tensor_scalar_mul(mu[:], mu[:], 1.0 / D)
        v.drain()
        v.tensor_scalar(dst[:], src[:], mu[:], None, ALU.subtract).then_inc(dve, 1)
    ctr.inc(dve, 1)
    dvev = ctr.val(dve)
    @_scl(nc)
    def _(s):
        s.wait_ge(dve, dvev)
        s.activation(sqs[:], dst[:], AF.Square, accum_out=ssq[:])
        s.drain()
        s.activation(rstd[:], ssq[:], AF.Ln, bias=epsb[:], scale=1.0 / D)
        s.drain()
        s.activation(rstd[:], rstd[:], AF.Exp, scale=-0.5).then_inc(act, 1)
    ctr.inc(act, 1)
    actv = ctr.val(act)
    @_vec(nc)
    def _(v):
        v.wait_ge(act, actv)
        if gt is not None:
            v.scalar_tensor_tensor(dst[:], dst[:], rstd[:], gt[:], ALU.mult, ALU.mult)
            v.drain()
            v.tensor_add(dst[:], dst[:], bt[:])
        else:
            v.tensor_scalar(dst[:], dst[:], rstd[:], None, ALU.mult)
        v.drain()
        v.engine_nop().then_inc(dve, 1)
    ctr.inc(dve, 1)


def emit_ln_fast(nc, ctr, dve, act, src, dst, cen, mu, ssq, rstd, sqs, epsb, tmp1,
                 gt=None, bt=None):
    dvev0 = ctr.val(dve)
    @_scl(nc)
    def _(s):
        s.wait_ge(dve, dvev0)
        s.activation(sqs[:], src[:], AF.Square, accum_out=ssq[:]).then_inc(act, 1)
    ctr.inc(act, 1)
    @_vec(nc)
    def _(v):
        v.reduce_sum(mu[:], src[:], axis=AX.X)
        v.drain()
        v.tensor_scalar_mul(mu[:], mu[:], 1.0 / D)
        v.drain()
        v.tensor_mul(tmp1[:], mu[:], mu[:])
        v.drain()
        v.tensor_sub(tmp1[:], epsb[:], tmp1[:]).then_inc(dve, 1)
        v.drain()
        v.tensor_scalar(cen[:], src[:], mu[:], None, ALU.subtract).then_inc(dve, 1)
    ctr.inc(dve, 2)
    dvev1 = ctr.val(dve) - 1
    @_scl(nc)
    def _(s):
        s.drain()
        s.wait_ge(dve, dvev1)
        s.activation(rstd[:], ssq[:], AF.Ln, bias=tmp1[:], scale=1.0 / D)
        s.drain()
        s.activation(rstd[:], rstd[:], AF.Exp, scale=-0.5).then_inc(act, 1)
    ctr.inc(act, 1)
    actv = ctr.val(act)
    @_vec(nc)
    def _(v):
        v.wait_ge(act, actv)
        v.drain()
        if gt is not None:
            v.scalar_tensor_tensor(dst[:], cen[:], rstd[:], gt[:], ALU.mult, ALU.mult)
            v.drain()
            v.tensor_add(dst[:], dst[:], bt[:])
        else:
            v.tensor_scalar(dst[:], cen[:], rstd[:], None, ALU.mult)
        v.drain()
        v.engine_nop().then_inc(dve, 1)
    ctr.inc(dve, 1)


class _eng_ctx:
    def __init__(self, eng):
        self.eng = eng
    def __call__(self, f):
        f(self.eng)

def _vec(nc):
    return _eng_ctx(nc.vector)
def _scl(nc):
    return _eng_ctx(nc.scalar)
def _ten(nc):
    return _eng_ctx(nc.tensor)
def _gps(nc):
    return _eng_ctx(nc.gpsimd)
def _syn(nc):
    return _eng_ctx(nc.sync)


def emit_stage_c(nc, ctr, sems, io):
    _pref = io.get("prefix", "c_")
    (x2full, A_T, B_T, C_T, gbc, bbc, x3s, identC) = (
        io["x2full"], io["A_T"], io["B_T"], io["C_T"],
        io["gbc"], io["bbc"], io["x3s"], io["ident"])
    NH = TPC + HALO
    hoff = io["halo_off"]
    roff = hoff + HALO
    ZW = NH + 1
    TT = []
    t0 = 0
    while t0 < NH:
        TT.append((t0, min(512, NH - t0)))
        t0 += 512

    from contextlib import ExitStack
    with ExitStack() as es:
        E = es.enter_context
        ATt = E(nc.sbuf_tensor(_pref + "ATt", [128, 8 * D], BF16))
        CTt = E(nc.sbuf_tensor(_pref + "CTt", [128, 8 * D], BF16))
        xb = E(nc.sbuf_tensor(_pref + "xb", [128, 8 * NH], BF16))
        z0 = E(nc.sbuf_tensor(_pref + "z0", [128, 8 * ZW], BF16))
        z1 = E(nc.sbuf_tensor(_pref + "z1", [128, 8 * ZW], BF16))
        scr = E(nc.sbuf_tensor(_pref + "scr", [128, 4 * 512], F32))
        idn_c = E(nc.sbuf_tensor(_pref + "idn_c", [128, 128], F32))
        mu = E(nc.sbuf_tensor(_pref + "mu", [128, 1], F32))
        ssq = E(nc.sbuf_tensor(_pref + "ssq", [128, 1], F32))
        rstd = E(nc.sbuf_tensor(_pref + "rstd", [128, 1], F32))
        epsb = E(nc.sbuf_tensor(_pref + "epsb", [128, 1], F32))
        pa = E(nc.psum_tensor(_pref + "pa", [128, 512], F32))
        pb = E(nc.psum_tensor(_pref + "pb", [128, 512], F32))
        pc = E(nc.psum_tensor(_pref + "pc", [128, 512], F32))
        pd = E(nc.psum_tensor(_pref + "pd", [128, 512], F32))
        ld = sems["ld"]
        ldx = sems["ldx"]
        ldr0 = sems["ldr0"]
        ldr1 = sems["ldr1"]
        pe = sems["pe"]
        dve = sems["dve"]
        act = sems["act"]
        st = sems["st"]
        zbuf = [z0, z1]
        psum = [pa, pb]

        es_p1 = ExitStack()
        x2T = es_p1.enter_context(nc.sbuf_tensor(_pref + "x2T", [128, 8 * NH], BF16))
        BTt = es_p1.enter_context(nc.sbuf_tensor(_pref + "BTt", [128, 8 * D], BF16))

        @_gps(nc)
        def _(g):
            g.dma_start(out=ATt[:].rearrange("p (k e) -> p k e", e=D),
                        in_=A_T.rearrange("(k p) e -> p k e", p=128)).then_inc(ld, 16)
            g.dma_start(out=CTt[:].rearrange("p (k e) -> p k e", e=D),
                        in_=C_T.rearrange("(k p) e -> p k e", p=128)).then_inc(ld, 16)
            g.dma_start(out=BTt[:].rearrange("p (k e) -> p k e", e=D),
                        in_=B_T.rearrange("(k p) e -> p k e", p=128)).then_inc(ld, 16)
            g.dma_start(out=idn_c[:], in_=identC).then_inc(ld, 16)
        ctr.inc(ld, 64)
        ld_all = ctr.val(ld)

        x2q = es_p1.enter_context(nc.sbuf_tensor(_pref + "x2q", [128, D], F32))
        nchk = (NH + 127) // 128
        for kchk in range(nchk):
            rows = min(128, NH - kchk * 128)
            dve_now = ctr.val(dve)
            @_syn(nc)
            def _(s, kchk=kchk, rows=rows, dve_now=dve_now):
                if kchk > 0:
                    s.wait_ge(dve, dve_now)
                s.dma_start(out=x2q[0:rows, :],
                            in_=x2full[hoff + kchk * 128: hoff + kchk * 128 + rows, :]
                            ).then_inc(ldx, 16)
            ctr.inc(ldx, 16)
            ldxv = ctr.val(ldx)
            for kd in range(8):
                dve_t = ctr.val(dve)
                @_ten(nc)
                def _(t, kchk=kchk, rows=rows, kd=kd, ldxv=ldxv, dve_t=dve_t):
                    if kd == 0:
                        t.wait_ge(ldx, ldxv)
                        if kchk == 0:
                            t.wait_ge(ld, ld_all)
                    else:
                        t.wait_ge(dve, dve_t)
                    t.transpose(pa[:, 0:rows], x2q[0:rows, kd * 128:(kd + 1) * 128],
                                idn_c[0:rows, 0:rows]).then_inc(pe, 1)
                ctr.inc(pe, 1)
                pev = ctr.val(pe)
                @_vec(nc)
                def _(v, kchk=kchk, rows=rows, kd=kd, pev=pev):
                    v.wait_ge(pe, pev)
                    v.tensor_copy(x2T[:, kd * NH + kchk * 128: kd * NH + kchk * 128 + rows],
                                  pa[:, 0:rows]).then_inc(dve, 1)
                ctr.inc(dve, 1)

        @_vec(nc)
        def _(v):
            v.memset(z0[:], 0.0)
            v.drain()
            v.memset(z1[:], 0.0)
            v.drain()
            v.memset(epsb[:], LN_EPS)
            v.drain()
            v.engine_nop().then_inc(dve, 1)
        ctr.inc(dve, 1)
        dve_z = ctr.val(dve)

        dve_tp = ctr.val(dve)
        @_ten(nc)
        def _(t):
            t.wait_ge(ld, ld_all)
            t.wait_ge(dve, dve_tp)
        ti = 0
        for (tt0, tw) in TT:
            for ke in range(8):
                par = ti % 2
                @_ten(nc)
                def _(t, tt0=tt0, tw=tw, ke=ke, par=par, ti=ti):
                    if ti >= 2:
                        t.wait_ge(dve, dve_z + ti - 1)
                    last = None
                    for kd in range(8):
                        last = t.matmul(psum[par][:, 0:tw],
                                        BTt[:, kd * D + ke * 128: kd * D + ke * 128 + 128],
                                        x2T[:, kd * NH + tt0: kd * NH + tt0 + tw],
                                        start=(kd == 0), stop=(kd == 7))
                    last.then_inc(pe, 1)
                ctr.inc(pe, 1)
                pev = ctr.val(pe)
                @_vec(nc)
                def _(v, tt0=tt0, tw=tw, ke=ke, par=par, pev=pev):
                    v.wait_ge(pe, pev)
                    v.tensor_copy(xb[:, ke * NH + tt0: ke * NH + tt0 + tw],
                                  psum[par][:, 0:tw]).then_inc(dve, 1)
                ctr.inc(dve, 1)
                ti += 1

        pe_phase1 = ctr.val(pe)
        es_p1.close()

        psum4 = [pa, pb, pc, pd]
        TILES = [(m, tt0, tw) for (tt0, tw) in TT for m in range(8)]
        ntile = len(TILES)
        dve_pic0 = ctr.val(dve)
        act_pic0 = ctr.val(act)
        dve_base0 = ctr.val(dve)
        for sw in range(KPIC):
            rd, wr = sw % 2, (sw + 1) % 2
            for i, (m, tt0, tw) in enumerate(TILES):
                g = sw * ntile + i
                par = g % 4
                ct = i // 8
                wav = act_pic0 + (sw - 1) * ntile + (ct + 1) * 8 if sw > 0 else 0
                @_ten(nc)
                def _(t, m=m, tt0=tt0, tw=tw, par=par, g=g, sw=sw, i=i, rd=rd, wav=wav):
                    if sw == 0 and i == 0:
                        t.wait_ge(dve, dve_base0)
                    if wav > 0 and (i % 8 == 0 or i == 0):
                        t.wait_ge(act, wav)
                    if g >= 4:
                        t.wait_ge(dve, dve_pic0 + g - 3)
                    last = None
                    for kd in range(8):
                        last = t.matmul(psum4[par][:, 0:tw],
                                        ATt[:, kd * D + m * 128: kd * D + m * 128 + 128],
                                        zbuf[rd][:, kd * ZW + tt0: kd * ZW + tt0 + tw],
                                        start=(kd == 0), stop=(kd == 7))
                    last.then_inc(pe, 1)
                ctr.inc(pe, 1)
                pev = ctr.val(pe)
                act_guard = act_pic0 + g - 3 if g >= 4 else 0
                @_vec(nc)
                def _(v, m=m, tt0=tt0, tw=tw, par=par, pev=pev, act_guard=act_guard):
                    v.wait_ge(pe, pev)
                    if act_guard > 0:
                        v.wait_ge(act, act_guard)
                    v.tensor_add(scr[:, par * 512: par * 512 + tw],
                                 psum4[par][:, 0:tw],
                                 xb[:, m * NH + tt0: m * NH + tt0 + tw]).then_inc(dve, 1)
                ctr.inc(dve, 1)
                dvev = ctr.val(dve)
                @_scl(nc)
                def _(s, m=m, tt0=tt0, tw=tw, par=par, dvev=dvev, wr=wr):
                    s.wait_ge(dve, dvev)
                    s.activation(zbuf[wr][:, m * ZW + 1 + tt0: m * ZW + 1 + tt0 + tw],
                                 scr[:, par * 512: par * 512 + tw],
                                 AF.Tanh).then_inc(act, 1)
                ctr.inc(act, 1)
        pic_lastbase = act_pic0 + (KPIC - 1) * ntile
        pic_done = ctr.val(act)
        pic_dve_done = ctr.val(dve)
        zfin = zbuf[KPIC % 2]

        gt = E(nc.sbuf_tensor(_pref + "gt", [128, D], F32))
        bt = E(nc.sbuf_tensor(_pref + "bt", [128, D], F32))
        x2r = E(nc.sbuf_tensor(_pref + "x2r", [128, 2 * D], F32))
        att = E(nc.sbuf_tensor(_pref + "att", [128, 2 * D], F32))
        sqs = E(nc.sbuf_tensor(_pref + "sqs", [128, 2 * D], F32))
        x3t = E(nc.sbuf_tensor(_pref + "x3t", [128, 2 * D], F32))
        mu2 = E(nc.sbuf_tensor(_pref + "mu2", [128, 4], F32))
        ssq2 = E(nc.sbuf_tensor(_pref + "ssq2", [128, 2], F32))
        rstd2 = E(nc.sbuf_tensor(_pref + "rstd2", [128, 2], F32))
        pep = [E(nc.psum_tensor(_pref + "pe%d" % j, [128, 512], F32)) for j in range(4)]
        @_syn(nc)
        def _(sync):
            sync.wait_ge(pe, pe_phase1)
            sync.dma_start(out=gt[:], in_=gbc).then_inc(ldx, 16)
            sync.dma_start(out=bt[:], in_=bbc).then_inc(ldx, 16)
        ctr.inc(ldx, 32)
        ldx_gb = ctr.val(ldx)
        nt = TPC // 128
        tile_act_end = {}
        tile_add_dve = {}
        st_base_c = ctr.val(st)
        for it in range(nt):
            tb = HALO + it * 128
            par = it % 2
            ct_hi = (tb + 127) // 512
            wav = pic_lastbase + min((ct_hi + 1) * 8, ntile)
            st_guard = st_base_c + (it - 1) * 16 if it >= 2 else 0
            @_syn(nc)
            def _(sync, it=it, par=par, st_guard=st_guard, wav=wav):
                if st_guard > 0:
                    sync.wait_ge(st, st_guard)
                else:
                    sync.wait_ge(act, wav)
                ldr = ldr0 if par == 0 else ldr1
                sync.dma_start(out=x2r[:, par * D: par * D + D],
                               in_=x2full[roff + it * 128: roff + (it + 1) * 128, :]).then_inc(ldr, 16)
            ctr.inc(ldr0 if par == 0 else ldr1, 16)
            ldv = ctr.val(ldr0 if par == 0 else ldr1)
            pguard = tile_add_dve.get(it - 2, 0)
            for eh in range(2):
                @_ten(nc)
                def _(t, tb=tb, eh=eh, par=par, wav=wav, pguard=pguard, it=it):
                    if eh == 0:
                        t.wait_ge(act, wav)
                        if it == 0:
                            t.wait_ge(dve, pic_dve_done)
                        if pguard > 0:
                            t.wait_ge(dve, pguard)
                    last = None
                    for m in range(8):
                        last = t.matmul(pep[2 * par + eh][:, 0:512],
                                        zfin[:, m * ZW + 1 + tb: m * ZW + 1 + tb + 128],
                                        CTt[:, m * D + eh * 512: m * D + eh * 512 + 512],
                                        start=(m == 0), stop=(m == 7))
                    last.then_inc(pe, 1)
                ctr.inc(pe, 1)
            pev = ctr.val(pe)
            aguard = tile_act_end.get(it - 2, 0)
            @_vec(nc)
            def _(v, it=it, par=par, pev=pev, ldv=ldv, aguard=aguard):
                v.wait_ge(pe, pev)
                v.wait_ge(ldx, ldx_gb)
                v.wait_ge(ldr0 if par == 0 else ldr1, ldv)
                if aguard > 0:
                    v.wait_ge(act, aguard)
                if it < 2:
                    v.wait_ge(act, pic_done)
                x2t = x2r[:, par * D: par * D + D]
                v.tensor_add(att[:, par * D: par * D + 512], pep[2 * par][:, 0:512], x2t[:, 0:512])
                v.drain()
                v.tensor_add(att[:, par * D + 512: par * D + 1024], pep[2 * par + 1][:, 0:512],
                             x2t[:, 512:1024])
                v.drain()
                v.engine_nop().then_inc(dve, 1)
            ctr.inc(dve, 1)
            tile_add_dve[it] = ctr.val(dve)
            attp = att[:, par * D: par * D + D]
            x3tp = x3t[:, par * D: par * D + D]
            sqsp = sqs[:, par * D: par * D + D]
            cen1 = scr[:, par * D: par * D + D]
            cen2 = att[:, (1 - par) * D: (1 - par) * D + D]
            mup = mu2[:, par: par + 1]
            tmp1p = mu2[:, 2 + par: 3 + par]
            ssqp = ssq2[:, par: par + 1]
            rstdp = rstd2[:, par: par + 1]
            emit_ln_fast(nc, ctr, dve, act, attp, attp, cen1, mup, ssqp, rstdp, sqsp,
                         epsb, tmp1p, gt, bt)
            dvev = ctr.val(dve)
            @_vec(nc)
            def _(v, par=par, dvev=dvev, attp=attp):
                v.wait_ge(dve, dvev)
                x2t = x2r[:, par * D: par * D + D]
                v.tensor_add(attp, attp, x2t)
                v.drain()
                v.engine_nop().then_inc(dve, 1)
            ctr.inc(dve, 1)
            emit_ln_fast(nc, ctr, dve, act, attp, x3tp, cen2, mup, ssqp, rstdp, sqsp,
                         epsb, tmp1p)
            tile_act_end[it] = ctr.val(act)
            dvev = ctr.val(dve)
            st_prev = ctr.val(st)
            @_gps(nc)
            def _(g, it=it, dvev=dvev, x3tp=x3tp, st_prev=st_prev):
                g.wait_ge(dve, dvev)
                if it > 0:
                    g.wait_ge(st, st_prev)
                g.dma_start(out=x3s[it * 128:(it + 1) * 128, :], in_=x3tp).then_inc(st, 16)
            ctr.inc(st, 16)
        @_gps(nc)
        def _(g):
            g.wait_ge(st, ctr.val(st))
    return nc


def build_stage_c():
    nc = bass.Bass("TRN2", target_bir_lowering=False, debug=False)
    NH = TPC + HALO
    io = {}
    io["x2full"] = nc.dram_tensor("x2full", [NH, D], F32, kind="ExternalInput").ap()
    io["halo_off"] = 0
    io["ident"] = nc.dram_tensor("ident", [128, 128], F32, kind="ExternalInput").ap()
    io["A_T"] = nc.dram_tensor("A_T", [D, D], F32, kind="ExternalInput").ap()
    io["B_T"] = nc.dram_tensor("B_T", [D, D], F32, kind="ExternalInput").ap()
    io["C_T"] = nc.dram_tensor("C_T", [D, D], F32, kind="ExternalInput").ap()
    io["gbc"] = nc.dram_tensor("gbc", [128, D], F32, kind="ExternalInput").ap()
    io["bbc"] = nc.dram_tensor("bbc", [128, D], F32, kind="ExternalInput").ap()
    io["x3s"] = nc.dram_tensor("x3s", [TPC, D], F32, kind="ExternalOutput").ap()
    ctr = Ctr()
    sems = {}
    from contextlib import ExitStack
    with ExitStack() as ses:
        for sname in ["ld", "ldx", "ldr0", "ldr1", "pe", "dve", "act", "st"]:
            sems[sname] = ses.enter_context(nc.semaphore(sname))
        emit_stage_c(nc, ctr, sems, io)
    return nc


def run_stage_c(inp, x2_full):
    nc = build_stage_c()
    xf = x2_full
    gamma, beta = inp["L1_attn_gamma"], inp["L1_attn_beta"]
    shared = {
        "ident": np.eye(128, dtype=np.float32),
        "A_T": np.ascontiguousarray(inp["L1_attn_A"].T),
        "B_T": np.ascontiguousarray(inp["L1_attn_B"].T),
        "C_T": np.ascontiguousarray(inp["L1_attn_C"].T),
        "gbc": np.ascontiguousarray(np.broadcast_to(gamma, (128, D))),
        "bbc": np.ascontiguousarray(np.broadcast_to(beta, (128, D))),
    }
    in_maps = []
    for c in range(NCORE):
        lo = c * TPC
        sl = xf[lo:lo + TPC]
        if c % 4 == 0:
            halo = np.zeros((HALO, D), np.float32)
        else:
            halo = xf[lo - HALO:lo]
        x2h = np.concatenate([halo, sl], 0)
        in_maps.append(dict(shared, x2full=np.ascontiguousarray(x2h)))
    res = run_bass_kernel_spmd(nc, in_maps, core_ids=list(range(NCORE)))
    return np.concatenate([np.asarray(r["x3s"]) for r in res.results], 0)




def emit_stage_a(nc, ctr, sems, io):
    _pref = io.get("prefix", "a_")
    (fT, f_tm, wq_T, w2col, wqbcol, maskM, ones164, onesr128, w2b_in, ident,
     cc_init, cr_init, gbcA, bbcA, x1s) = (
        io["fT"], io["f_tm"], io["wq_T"], io["w2col"], io["wqbcol"], io["maskM"],
        io["ones164"], io["onesr128"], io["w2b_in"], io["ident"],
        io["cc_init"], io["cr_init"], io["gbcA"], io["bbcA"], io["x1s"])
    from contextlib import ExitStack
    with ExitStack() as es:
        E = es.enter_context
        cc0 = E(nc.sbuf_tensor(_pref + "cc0", [64, UW], F32))
        cc1 = E(nc.sbuf_tensor(_pref + "cc1", [64, UW], F32))
        cr0 = E(nc.sbuf_tensor(_pref + "cr0", [128, 65 * 65], F32))
        cr1 = E(nc.sbuf_tensor(_pref + "cr1", [128, 65 * 65], F32))
        Psb = E(nc.sbuf_tensor(_pref + "Psb", [64, T], F32))
        F0sb = E(nc.sbuf_tensor(_pref + "F0sb", [64, D], F32))
        F0T = E(nc.sbuf_tensor(_pref + "F0T", [128, 8 * 64], F32))
        Msb = E(nc.sbuf_tensor(_pref + "Msb", [128, 8 * 64], F32))
        w2sb = E(nc.sbuf_tensor(_pref + "w2sb", [128, 8], F32))
        wqbsb = E(nc.sbuf_tensor(_pref + "wqbsb", [128, 8], F32))
        rsb = E(nc.sbuf_tensor(_pref + "rsb", [64, 1], F32))
        msk = E(nc.sbuf_tensor(_pref + "msk", [128, 3 * 256], F32))
        o164 = E(nc.sbuf_tensor(_pref + "o164", [1, 64], F32))
        or128 = E(nc.sbuf_tensor(_pref + "or128", [1, 128], F32))
        w2bsb = E(nc.sbuf_tensor(_pref + "w2bsb", [1, 1], F32))
        w2bb = E(nc.sbuf_tensor(_pref + "w2bb", [128, 1], F32))
        ebv = E(nc.sbuf_tensor(_pref + "ebv", [1, 1], F32))
        idn = E(nc.sbuf_tensor(_pref + "idn", [128, 128], F32))
        esb = [E(nc.sbuf_tensor(_pref + "esb0", [128, 3 * 256], F32)),
               E(nc.sbuf_tensor(_pref + "esb1", [128, 3 * 256], F32))]
        zr = [E(nc.sbuf_tensor(_pref + "zr0", [1, 256], F32)),
              E(nc.sbuf_tensor(_pref + "zr1", [1, 256], F32))]
        rzb = [E(nc.sbuf_tensor(_pref + "rzb0", [64, 256], F32)),
               E(nc.sbuf_tensor(_pref + "rzb1", [64, 256], F32))]
        tmpc = [E(nc.sbuf_tensor(_pref + "tmpc0", [64, 256], F32)),
                E(nc.sbuf_tensor(_pref + "tmpc1", [64, 256], F32))]
        lnw = E(nc.sbuf_tensor(_pref + "lnw", [1, 1], F32))
        gtA = E(nc.sbuf_tensor(_pref + "gtA", [128, D], F32))
        btA = E(nc.sbuf_tensor(_pref + "btA", [128, D], F32))
        mu = E(nc.sbuf_tensor(_pref + "mu", [128, 8], F32))
        ssq = E(nc.sbuf_tensor(_pref + "ssq", [128, 4], F32))
        rstd = E(nc.sbuf_tensor(_pref + "rstd", [128, 4], F32))
        epsb = E(nc.sbuf_tensor(_pref + "epsb", [128, 1], F32))
        pS = [E(nc.psum_tensor(_pref + "pS0", [128, 3 * 256], F32)),
              E(nc.psum_tensor(_pref + "pS1", [128, 3 * 256], F32))]
        pM = [E(nc.psum_tensor(_pref + "pM0", [128, 512], F32)),
              E(nc.psum_tensor(_pref + "pM1", [128, 512], F32))]
        pV = E(nc.psum_tensor(_pref + "pV", [128, 1024], F32))
        ld = sems["ld"]
        lds = sems["lds"]
        ldf0 = sems["ldf0"]
        ldf1 = sems["ldf1"]
        ldr0 = sems["ldr0"]
        ldr1 = sems["ldr1"]
        pe = sems["pe"]
        dve = sems["dve"]
        act = sems["act"]
        st = sems["st"]
        ccb = [cc0, cc1]
        crb = [cr0, cr1]

        @_syn(nc)
        def _(s):
            s.dma_start(out=w2sb[:], in_=w2col).then_inc(lds, 16)
            s.dma_start(out=wqbsb[:], in_=wqbcol).then_inc(lds, 16)
            s.dma_start(out=msk[:], in_=maskM).then_inc(lds, 16)
            s.dma_start(out=o164[:], in_=ones164).then_inc(lds, 16)
            s.dma_start(out=or128[:], in_=onesr128).then_inc(lds, 16)
            s.dma_start(out=w2bsb[:], in_=w2b_in).then_inc(lds, 16)
            s.dma_start(out=idn[:], in_=ident).then_inc(lds, 16)
            s.dma_start(out=F0sb[:], in_=f_tm[0:W, :]).then_inc(lds, 16)
            s.dma_start(out=cc0[:], in_=cc_init).then_inc(lds, 16)
            s.dma_start(out=cc1[:], in_=cc_init).then_inc(lds, 16)
            s.dma_start(out=cr0[:], in_=cr_init).then_inc(lds, 16)
            s.dma_start(out=cr1[:], in_=cr_init).then_inc(lds, 16)
            s.dma_start(out=gtA[:], in_=gbcA).then_inc(lds, 16)
            s.dma_start(out=btA[:], in_=bbcA).then_inc(lds, 16)
        ctr.inc(lds, 14 * 16)
        lds_all = ctr.val(lds)
        @_gps(nc)
        def _(g):
            g.dma_start(out=F0T[:].rearrange("p (k j) -> p k j", j=64),
                        in_=fT.rearrange("(k p) t -> p k t", p=128)[:, :, 0:64]).then_inc(ld, 16)
        ctr.inc(ld, 16)
        ld_all = ctr.val(ld)

        @_vec(nc)
        def _(v):
            v.wait_ge(lds, lds_all)
            v.wait_ge(ld, ld_all)
            for k in range(8):
                v.tensor_scalar(F0T[:, k * 64:(k + 1) * 64], F0T[:, k * 64:(k + 1) * 64],
                                w2sb[:, k:k + 1], None, ALU.mult)
                v.drain()
            v.memset(epsb[:], LN_EPS)
            v.drain()
            v.memset(lnw[:], 0.0)
            v.drain()
            v.engine_nop().then_inc(dve, 1)
        ctr.inc(dve, 1)
        prep_dve = ctr.val(dve)

        @_ten(nc)
        def _(t):
            t.wait_ge(dve, prep_dve)
            last = None
            for k in range(8):
                last = t.matmul(pM[0][0:64, 0:1], F0T[:, k * 64:(k + 1) * 64],
                                wqbsb[:, k:k + 1], start=(k == 0), stop=(k == 7))
            last.then_inc(pe, 1)
        ctr.inc(pe, 1)
        pev = ctr.val(pe)
        @_vec(nc)
        def _(v, pev=pev):
            v.wait_ge(pe, pev)
            v.tensor_copy(rsb[:], pM[0][0:64, 0:1]).then_inc(dve, 1)
        ctr.inc(dve, 1)
        r_dve = ctr.val(dve)

        @_ten(nc)
        def _(t):
            t.matmul(pM[0][:, 4:5], or128[:], w2bsb[:], start=True, stop=True).then_inc(pe, 1)
        ctr.inc(pe, 1)
        pev = ctr.val(pe)
        @_vec(nc)
        def _(v, pev=pev):
            v.wait_ge(pe, pev)
            v.tensor_copy(w2bb[:], pM[0][:, 4:5]).then_inc(dve, 1)
        ctr.inc(dve, 1)
        @_scl(nc)
        def _(s):
            s.wait_ge(dve, ctr.val(dve))
            s.activation(ebv[:], w2bsb[:], AF.Exp).then_inc(act, 1)
        ctr.inc(act, 1)

        es_mp = ExitStack()
        fsl = es_mp.enter_context(nc.sbuf_tensor(_pref + "fsl", [128, 8 * 512], F32))
        for m in range(8):
            par = 0
            dve_now = ctr.val(dve)
            @_syn(nc)
            def _(s, m=m, par=par, dve_now=dve_now):
                if m >= 1:
                    s.wait_ge(dve, dve_now)
                s.dma_start(out=fsl[:, par * 4096: par * 4096 + 1024].rearrange("p (k c) -> p k c", c=128),
                            in_=wq_T.rearrange("(k p) d -> p k d", p=128)[:, :, m * 128:(m + 1) * 128]
                            ).then_inc(ldf0 if par == 0 else ldf1, 16)
            ctr.inc(ldf0 if par == 0 else ldf1, 16)
            ldfv = ctr.val(ldf0 if par == 0 else ldf1)
            @_ten(nc)
            def _(t, m=m, par=par, ldfv=ldfv):
                t.wait_ge(ldf0 if par == 0 else ldf1, ldfv)
                last = None
                for kq in range(8):
                    last = t.matmul(pS[0][:, 0:64],
                                    fsl[:, par * 4096 + kq * 128: par * 4096 + kq * 128 + 128],
                                    F0T[:, kq * 64:(kq + 1) * 64],
                                    start=(kq == 0), stop=(kq == 7))
                last.then_inc(pe, 1)
            ctr.inc(pe, 1)
            pev = ctr.val(pe)
            @_vec(nc)
            def _(v, m=m, pev=pev):
                v.wait_ge(pe, pev)
                v.tensor_copy(Msb[:, m * 64:(m + 1) * 64], pS[0][:, 0:64]).then_inc(dve, 1)
            ctr.inc(dve, 1)
        m_done_dve = ctr.val(dve)

        for sidx in range(16):
            par = 0
            @_syn(nc)
            def _(s, sidx=sidx, par=par):
                s.wait_ge(dve, m_done_dve + sidx)
                s.dma_start(out=fsl[:, par * 4096: (par + 1) * 4096].rearrange("p (k c) -> p k c", c=512),
                            in_=fT.rearrange("(k p) t -> p k t", p=128)[:, :, sidx * 512:(sidx + 1) * 512]
                            ).then_inc(ldf0 if par == 0 else ldf1, 16)
            ctr.inc(ldf0 if par == 0 else ldf1, 16)
            ldfv = ctr.val(ldf0 if par == 0 else ldf1)
            @_ten(nc)
            def _(t, sidx=sidx, par=par, ldfv=ldfv):
                t.wait_ge(ldf0 if par == 0 else ldf1, ldfv)
                last = None
                for k in range(8):
                    last = t.matmul(pV[0:64, 0:512], Msb[:, k * 64:(k + 1) * 64],
                                    fsl[:, par * 4096 + k * 512: par * 4096 + (k + 1) * 512],
                                    start=(k == 0), stop=(k == 7))
                last.then_inc(pe, 1)
            ctr.inc(pe, 1)
            pev = ctr.val(pe)
            @_vec(nc)
            def _(v, sidx=sidx, pev=pev):
                v.wait_ge(pe, pev)
                v.tensor_scalar(Psb[:, sidx * 512:(sidx + 1) * 512], pV[0:64, 0:512],
                                rsb[:], None, ALU.add).then_inc(dve, 1)
            ctr.inc(dve, 1)
        p_done_dve = ctr.val(dve)
        es_mp.close()
        fr = E(nc.sbuf_tensor(_pref + "fr", [128, 2 * D], F32))
        th = E(nc.sbuf_tensor(_pref + "th", [128, 2 * D], F32))
        att = E(nc.sbuf_tensor(_pref + "att", [128, 4 * D], F32))
        sqs = E(nc.sbuf_tensor(_pref + "sqs", [128, D], F32))
        x1t = E(nc.sbuf_tensor(_pref + "x1t", [128, 2 * D], F32))

        t_tiles = []
        for g in range(16):
            blk = []
            base = 64 + g * 512
            for (o, wdt) in ((0, 256), (256, 256)):
                ta = base + o
                tw = min(wdt, T - ta)
                if tw > 0:
                    blk.append((ta, tw))
            t_tiles.append(blk)

        @_ten(nc)
        def _(t):
            t.wait_ge(dve, p_done_dve)

        for g in range(16):
            dve_blk = ctr.val(dve)
            for sw in range(NSWEEP):
                rd, wr = sw % 2, (sw + 1) % 2
                tiles = t_tiles[g]
                nt_ = len(tiles)
                pevs = []
                for i, (ta, tw) in enumerate(tiles):
                    ua = ta + 64
                    @_ten(nc)
                    def _(t, ta=ta, tw=tw, ua=ua, rd=rd, i=i, sw=sw, dve_blk=dve_blk):
                        if sw == 0 and i == 0:
                            t.wait_ge(dve, dve_blk)
                        last = None
                        for c in range(3):
                            u0 = ua - 128 + c * 128
                            last = t.matmul(pS[i][:, c * 256: c * 256 + tw],
                                            ccb[rd][:, u0:u0 + 128],
                                            Psb[:, ta:ta + tw],
                                            start=True, stop=True)
                        last.then_inc(pe, 1)
                    ctr.inc(pe, 1)
                    pevs.append(ctr.val(pe))
                actvs = []
                for i, (ta, tw) in enumerate(tiles):
                    @_scl(nc)
                    def _(s, i=i, pev=pevs[i]):
                        s.wait_ge(pe, pev)
                        s.activation(esb[i][:], pS[i][:, 0:768], AF.Exp,
                                     bias=w2bb[:, 0:1]).then_inc(act, 1)
                    ctr.inc(act, 1)
                    actvs.append(ctr.val(act))
                dvevs = []
                for i, (ta, tw) in enumerate(tiles):
                    @_vec(nc)
                    def _(v, i=i, actv=actvs[i]):
                        v.wait_ge(act, actv)
                        v.tensor_mul(esb[i][:], esb[i][:], msk[:]).then_inc(dve, 1)
                    ctr.inc(dve, 1)
                    dvevs.append(ctr.val(dve))
                pevs = []
                for i, (ta, tw) in enumerate(tiles):
                    ua = ta + 64
                    @_ten(nc)
                    def _(t, ua=ua, tw=tw, rd=rd, i=i, dvev=dvevs[i]):
                        t.wait_ge(dve, dvev)
                        last = None
                        for c in range(3):
                            cch = (ua - 128) // 128 + c
                            last = t.matmul(pM[i][0:65, 0:tw],
                                            crb[rd][:, cch * 65: cch * 65 + 65],
                                            esb[i][:, c * 256: c * 256 + tw],
                                            start=(c == 0), stop=(c == 2))
                        last.then_inc(pe, 1)
                    ctr.inc(pe, 1)
                    pevs.append(ctr.val(pe))
                actvs = []
                for i, (ta, tw) in enumerate(tiles):
                    @_scl(nc)
                    def _(s, tw=tw, i=i, pev=pevs[i]):
                        s.wait_ge(pe, pev)
                        s.activation(zr[i][:, 0:tw], pM[i][64:65, 0:tw], AF.Ln,
                                     bias=ebv[0:1, 0:1])
                        s.drain()
                        s.activation(zr[i][:, 0:tw], zr[i][:, 0:tw], AF.Exp,
                                     bias=lnw[0:1, 0:1], scale=-1.0).then_inc(act, 1)
                    ctr.inc(act, 1)
                    actvs.append(ctr.val(act))
                pevs = []
                for i, (ta, tw) in enumerate(tiles):
                    @_ten(nc)
                    def _(t, tw=tw, i=i, actv=actvs[i]):
                        t.wait_ge(act, actv)
                        t.matmul(pM[i][0:64, 256:256 + tw], o164[:], zr[i][:, 0:tw],
                                 start=True, stop=True).then_inc(pe, 1)
                    ctr.inc(pe, 1)
                    pevs.append(ctr.val(pe))
                actvs = []
                for i, (ta, tw) in enumerate(tiles):
                    @_scl(nc)
                    def _(s, tw=tw, i=i, pev=pevs[i]):
                        s.wait_ge(pe, pev)
                        s.copy(rzb[i][:, 0:tw], pM[i][0:64, 256:256 + tw]).then_inc(act, 1)
                    ctr.inc(act, 1)
                    actvs.append(ctr.val(act))
                dvevs = []
                for i, (ta, tw) in enumerate(tiles):
                    ua = ta + 64
                    @_vec(nc)
                    def _(v, ua=ua, tw=tw, wr=wr, rd=rd, i=i, actv=actvs[i]):
                        v.wait_ge(act, actv)
                        v.tensor_mul(tmpc[i][:, 0:tw], pM[i][0:64, 0:tw],
                                     rzb[i][:, 0:tw])
                        v.drain()
                        v.scalar_tensor_tensor(ccb[wr][:, ua:ua + tw],
                                               ccb[rd][:, ua:ua + tw],
                                               1.0 - float(np.float32(OMEGA)),
                                               tmpc[i][:, 0:tw],
                                               ALU.mult, ALU.add).then_inc(dve, 1)
                    ctr.inc(dve, 1)
                    dvevs.append(ctr.val(dve))
                for i, (ta, tw) in enumerate(tiles):
                    ua = ta + 64
                    for sub in range((tw + 127) // 128):
                        swd = min(128, tw - sub * 128)
                        dve_pre = ctr.val(dve)
                        @_ten(nc)
                        def _(t, ua=ua, sub=sub, swd=swd, wr=wr, i=i,
                              dvev=dvevs[i], dve_pre=dve_pre):
                            t.wait_ge(dve, dvev if sub == 0 else dve_pre)
                            t.transpose(pM[i][0:swd, 448:512],
                                        ccb[wr][:, ua + sub * 128: ua + sub * 128 + swd],
                                        idn[0:64, 0:64]).then_inc(pe, 1)
                        ctr.inc(pe, 1)
                        pev = ctr.val(pe)
                        @_vec(nc)
                        def _(v, ua=ua, sub=sub, swd=swd, wr=wr, i=i, pev=pev):
                            v.wait_ge(pe, pev)
                            cch = ua // 128 + sub
                            v.tensor_copy(crb[wr][0:swd, cch * 65: cch * 65 + 64],
                                          pM[i][0:swd, 448:512]).then_inc(dve, 1)
                        ctr.inc(dve, 1)
            last_par = NSWEEP % 2
            u0b = 64 + g * 512 + 64
            u1b = min(u0b + 512, UW)
            dvev = ctr.val(dve)
            pev_all = ctr.val(pe)
            @_vec(nc)
            def _(v, u0b=u0b, u1b=u1b, last_par=last_par, dvev=dvev, pev_all=pev_all):
                v.wait_ge(dve, dvev)
                v.wait_ge(pe, pev_all)
                v.tensor_copy(ccb[1 - last_par][:, u0b:u1b], ccb[last_par][:, u0b:u1b])
                v.drain()
                c0, c1 = u0b // 128, (u1b + 127) // 128
                v.tensor_copy(crb[1 - last_par][:, c0 * 65: c1 * 65],
                              crb[last_par][:, c0 * 65: c1 * 65])
                v.drain()
                v.engine_nop().then_inc(dve, 1)
            ctr.inc(dve, 1)
        scan_dve = ctr.val(dve)
        ccf = ccb[NSWEEP % 2]

        @_ten(nc)
        def _(t):
            t.wait_ge(dve, scan_dve)
        tile_tanh_act = {}
        tile_mul_dve = {}
        tile_fadd_dve = {}
        tile_act_end = {}
        st_base_a = ctr.val(st)
        for it in range(64):
            tb = it * 128
            par = it % 2
            par4 = it % 4
            fguard = tile_fadd_dve.get(it - 2, 0)
            @_syn(nc)
            def _(s, it=it, par=par, fguard=fguard):
                if fguard > 0:
                    s.wait_ge(dve, fguard)
                elif it == 0:
                    s.wait_ge(dve, p_done_dve)
                ldr = ldr0 if par == 0 else ldr1
                s.dma_start(out=fr[:, par * D: par * D + D],
                            in_=f_tm[it * 128:(it + 1) * 128, :]).then_inc(ldr, 16)
            ctr.inc(ldr0 if par == 0 else ldr1, 16)
            ldrv = ctr.val(ldr0 if par == 0 else ldr1)
            vguard = tile_tanh_act.get(it - 2, 0)
            @_ten(nc)
            def _(t, tb=tb, par=par, vguard=vguard):
                if vguard > 0:
                    t.wait_ge(act, vguard)
                for vh in range(2):
                    pdst = pV[:, vh * 512:(vh + 1) * 512] if par == 0 \
                        else pS[vh][:, 0:512]
                    t.matmul(pdst, ccf[:, tb + 64: tb + 64 + 128],
                             F0sb[:, vh * 512:(vh + 1) * 512],
                             start=True, stop=True).then_inc(pe, 1)
            ctr.inc(pe, 2)
            pev = ctr.val(pe)
            tguard = tile_mul_dve.get(it - 2, 0)
            @_scl(nc)
            def _(s, par=par, pev=pev, tguard=tguard):
                s.wait_ge(pe, pev)
                if tguard > 0:
                    s.wait_ge(dve, tguard)
                if par == 0:
                    s.activation(th[:, 0:D], pV[:, 0:1024], AF.Tanh).then_inc(act, 1)
                else:
                    s.activation(th[:, D: D + 512], pS[0][:, 0:512], AF.Tanh)
                    s.drain()
                    s.activation(th[:, D + 512: 2 * D], pS[1][:, 0:512],
                                 AF.Tanh).then_inc(act, 1)
            ctr.inc(act, 1)
            tile_tanh_act[it] = ctr.val(act)
            actv = ctr.val(act)
            aguard = tile_act_end.get(it - 4, 0)
            @_vec(nc)
            def _(v, par=par, par4=par4, actv=actv, ldrv=ldrv, aguard=aguard):
                v.wait_ge(act, actv)
                v.wait_ge(ldr0 if par == 0 else ldr1, ldrv)
                if aguard > 0:
                    v.wait_ge(act, aguard)
                frt = fr[:, par * D: par * D + D]
                thp = th[:, par * D: par * D + D]
                v.scalar_tensor_tensor(att[:, par4 * D: par4 * D + D], thp, 1.0,
                                       frt, ALU.add, ALU.mult)
                v.drain()
                v.engine_nop().then_inc(dve, 1)
            ctr.inc(dve, 1)
            tile_mul_dve[it] = ctr.val(dve)
            attp = att[:, par4 * D: par4 * D + D]
            x1tp = x1t[:, par * D: par * D + D]
            cen1 = att[:, ((par4 + 1) % 4) * D: ((par4 + 1) % 4) * D + D]
            cen2 = att[:, ((par4 + 2) % 4) * D: ((par4 + 2) % 4) * D + D]
            mup = mu[:, par4: par4 + 1]
            tmp1p = mu[:, 4 + par4: 5 + par4]
            ssqp = ssq[:, par4: par4 + 1]
            rstdp = rstd[:, par4: par4 + 1]
            emit_ln_fast(nc, ctr, dve, act, attp, attp, cen1, mup, ssqp, rstdp, sqs,
                         epsb, tmp1p, gtA, btA)
            dvev = ctr.val(dve)
            st_w = st_base_a + (it - 1) * 16 if it >= 2 else 0
            @_vec(nc)
            def _(v, par=par, dvev=dvev, attp=attp, st_w=st_w):
                v.wait_ge(dve, dvev)
                if st_w > 0:
                    v.wait_ge(st, st_w)
                frt = fr[:, par * D: par * D + D]
                v.tensor_add(attp, attp, frt)
                v.drain()
                v.engine_nop().then_inc(dve, 1)
            ctr.inc(dve, 1)
            tile_fadd_dve[it] = ctr.val(dve)
            emit_ln_fast(nc, ctr, dve, act, attp, x1tp, cen2, mup, ssqp, rstdp, sqs,
                         epsb, tmp1p)
            tile_act_end[it] = ctr.val(act)
            dvev = ctr.val(dve)
            st_prev = ctr.val(st)
            @_gps(nc)
            def _(gg, it=it, dvev=dvev, x1tp=x1tp, st_prev=st_prev):
                gg.wait_ge(dve, dvev)
                if it > 0:
                    gg.wait_ge(st, st_prev)
                gg.dma_start(out=x1s[it * 128:(it + 1) * 128, :], in_=x1tp).then_inc(st, 16)
            ctr.inc(st, 16)
        @_gps(nc)
        def _(g):
            g.wait_ge(st, ctr.val(st))
    return nc


def build_stage_a():
    nc = bass.Bass("TRN2", target_bir_lowering=False, debug=False)
    io = {}
    io["fT"] = nc.dram_tensor("fT", [D, T], F32, kind="ExternalInput").ap()
    io["f_tm"] = nc.dram_tensor("f_tm", [T, D], F32, kind="ExternalInput").ap()
    io["wq_T"] = nc.dram_tensor("wq_T", [D, D], F32, kind="ExternalInput").ap()
    io["w2col"] = nc.dram_tensor("w2col", [128, 8], F32, kind="ExternalInput").ap()
    io["wqbcol"] = nc.dram_tensor("wqbcol", [128, 8], F32, kind="ExternalInput").ap()
    io["maskM"] = nc.dram_tensor("maskM", [128, 3 * 256], F32, kind="ExternalInput").ap()
    io["ones164"] = nc.dram_tensor("ones164", [1, 64], F32, kind="ExternalInput").ap()
    io["onesr128"] = nc.dram_tensor("onesr128", [1, 128], F32, kind="ExternalInput").ap()
    io["w2b_in"] = nc.dram_tensor("w2b_in", [1, 1], F32, kind="ExternalInput").ap()
    io["ident"] = nc.dram_tensor("ident", [128, 128], F32, kind="ExternalInput").ap()
    io["cc_init"] = nc.dram_tensor("cc_init", [64, UW], F32, kind="ExternalInput").ap()
    io["cr_init"] = nc.dram_tensor("cr_init", [128, 65 * 65], F32, kind="ExternalInput").ap()
    io["gbcA"] = nc.dram_tensor("gbcA", [128, D], F32, kind="ExternalInput").ap()
    io["bbcA"] = nc.dram_tensor("bbcA", [128, D], F32, kind="ExternalInput").ap()
    io["x1s"] = nc.dram_tensor("x1s", [T, D], F32, kind="ExternalOutput").ap()
    ctr = Ctr()
    sems = {}
    from contextlib import ExitStack
    with ExitStack() as ses:
        for sname in ["ld", "lds", "ldf0", "ldf1", "ldr0", "ldr1", "pe", "dve", "act", "st"]:
            sems[sname] = ses.enter_context(nc.semaphore(sname))
        emit_stage_a(nc, ctr, sems, io)
    return nc


def make_stage_a_consts():
    UWl = UW
    cc = np.zeros((64, UWl), np.float32)
    cc[:, 64:128] = np.eye(64, dtype=np.float32)
    cr = np.zeros((128, 65 * 65), np.float32)
    for ch in range(65):
        u0 = ch * 128
        for p in range(128):
            u = u0 + p
            if 64 <= u < 128:
                cr[p, ch * 65 + (u - 64)] = 1.0
        cr[:, ch * 65 + 64] = 1.0
    msk = np.zeros((128, 3 * 256), np.float32)
    for c in range(3):
        for tr in range(256):
            for taur in range(128):
                dtt = tr - (taur + c * 128 - 128)
                if 0 < dtt <= 64:
                    msk[taur, c * 256 + tr] = 1.0
    return cc, cr, msk


def run_stage_a(inp):
    nc = build_stage_a()
    cc, cr, msk = make_stage_a_consts()
    f = inp["L0_attn_feature"]
    wq = inp["L0_attn_wq_w"]
    w2 = inp["L0_attn_w2_w"]
    wqb = inp["L0_attn_wq_b"]
    g_, b_ = inp["L0_attn_ln_g"], inp["L0_attn_ln_b"]
    in_maps = []
    for c in range(NCORE):
        bb = c % 2
        in_maps.append({
            "fT": np.ascontiguousarray(f[bb].T),
            "f_tm": np.ascontiguousarray(f[bb]),
            "wq_T": np.ascontiguousarray(wq.T),
            "w2col": np.ascontiguousarray(w2.reshape(8, 128).T),
            "wqbcol": np.ascontiguousarray(wqb.reshape(8, 128).T),
            "maskM": msk,
            "ones164": np.full((1, 64), np.float32(OMEGA), np.float32),
            "onesr128": np.ones((1, 128), np.float32),
            "w2b_in": np.asarray(inp["L0_attn_w2_b"], dtype=np.float32).reshape(1, 1),
            "ident": np.eye(128, dtype=np.float32),
            "cc_init": cc,
            "cr_init": cr,
            "gbcA": np.ascontiguousarray(np.broadcast_to(g_, (128, D))),
            "bbcA": np.ascontiguousarray(np.broadcast_to(b_, (128, D))),
        })
    res = run_bass_kernel_spmd(nc, in_maps, core_ids=list(range(NCORE)))
    x1 = np.stack([np.asarray(res.results[0]["x1s"]), np.asarray(res.results[1]["x1s"])], 0)
    return x1.reshape(-1, D)




def emit_stage_b(nc, ctr, sems, io):
    _pref = io.get("prefix", "b_")
    (x1full, bgidx, bjunk, rw_T, rb_bc, iota32, iotac, lt128, ones_col, ones_row,
     ident, w1_t, b1c, w2_t, b2bc, x2s, xstage, tokstage, ydram) = (
        io["x1full"], io["bgidx"], io["bjunk"], io["rw_T"], io["rb_bc"], io["iota32"],
        io["iotac"], io["lt128"], io["ones_col"], io["ones_row"], io["ident"],
        io["w1_t"], io["b1c"], io["w2_t"], io["b2bc"], io["x2s"],
        io["xstage"], io["tokstage"], io["ydram"])
    NSLOT = 32 * CAP_B
    NTOK = io["ntok"]
    NX1 = io["nx1"]
    NT = NTOK // 128
    from contextlib import ExitStack
    with ExitStack() as es:
        E = es.enter_context
        rwt = E(nc.sbuf_tensor(_pref + "rwt", [128, 8 * 32], F32))
        rbt = E(nc.sbuf_tensor(_pref + "rbt", [128, 32], F32))
        io32 = E(nc.sbuf_tensor(_pref + "io32", [128, 32], F32))
        ioc = E(nc.sbuf_tensor(_pref + "ioc", [128, 1], F32))
        tcol = E(nc.sbuf_tensor(_pref + "tcol", [128, 1], F32))
        ltt = E(nc.sbuf_tensor(_pref + "ltt", [128, 128], F32))
        onc = E(nc.sbuf_tensor(_pref + "onc", [128, 1], F32))
        onr = E(nc.sbuf_tensor(_pref + "onr", [1, 128], F32))
        idn = E(nc.sbuf_tensor(_pref + "idn", [128, 128], F32))
        hug = E(nc.sbuf_tensor(_pref + "hug", [128, NSLOT // 128], F32))
        zt = E(nc.sbuf_tensor(_pref + "zt", [128, D], F32))
        xsl = E(nc.sbuf_tensor(_pref + "xsl", [128, 8 * 128], F32))
        idxt = E(nc.sbuf_tensor(_pref + "idxt", [128, 2], I32))
        jkt = E(nc.sbuf_tensor(_pref + "jkt", [128, 1], F32))
        x1r = E(nc.sbuf_tensor(_pref + "x1r", [128, 2 * D], F32))
        lg = E(nc.sbuf_tensor(_pref + "lg", [128, 32], F32))
        mx8 = E(nc.sbuf_tensor(_pref + "mx8", [128, 8], F32))
        ix8 = E(nc.sbuf_tensor(_pref + "ix8", [128, 8], U32))
        ef32 = E(nc.sbuf_tensor(_pref + "ef32", [128, 1], F32))
        eq = E(nc.sbuf_tensor(_pref + "eq", [128, 32], F32))
        bb32 = E(nc.sbuf_tensor(_pref + "bb32", [128, 32], F32))
        tmp32 = E(nc.sbuf_tensor(_pref + "tmp32", [128, 32], F32))
        basev = E(nc.sbuf_tensor(_pref + "basev", [1, 32], F32))
        rankv = E(nc.sbuf_tensor(_pref + "rankv", [128, 1], F32))
        posv = E(nc.sbuf_tensor(_pref + "posv", [128, 1], F32))
        ovf = E(nc.sbuf_tensor(_pref + "ovf", [128, 1], F32))
        posi = E(nc.sbuf_tensor(_pref + "posi", [128, 1], I32))
        scr32 = E(nc.sbuf_tensor(_pref + "scr32", [128, 32], F32))
        xg = E(nc.sbuf_tensor(_pref + "xg", [128, D], F32))
        xgT = E(nc.sbuf_tensor(_pref + "xgT", [128, 8 * 128], BF16))
        w1sb = E(nc.sbuf_tensor(_pref + "w1sb", [128, 2 * 8 * 256], BF16))
        w2sb = E(nc.sbuf_tensor(_pref + "w2sb", [128, 2 * 2 * D], BF16))
        b1sb = E(nc.sbuf_tensor(_pref + "b1sb", [128, 4], F32))
        b2sb = E(nc.sbuf_tensor(_pref + "b2sb", [128, 2 * D], F32))
        hg = E(nc.sbuf_tensor(_pref + "hg", [128, 2 * 128], BF16))
        ysb = E(nc.sbuf_tensor(_pref + "ysb", [128, D], F32))
        tif = E(nc.sbuf_tensor(_pref + "tif", [128, 1], F32))
        ti32 = E(nc.sbuf_tensor(_pref + "ti32", [128, 1], I32))
        att = E(nc.sbuf_tensor(_pref + "att", [128, 2 * D], F32))
        sqs = E(nc.sbuf_tensor(_pref + "sqs", [128, D], F32))
        x2t = E(nc.sbuf_tensor(_pref + "x2t", [128, 2 * D], F32))
        yr = E(nc.sbuf_tensor(_pref + "yr", [128, 2 * D], F32))
        mu = E(nc.sbuf_tensor(_pref + "mu", [128, 4], F32))
        ssq = E(nc.sbuf_tensor(_pref + "ssq", [128, 2], F32))
        rstd = E(nc.sbuf_tensor(_pref + "rstd", [128, 2], F32))
        epsb = E(nc.sbuf_tensor(_pref + "epsb", [128, 1], F32))
        pL = E(nc.psum_tensor(_pref + "pL", [128, 128], F32))
        pL2 = E(nc.psum_tensor(_pref + "pL2", [128, 128], F32))
        pQ = E(nc.psum_tensor(_pref + "pQ", [128, 1024], F32))
        pH = E(nc.psum_tensor(_pref + "pH", [128, 256], F32))
        lds = sems["lds"]
        ldr0 = sems["ldr0"]
        ldsh = sems["ldsh"]
        ldq = sems["ldq"]
        ldg = sems["ldg"]
        sct = sems["sct"]
        yst = sems["yst"]
        zst = sems["zst"]
        gth = sems["gth"]
        pe = sems["pe"]
        dve = sems["dve"]
        act = sems["act"]
        st = sems["st"]
        rb_slot = nc.gpsimd.to_reg(NSLOT)
        rb_tok = nc.gpsimd.to_reg(NTOK)
        rb_x1 = nc.gpsimd.to_reg(NX1)

        @_syn(nc)
        def _(s):
            s.dma_start(out=rwt[:].rearrange("p (k e) -> p k e", e=32),
                        in_=rw_T.rearrange("(k p) e -> p k e", p=128)).then_inc(lds, 16)
            s.dma_start(out=rbt[:], in_=rb_bc).then_inc(lds, 16)
            s.dma_start(out=io32[:], in_=iota32).then_inc(lds, 16)
            s.dma_start(out=ioc[:], in_=iotac).then_inc(lds, 16)
            s.dma_start(out=ltt[:], in_=lt128).then_inc(lds, 16)
            s.dma_start(out=onc[:], in_=ones_col).then_inc(lds, 16)
            s.dma_start(out=onr[:], in_=ones_row).then_inc(lds, 16)
            s.dma_start(out=idn[:], in_=ident).then_inc(lds, 16)
        ctr.inc(lds, 8 * 16)
        lds_all = ctr.val(lds)

        @_vec(nc)
        def _(v):
            v.memset(hug[:], 1.0e9)
            v.drain()
            v.memset(zt[:], 0.0)
            v.drain()
            v.memset(basev[:], 0.0)
            v.drain()
            v.memset(epsb[:], LN_EPS)
            v.drain()
            v.engine_nop().then_inc(dve, 1)
        ctr.inc(dve, 1)
        ms_dve = ctr.val(dve)

        @_syn(nc)
        def _(s):
            s.wait_ge(dve, ms_dve)
            s.dma_start(out=tokstage.rearrange("(b a) c -> b (a c)", b=128),
                        in_=hug[:]).then_inc(zst, 16)
            for i in range(NT):
                s.dma_start(out=ydram[i * 128:(i + 1) * 128, :], in_=zt[:]).then_inc(zst, 16)
            for i in range(NSLOT // 128):
                s.dma_start(out=xstage[i * 128:(i + 1) * 128, :], in_=zt[:]).then_inc(zst, 16)
        ctr.inc(zst, 16 * (1 + NT + NSLOT // 128))
        zst_all = ctr.val(zst)

        sct_rt0 = ctr.val(sct)
        for it in range(NT):
            sct_prev = sct_rt0 + 32 * it
            sct_base = ctr.val(sct)
            @_syn(nc)
            def _(s, it=it, sct_prev=sct_prev):
                if it > 0:
                    s.wait_ge(sct, sct_prev)
                s.dma_start(out=idxt[:, 0:1], in_=bgidx[it * 128:(it + 1) * 128, :]).then_inc(ldq, 16)
                s.dma_start(out=jkt[:], in_=bjunk[it * 128:(it + 1) * 128, :]).then_inc(ldq, 16)
            ctr.inc(ldq, 32)
            ldqv = ctr.val(ldq)
            dve_m = ctr.val(dve)
            @_vec(nc)
            def _(v, ldqv=ldqv, it=it, sct_prev=sct_prev):
                v.wait_ge(ldq, ldqv)
                if it > 0:
                    v.wait_ge(sct, sct_prev)
                v.memset(x1r[:, 0:D], 0.0)
                v.drain()
                v.engine_nop().then_inc(dve, 1)
            ctr.inc(dve, 1)
            dvev = ctr.val(dve)
            @_gps(nc)
            def _(g, dvev=dvev):
                g.wait_ge(dve, dvev)
                g.indirect_dma_start(x1r[:, 0:D], None, x1full,
                                     IndirectOffsetOnAxis(ap=idxt[:, 0:1], axis=0),
                                     bounds_check=rb_x1, oob_is_err=False).then_inc(gth, 16)
            ctr.inc(gth, 16)
            gthv = ctr.val(gth)
            tr_ev = {}
            for kd in range(8):
                pLx = pL if kd % 2 == 0 else pL2
                ev_pre = tr_ev.get(kd - 2, 0)
                @_ten(nc)
                def _(t, kd=kd, gthv=gthv, ev_pre=ev_pre, pLx=pLx):
                    if kd == 0:
                        t.wait_ge(lds, lds_all)
                        t.wait_ge(gth, gthv)
                    if ev_pre > 0:
                        t.wait_ge(dve, ev_pre)
                    t.transpose(pLx[:, 0:128], x1r[:, kd * 128:(kd + 1) * 128],
                                idn[:]).then_inc(pe, 1)
                ctr.inc(pe, 1)
                pev = ctr.val(pe)
                @_vec(nc)
                def _(v, kd=kd, pev=pev, pLx=pLx):
                    v.wait_ge(pe, pev)
                    v.tensor_copy(xsl[:, kd * 128:(kd + 1) * 128], pLx[:, 0:128]).then_inc(dve, 1)
                ctr.inc(dve, 1)
                tr_ev[kd] = ctr.val(dve)
            dvev = ctr.val(dve)
            @_ten(nc)
            def _(t, dvev=dvev):
                t.wait_ge(dve, dvev)
                last = None
                for kd in range(8):
                    last = t.matmul(pL[:, 0:32], xsl[:, kd * 128:(kd + 1) * 128],
                                    rwt[:, kd * 32:(kd + 1) * 32],
                                    start=(kd == 0), stop=(kd == 7))
                last.then_inc(pe, 1)
            ctr.inc(pe, 1)
            pev = ctr.val(pe)
            @_vec(nc)
            def _(v, it=it, pev=pev):
                v.wait_ge(pe, pev)
                v.tensor_add(lg[:], pL[:, 0:32], rbt[:])
                v.drain()
                v.max(mx8[:], lg[:])
                v.drain()
                v.max_index(ix8[:], mx8[:], lg[:])
                v.drain()
                v.tensor_copy(ef32[:], ix8[:, 0:1])
                v.drain()
                v.tensor_scalar(eq[:], io32[:], ef32[:], None, ALU.is_equal)
                v.drain()
                v.tensor_scalar(tcol[:], ioc[:], float(it * 128), None, ALU.add)
                v.drain()
                v.engine_nop().then_inc(dve, 1)
            ctr.inc(dve, 1)
            dvev = ctr.val(dve)
            @_ten(nc)
            def _(t, dvev=dvev):
                t.wait_ge(dve, dvev)
                t.matmul(pL[:, 32:64], ltt[:], eq[:], start=True, stop=True)
                t.matmul(pL[:, 64:96], onr[:], basev[:], start=True, stop=True)
                t.matmul(pL[0:1, 96:128], onc[:], eq[:], start=True, stop=True).then_inc(pe, 1)
            ctr.inc(pe, 1)
            pev = ctr.val(pe)
            @_scl(nc)
            def _(s, pev=pev):
                s.wait_ge(pe, pev)
                s.copy(bb32[:], pL[:, 64:96]).then_inc(act, 1)
            ctr.inc(act, 1)
            actv = ctr.val(act)
            @_vec(nc)
            def _(v, actv=actv):
                v.wait_ge(act, actv)
                v.tensor_add(tmp32[:], pL[:, 32:64], bb32[:])
                v.drain()
                v.scalar_tensor_tensor(scr32[:], tmp32[:], 1.0, eq[:], ALU.mult, ALU.mult,
                                       accum_out=rankv[:])
                v.drain()
                v.tensor_scalar(posv[:], ef32[:], float(CAP_B), None, ALU.mult)
                v.drain()
                v.tensor_add(posv[:], posv[:], rankv[:])
                v.drain()
                v.tensor_add(posv[:], posv[:], jkt[:])
                v.drain()
                v.tensor_scalar(ovf[:], rankv[:], float(CAP_B), None, ALU.is_ge)
                v.drain()
                v.scalar_tensor_tensor(posv[:], ovf[:], 1.0e9, posv[:], ALU.mult, ALU.add)
                v.drain()
                v.tensor_copy(posi[:], posv[:])
                v.drain()
                v.tensor_add(basev[:], basev[:], pL[0:1, 96:128])
                v.drain()
                v.engine_nop().then_inc(dve, 1)
            ctr.inc(dve, 1)
            dvev = ctr.val(dve)
            @_gps(nc)
            def _(g, dvev=dvev):
                g.wait_ge(zst, zst_all)
                g.wait_ge(dve, dvev)
                g.indirect_dma_start(
                    xstage, IndirectOffsetOnAxis(ap=posi[:], axis=0), x1r[:, 0:D], None,
                    bounds_check=rb_slot, oob_is_err=False,
                ).then_inc(sct, 16)
                g.indirect_dma_start(
                    tokstage, IndirectOffsetOnAxis(ap=posi[:], axis=0), tcol[:], None,
                    bounds_check=rb_slot, oob_is_err=False,
                ).then_inc(sct, 16)
            ctr.inc(sct, 32)
        sct_all = ctr.val(sct)

        NSUB = (CAP_B + 127) // 128
        exp_pe = {}
        exp_dve = {}
        exp_act = {}
        prev_tr = 0
        prev_sttv = 0
        for e in range(32):
            epb = e % 2
            ldw = ldg if epb == 0 else ldsh
            wguard_e = exp_pe.get(e - 2, 0)
            bguard_e = exp_dve.get(e - 2, 0)
            aguard_e = exp_act.get(e - 2, 0) if e >= 2 else 0
            @_gps(nc)
            def _(g, e=e, epb=epb, ldw=ldw, wguard_e=wguard_e):
                if e == 0:
                    g.wait_ge(sct, sct_all)
                if wguard_e > 0:
                    g.wait_ge(pe, wguard_e)
                g.dma_start(out=w1sb[:, epb * 2048: (epb + 1) * 2048].rearrange("p (k h) -> p k h", h=256),
                            in_=w1_t[e].rearrange("(k p) h -> p k h", p=128)).then_inc(ldw, 16)
                g.dma_start(out=w2sb[:, epb * 2 * D: (epb + 1) * 2 * D].rearrange("p (k d) -> p k d", d=D),
                            in_=w2_t[e].rearrange("(k p) d -> p k d", p=128)).then_inc(ldw, 16)
            ctr.inc(ldw, 32)
            ldgv = ctr.val(ldw)
            @_syn(nc)
            def _(s, e=e, epb=epb, bguard_e=bguard_e, aguard_e=aguard_e):
                s.wait_ge(sct, sct_all)
                if bguard_e > 0:
                    s.wait_ge(dve, bguard_e)
                if aguard_e > 0:
                    s.wait_ge(act, aguard_e)
                s.dma_start(out=b1sb[:, epb * 2: epb * 2 + 2], in_=b1c[e]).then_inc(ldq, 16)
                s.dma_start(out=b2sb[:, epb * D: (epb + 1) * D], in_=b2bc[e]).then_inc(ldq, 16)
            ctr.inc(ldq, 32)
            bias_ldq = ctr.val(ldq)
            for sub in range(NSUB):
                stt = min(128, CAP_B - sub * 128)
                base_slot = e * CAP_B + sub * 128
                dve_sub = ctr.val(dve)
                @_syn(nc)
                def _(s, base_slot=base_slot, stt=stt, sub=sub, dve_sub=dve_sub, edve=prev_sttv):
                    s.wait_ge(sct, sct_all)
                    if sub > 0:
                        s.wait_ge(dve, dve_sub)
                    elif edve > 0:
                        s.wait_ge(dve, edve)
                    s.dma_start(out=xg[0:stt, :], in_=xstage[base_slot:base_slot + stt, :]).then_inc(ldr0, 16)
                    s.dma_start(out=tif[0:stt, :], in_=tokstage[base_slot:base_slot + stt, :]).then_inc(ldr0, 16)
                ctr.inc(ldr0, 32)
                ldqv = ctr.val(ldr0)
                tr_ev = {}
                for kd in range(8):
                    pLx = pL if kd % 2 == 0 else pL2
                    ev_pre = tr_ev.get(kd - 2, 0)
                    @_ten(nc)
                    def _(t, stt=stt, kd=kd, ldqv=ldqv, ev_pre=ev_pre, pLx=pLx):
                        if kd == 0:
                            t.wait_ge(ldr0, ldqv)
                        if ev_pre > 0:
                            t.wait_ge(dve, ev_pre)
                        t.transpose(pLx[0:128, 0:stt], xg[0:stt, kd * 128:(kd + 1) * 128],
                                    idn[0:stt, 0:stt]).then_inc(pe, 1)
                    ctr.inc(pe, 1)
                    pev = ctr.val(pe)
                    @_vec(nc)
                    def _(v, stt=stt, kd=kd, pev=pev, pLx=pLx):
                        v.wait_ge(pe, pev)
                        v.tensor_copy(xgT[:, kd * 128: kd * 128 + stt],
                                      pLx[0:128, 0:stt]).then_inc(dve, 1)
                    ctr.inc(dve, 1)
                    tr_ev[kd] = ctr.val(dve)
                prev_tr = ctr.val(dve)
                dvev = ctr.val(dve)
                @_ten(nc)
                def _(t, stt=stt, dvev=dvev, ldgv=ldgv, epb=epb):
                    t.wait_ge(dve, dvev)
                    t.wait_ge(ldg if epb == 0 else ldsh, ldgv)
                    last = None
                    for hc in range(2):
                        for kd in range(8):
                            last = t.matmul(pH[:, hc * 128: hc * 128 + stt],
                                            w1sb[:, epb * 2048 + kd * 256 + hc * 128: epb * 2048 + kd * 256 + hc * 128 + 128],
                                            xgT[:, kd * 128: kd * 128 + stt],
                                            start=(kd == 0), stop=(kd == 7))
                    last.then_inc(pe, 1)
                ctr.inc(pe, 1)
                pev = ctr.val(pe)
                @_scl(nc)
                def _(s, stt=stt, pev=pev, epb=epb, bias_ldq=bias_ldq):
                    s.wait_ge(pe, pev)
                    s.wait_ge(ldq, bias_ldq)
                    s.activation(hg[:, 0:stt], pH[:, 0:stt], AF.Relu, bias=b1sb[:, epb * 2: epb * 2 + 1])
                    s.drain()
                    s.activation(hg[:, 128:128 + stt], pH[:, 128:128 + stt], AF.Relu,
                                 bias=b1sb[:, epb * 2 + 1: epb * 2 + 2]).then_inc(act, 1)
                ctr.inc(act, 1)
                actv = ctr.val(act)
                @_ten(nc)
                def _(t, stt=stt, actv=actv, epb=epb):
                    t.wait_ge(act, actv)
                    last = None
                    for dh in range(2):
                        for kh in range(2):
                            last = t.matmul(pQ[0:stt, dh * 512: dh * 512 + 512],
                                            hg[:, kh * 128: kh * 128 + stt],
                                            w2sb[:, epb * 2 * D + kh * D + dh * 512: epb * 2 * D + kh * D + dh * 512 + 512],
                                            start=(kh == 0), stop=(kh == 1))
                    last.then_inc(pe, 1)
                ctr.inc(pe, 1)
                pev = ctr.val(pe)
                yst_prev = ctr.val(yst)
                @_vec(nc)
                def _(v, stt=stt, pev=pev, yst_prev=yst_prev, epb=epb, bias_ldq=bias_ldq):
                    v.wait_ge(pe, pev)
                    v.wait_ge(ldq, bias_ldq)
                    if yst_prev > 0:
                        v.wait_ge(yst, yst_prev)
                    v.scalar_tensor_tensor(ysb[0:stt, :], pQ[0:stt, :], 1.0,
                                           b2sb[0:stt, epb * D: (epb + 1) * D], ALU.mult, ALU.add)
                    v.drain()
                    v.tensor_scalar_max(ysb[0:stt, :], ysb[0:stt, :], 0.0)
                    v.drain()
                    v.tensor_copy(ti32[0:stt, :], tif[0:stt, :])
                    v.drain()
                    v.engine_nop().then_inc(dve, 1)
                ctr.inc(dve, 1)
                prev_sttv = ctr.val(dve)
                dvev = ctr.val(dve)
                @_gps(nc)
                def _(g, stt=stt, dvev=dvev):
                    g.wait_ge(dve, dvev)
                    g.indirect_dma_start(
                        ydram, IndirectOffsetOnAxis(ap=ti32[0:stt, :], axis=0),
                        ysb[0:stt, :], None,
                        bounds_check=rb_tok, oob_is_err=False,
                    ).then_inc(yst, 16)
                ctr.inc(yst, 16)
            exp_pe[e] = ctr.val(pe)
            exp_dve[e] = ctr.val(dve)
            exp_act[e] = ctr.val(act)
        yst_all = ctr.val(yst)

        st_base_f = ctr.val(st)
        fin_gth = {}
        fin_add = {}
        for it in range(NT):
            par = it % 2
            ldq_prev = ctr.val(ldq)
            gguard = fin_gth.get(it - 2, 0)
            aguard2 = fin_add.get(it - 2, 0)
            @_syn(nc)
            def _(s, it=it, par=par, ldq_prev=ldq_prev, gguard=gguard, aguard2=aguard2):
                if it == 0:
                    s.wait_ge(yst, yst_all)
                else:
                    s.wait_ge(ldq, ldq_prev)
                if gguard > 0:
                    s.wait_ge(gth, gguard)
                if aguard2 > 0:
                    s.wait_ge(dve, aguard2)
                s.dma_start(out=yr[:, par * D: par * D + D],
                            in_=ydram[it * 128:(it + 1) * 128, :]).then_inc(ldq, 16)
                s.dma_start(out=idxt[:, par: par + 1],
                            in_=bgidx[it * 128:(it + 1) * 128, :]).then_inc(ldq, 16)
            ctr.inc(ldq, 32)
            ldqv = ctr.val(ldq)
            @_vec(nc)
            def _(v, par=par, ldqv=ldqv):
                v.wait_ge(ldq, ldqv)
                v.memset(x1r[:, par * D: par * D + D], 0.0)
                v.drain()
                v.engine_nop().then_inc(dve, 1)
            ctr.inc(dve, 1)
            dvev = ctr.val(dve)
            gth_prev = ctr.val(gth)
            @_gps(nc)
            def _(g, par=par, dvev=dvev, gth_prev=gth_prev, it=it):
                g.wait_ge(dve, dvev)
                if it > 0:
                    g.wait_ge(gth, gth_prev)
                g.indirect_dma_start(x1r[:, par * D: par * D + D], None, x1full,
                                     IndirectOffsetOnAxis(ap=idxt[:, par: par + 1], axis=0),
                                     bounds_check=rb_x1, oob_is_err=False).then_inc(gth, 16)
            ctr.inc(gth, 16)
            gthv = ctr.val(gth)
            fin_gth[it] = gthv
            st_w = st_base_f + (it - 1) * 16 if it >= 2 else 0
            @_vec(nc)
            def _(v, par=par, gthv=gthv, st_w=st_w):
                v.wait_ge(gth, gthv)
                if st_w > 0:
                    v.wait_ge(st, st_w)
                v.tensor_add(att[:, par * D: par * D + D],
                             x1r[:, par * D: par * D + D],
                             yr[:, par * D: par * D + D])
                v.drain()
                v.engine_nop().then_inc(dve, 1)
            ctr.inc(dve, 1)
            fin_add[it] = ctr.val(dve)
            emit_ln_fast(nc, ctr, dve, act,
                         att[:, par * D: par * D + D],
                         x2t[:, par * D: par * D + D],
                         x1r[:, par * D: par * D + D],
                         mu[:, par: par + 1], ssq[:, par: par + 1],
                         rstd[:, par: par + 1], sqs, epsb,
                         mu[:, 2 + par: 3 + par])
            dvev = ctr.val(dve)
            st_prev = ctr.val(st)
            @_gps(nc)
            def _(g, it=it, par=par, dvev=dvev, st_prev=st_prev):
                g.wait_ge(dve, dvev)
                if it > 0:
                    g.wait_ge(st, st_prev)
                g.dma_start(out=x2s[it * 128:(it + 1) * 128, :],
                            in_=x2t[:, par * D: par * D + D]).then_inc(st, 16)
            ctr.inc(st, 16)
        @_gps(nc)
        def _(g):
            g.wait_ge(st, ctr.val(st))
    return nc


def build_stage_b():
    nc = bass.Bass("TRN2", target_bir_lowering=False, debug=False)
    NSLOT = 32 * CAP_B
    io = {}
    io["ntok"] = TPC
    io["nx1"] = TPC
    io["x1full"] = nc.dram_tensor("x1full", [TPC, D], F32, kind="ExternalInput").ap()
    io["bgidx"] = nc.dram_tensor("bgidx", [TPC, 1], I32, kind="ExternalInput").ap()
    io["bjunk"] = nc.dram_tensor("bjunk", [TPC, 1], F32, kind="ExternalInput").ap()
    io["rw_T"] = nc.dram_tensor("rw_T", [D, 32], F32, kind="ExternalInput").ap()
    io["rb_bc"] = nc.dram_tensor("rb_bc", [128, 32], F32, kind="ExternalInput").ap()
    io["iota32"] = nc.dram_tensor("iota32", [128, 32], F32, kind="ExternalInput").ap()
    io["iotac"] = nc.dram_tensor("iotac", [128, 1], F32, kind="ExternalInput").ap()
    io["lt128"] = nc.dram_tensor("lt128", [128, 128], F32, kind="ExternalInput").ap()
    io["ones_col"] = nc.dram_tensor("ones_col", [128, 1], F32, kind="ExternalInput").ap()
    io["ones_row"] = nc.dram_tensor("ones_row", [1, 128], F32, kind="ExternalInput").ap()
    io["ident"] = nc.dram_tensor("ident", [128, 128], F32, kind="ExternalInput").ap()
    io["w1_t"] = nc.dram_tensor("w1_t", [32, D, 256], F32, kind="ExternalInput").ap()
    io["b1c"] = nc.dram_tensor("b1c", [32, 128, 2], F32, kind="ExternalInput").ap()
    io["w2_t"] = nc.dram_tensor("w2_t", [32, 256, D], F32, kind="ExternalInput").ap()
    io["b2bc"] = nc.dram_tensor("b2bc", [32, 128, D], F32, kind="ExternalInput").ap()
    io["x2s"] = nc.dram_tensor("x2s", [TPC, D], F32, kind="ExternalOutput").ap()
    io["xstage"] = nc.dram_tensor("xstage", [NSLOT, D], F32, kind="Internal").ap()
    io["tokstage"] = nc.dram_tensor("tokstage", [NSLOT, 1], F32, kind="Internal").ap()
    io["ydram"] = nc.dram_tensor("ydram", [TPC, D], F32, kind="Internal").ap()
    ctr = Ctr()
    sems = {}
    from contextlib import ExitStack
    with ExitStack() as ses:
        for sname in ["lds", "ldq", "ldg", "sct", "yst", "zst", "gth", "pe", "dve",
                      "act", "st", "ldsh", "ldr0"]:
            sems[sname] = ses.enter_context(nc.semaphore(sname))
        emit_stage_b(nc, ctr, sems, io)
    return nc




def run_stage_b(inp, x1_full):
    nc = build_stage_b()
    lt = np.tril(np.ones((128, 128), np.float32), -1)
    shared = {
        "rw_T": np.ascontiguousarray(inp["L0_ffn_router_w"].T),
        "rb_bc": np.ascontiguousarray(np.broadcast_to(inp["L0_ffn_router_b"], (128, 32))),
        "iota32": np.ascontiguousarray(np.broadcast_to(np.arange(32, dtype=np.float32), (128, 32))),
        "iotac": np.arange(128, dtype=np.float32).reshape(128, 1),
        "lt128": lt,
        "ones_col": np.ones((128, 1), np.float32),
        "ones_row": np.ones((1, 128), np.float32),
        "ident": np.eye(128, dtype=np.float32),
        "w1_t": np.ascontiguousarray(inp["L0_ffn_weights1"]),
        "b1c": np.ascontiguousarray(inp["L0_ffn_bias1"].reshape(32, 2, 128).transpose(0, 2, 1)),
        "w2_t": np.ascontiguousarray(inp["L0_ffn_weights2"]),
        "b2bc": np.ascontiguousarray(np.broadcast_to(inp["L0_ffn_bias2"][:, None, :], (32, 128, 1024))),
        }
    in_maps = []
    for c in range(NCORE):
        sl = x1_full[c * TPC:(c + 1) * TPC]
        in_maps.append(dict(shared, x1s=np.ascontiguousarray(sl),
                            x1sT=np.ascontiguousarray(sl.T)))
    res = run_bass_kernel_spmd(nc, in_maps, core_ids=list(range(NCORE)))
    return np.concatenate([np.asarray(r["x2s"]) for r in res.results], 0)




def emit_stage_d(nc, ctr, sems, io):
    _pref = io.get("prefix", "d_")
    (x3s, gw_T, gb_bc, iota8, iotac, lt128, ones_col, ones_row, ident,
     w1_t, w3_t, w2_t, sw1_t, sw3_t, sw2_t, x4s,
     xstage, tokstage, gstage, y1dram, y2dram, zdram) = (
        io["x3s"], io["gw_T"], io["gb_bc"], io["iota8"], io["iotac"],
        io["lt128"], io["ones_col"], io["ones_row"], io["ident"],
        io["w1_t"], io["w3_t"], io["w2_t"], io["sw1_t"], io["sw3_t"], io["sw2_t"],
        io["x4s"], io["xstage"], io["tokstage"], io["gstage"],
        io["y1dram"], io["y2dram"], io["zdram"])
    NSLOT = 16 * CAP_D
    NT = TPC // 128
    from contextlib import ExitStack
    with ExitStack() as es:
        E = es.enter_context
        gwt = E(nc.sbuf_tensor(_pref + "gwt", [128, 8 * 8], F32))
        gbt = E(nc.sbuf_tensor(_pref + "gbt", [128, 8], F32))
        io8 = E(nc.sbuf_tensor(_pref + "io8", [128, 8], F32))
        ioc = E(nc.sbuf_tensor(_pref + "ioc", [128, 1], F32))
        tcol = E(nc.sbuf_tensor(_pref + "tcol", [128, 1], F32))
        gcol = E(nc.sbuf_tensor(_pref + "gcol", [128, 1], F32))
        ltt = E(nc.sbuf_tensor(_pref + "ltt", [128, 128], F32))
        onc = E(nc.sbuf_tensor(_pref + "onc", [128, 1], F32))
        onr = E(nc.sbuf_tensor(_pref + "onr", [1, 128], F32))
        idn = E(nc.sbuf_tensor(_pref + "idn", [128, 128], F32))
        hug = E(nc.sbuf_tensor(_pref + "hug", [128, NSLOT // 128], F32))
        zt = E(nc.sbuf_tensor(_pref + "zt", [128, D], F32))
        xsl = E(nc.sbuf_tensor(_pref + "xsl", [128, 8 * 128], F32))
        x3T = E(nc.sbuf_tensor(_pref + "x3T", [128, 8 * TPC], BF16))
        x3r = E(nc.sbuf_tensor(_pref + "x3r", [128, 2 * D], F32))
        lg = E(nc.sbuf_tensor(_pref + "lg", [128, 8], F32))
        nmx = E(nc.sbuf_tensor(_pref + "nmx", [128, 1], F32))
        e8 = E(nc.sbuf_tensor(_pref + "e8", [128, 8], F32))
        sumv = E(nc.sbuf_tensor(_pref + "sumv", [128, 1], F32))
        rz1 = E(nc.sbuf_tensor(_pref + "rz1", [128, 1], F32))
        probs = E(nc.sbuf_tensor(_pref + "probs", [128, 8], F32))
        scb = E(nc.sbuf_tensor(_pref + "scb", [128, 8], F32))
        mx8 = E(nc.sbuf_tensor(_pref + "mx8", [128, 8], F32))
        ix8 = E(nc.sbuf_tensor(_pref + "ix8", [128, 8], U32))
        ef32 = E(nc.sbuf_tensor(_pref + "ef32", [128, 1], F32))
        eq8 = E(nc.sbuf_tensor(_pref + "eq8", [128, 8], F32))
        bb8 = E(nc.sbuf_tensor(_pref + "bb8", [128, 8], F32))
        tmp8 = E(nc.sbuf_tensor(_pref + "tmp8", [128, 8], F32))
        scr8 = E(nc.sbuf_tensor(_pref + "scr8", [128, 8], F32))
        basev = E(nc.sbuf_tensor(_pref + "basev", [1, 16], F32))
        rankv = E(nc.sbuf_tensor(_pref + "rankv", [128, 1], F32))
        posv = E(nc.sbuf_tensor(_pref + "posv", [128, 1], F32))
        ovf = E(nc.sbuf_tensor(_pref + "ovf", [128, 1], F32))
        posi = E(nc.sbuf_tensor(_pref + "posi", [128, 1], I32))
        mu = E(nc.sbuf_tensor(_pref + "mu", [128, 4], F32))
        ssq = E(nc.sbuf_tensor(_pref + "ssq", [128, 2], F32))
        rstd = E(nc.sbuf_tensor(_pref + "rstd", [128, 2], F32))
        epsb = E(nc.sbuf_tensor(_pref + "epsb", [128, 1], F32))
        att = E(nc.sbuf_tensor(_pref + "att", [128, 2 * D], F32))
        sqs = E(nc.sbuf_tensor(_pref + "sqs", [128, D], F32))
        x4t = E(nc.sbuf_tensor(_pref + "x4t", [128, 2 * D], F32))
        yr1 = E(nc.sbuf_tensor(_pref + "yr1", [128, 2 * D], F32))
        yr2 = E(nc.sbuf_tensor(_pref + "yr2", [128, 2 * D], F32))
        pL = E(nc.psum_tensor(_pref + "pL", [128, 128], F32))
        pH1 = E(nc.psum_tensor(_pref + "pH1", [128, 512], F32))
        pH3 = E(nc.psum_tensor(_pref + "pH3", [128, 512], F32))
        pL2 = E(nc.psum_tensor(_pref + "pL2", [128, 128], F32))
        pH1b = E(nc.psum_tensor(_pref + "pH1b", [128, 512], F32))
        pH3b = E(nc.psum_tensor(_pref + "pH3b", [128, 512], F32))
        pQ = E(nc.psum_tensor(_pref + "pQ", [128, 1024], F32))
        lds = sems["lds"]
        ldr0 = sems["ldr0"]
        ldr1 = sems["ldr1"]
        ldq = sems["ldq"]
        ldg = sems["ldg"]
        sct = sems["sct"]
        yst = sems["yst"]
        zst = sems["zst"]
        pe = sems["pe"]
        dve = sems["dve"]
        act = sems["act"]
        st = sems["st"]
        ldsh = sems["ldsh"]
        rb_slot = nc.gpsimd.to_reg(NSLOT)
        rb_tok = nc.gpsimd.to_reg(TPC)

        @_syn(nc)
        def _(s):
            s.dma_start(out=gwt[:].rearrange("p (k e) -> p k e", e=8),
                        in_=gw_T.rearrange("(k p) e -> p k e", p=128)).then_inc(lds, 16)
            s.dma_start(out=gbt[:], in_=gb_bc).then_inc(lds, 16)
            s.dma_start(out=io8[:], in_=iota8).then_inc(lds, 16)
            s.dma_start(out=ioc[:], in_=iotac).then_inc(lds, 16)
            s.dma_start(out=ltt[:], in_=lt128).then_inc(lds, 16)
            s.dma_start(out=onc[:], in_=ones_col).then_inc(lds, 16)
            s.dma_start(out=onr[:], in_=ones_row).then_inc(lds, 16)
            s.dma_start(out=idn[:], in_=ident).then_inc(lds, 16)
        ctr.inc(lds, 8 * 16)
        lds_all = ctr.val(lds)

        @_vec(nc)
        def _(v):
            v.memset(hug[:], 1.0e9)
            v.drain()
            v.memset(zt[:], 0.0)
            v.drain()
            v.memset(basev[:], 0.0)
            v.drain()
            v.memset(epsb[:], LN_EPS)
            v.drain()
            v.engine_nop().then_inc(dve, 1)
        ctr.inc(dve, 1)
        ms_dve = ctr.val(dve)

        @_syn(nc)
        def _(s):
            s.wait_ge(dve, ms_dve)
            s.dma_start(out=tokstage.rearrange("(b a) c -> b (a c)", b=128),
                        in_=hug[:]).then_inc(zst, 16)
            s.dma_start(out=gstage.rearrange("(b a) c -> b (a c)", b=128),
                        in_=zt[:, 0:NSLOT // 128]).then_inc(zst, 16)
            for i in range(NT):
                s.dma_start(out=y1dram[i * 128:(i + 1) * 128, :], in_=zt[:]).then_inc(zst, 16)
                s.dma_start(out=y2dram[i * 128:(i + 1) * 128, :], in_=zt[:]).then_inc(zst, 16)
            for i in range(NSLOT // 128):
                s.dma_start(out=xstage[i * 128:(i + 1) * 128, :], in_=zt[:]).then_inc(zst, 16)
        ctr.inc(zst, 16 * (2 + 2 * NT + NSLOT // 128))
        zst_all = ctr.val(zst)

        sct_rt0 = ctr.val(sct)
        for it in range(NT):
            sct_prev = sct_rt0 + 16 * 6 * it
            @_syn(nc)
            def _(s, it=it, sct_prev=sct_prev):
                if it > 0:
                    s.wait_ge(sct, sct_prev)
                s.dma_start(out=x3r[:, 0:D], in_=x3s[it * 128:(it + 1) * 128, :]).then_inc(ldq, 16)
            ctr.inc(ldq, 16)
            ldqv = ctr.val(ldq)
            tr_ev = {}
            for kd in range(8):
                pLx = pL if kd % 2 == 0 else pL2
                ev_pre = tr_ev.get(kd - 2, 0)
                @_ten(nc)
                def _(t, kd=kd, ldqv=ldqv, ev_pre=ev_pre, pLx=pLx):
                    if kd == 0:
                        t.wait_ge(lds, lds_all)
                        t.wait_ge(ldq, ldqv)
                    if ev_pre > 0:
                        t.wait_ge(dve, ev_pre)
                    t.transpose(pLx[:, 0:128], x3r[:, kd * 128:(kd + 1) * 128],
                                idn[:]).then_inc(pe, 1)
                ctr.inc(pe, 1)
                pev = ctr.val(pe)
                @_vec(nc)
                def _(v, kd=kd, it=it, pev=pev, pLx=pLx):
                    v.wait_ge(pe, pev)
                    v.tensor_copy(xsl[:, kd * 128:(kd + 1) * 128], pLx[:, 0:128])
                    v.drain()
                    v.tensor_copy(x3T[:, kd * TPC + it * 128: kd * TPC + (it + 1) * 128],
                                  pLx[:, 0:128])
                    v.drain()
                    v.engine_nop().then_inc(dve, 1)
                ctr.inc(dve, 1)
                tr_ev[kd] = ctr.val(dve)
            dvev = ctr.val(dve)
            @_ten(nc)
            def _(t, dvev=dvev):
                t.wait_ge(dve, dvev)
                last = None
                for kd in range(8):
                    last = t.matmul(pL[:, 0:8], xsl[:, kd * 128:(kd + 1) * 128],
                                    gwt[:, kd * 8:(kd + 1) * 8],
                                    start=(kd == 0), stop=(kd == 7))
                last.then_inc(pe, 1)
            ctr.inc(pe, 1)
            pev = ctr.val(pe)
            @_vec(nc)
            def _(v, pev=pev):
                v.wait_ge(pe, pev)
                v.reduce_max(nmx[:], pL[:, 0:8], axis=AX.X, negate=True)
                v.drain()
                v.tensor_copy(lg[:], pL[:, 0:8])
                v.drain()
                v.engine_nop().then_inc(dve, 1)
            ctr.inc(dve, 1)
            dvev = ctr.val(dve)
            @_scl(nc)
            def _(s, dvev=dvev):
                s.wait_ge(dve, dvev)
                s.activation(e8[:], lg[:], AF.Exp, bias=nmx[:], accum_out=sumv[:]).then_inc(act, 1)
            ctr.inc(act, 1)
            actv = ctr.val(act)
            @_vec(nc)
            def _(v, actv=actv):
                v.wait_ge(act, actv)
                v.reciprocal(rz1[:], sumv[:])
                v.drain()
                v.tensor_scalar(probs[:], e8[:], rz1[:], None, ALU.mult)
                v.drain()
                v.tensor_add(scb[:], probs[:], gbt[:])
                v.drain()
                v.max(mx8[:], scb[:])
                v.drain()
                v.max_index(ix8[:], mx8[:], scb[:])
                v.drain()
                v.tensor_scalar(tcol[:], ioc[:], float(it * 128), None, ALU.add)
                v.drain()
                v.engine_nop().then_inc(dve, 1)
            ctr.inc(dve, 1)
            for k in range(2):
                dvev = ctr.val(dve)
                sct_k = ctr.val(sct)
                @_vec(nc)
                def _(v, k=k, sct_k=sct_k):
                    if k == 1:
                        v.wait_ge(sct, sct_k)
                    v.tensor_copy(ef32[:], ix8[:, k:k + 1])
                    v.drain()
                    v.tensor_scalar(eq8[:], io8[:], ef32[:], None, ALU.is_equal)
                    v.drain()
                    v.scalar_tensor_tensor(scr8[:], probs[:], 1.0, eq8[:], ALU.mult, ALU.mult,
                                           accum_out=gcol[:])
                    v.drain()
                    v.engine_nop().then_inc(dve, 1)
                ctr.inc(dve, 1)
                dvev = ctr.val(dve)
                @_ten(nc)
                def _(t, k=k, dvev=dvev):
                    t.wait_ge(dve, dvev)
                    t.matmul(pL[:, 16:24], ltt[:], eq8[:], start=True, stop=True)
                    t.matmul(pL[:, 32:40], onr[:], basev[0:1, k * 8:(k + 1) * 8],
                             start=True, stop=True)
                    t.matmul(pL[0:1, 48:56], onc[:], eq8[:], start=True, stop=True).then_inc(pe, 1)
                ctr.inc(pe, 1)
                pev = ctr.val(pe)
                @_scl(nc)
                def _(s, pev=pev):
                    s.wait_ge(pe, pev)
                    s.copy(bb8[:], pL[:, 32:40]).then_inc(act, 1)
                ctr.inc(act, 1)
                actv = ctr.val(act)
                @_vec(nc)
                def _(v, k=k, actv=actv):
                    v.wait_ge(act, actv)
                    v.tensor_add(tmp8[:], pL[:, 16:24], bb8[:])
                    v.drain()
                    v.scalar_tensor_tensor(scr8[:], tmp8[:], 1.0, eq8[:], ALU.mult, ALU.mult,
                                           accum_out=rankv[:])
                    v.drain()
                    v.tensor_scalar(posv[:], ef32[:], float(CAP_D), None, ALU.mult)
                    v.drain()
                    v.tensor_add(posv[:], posv[:], rankv[:])
                    v.drain()
                    v.tensor_scalar(posv[:], posv[:], float(k * 8 * CAP_D), None, ALU.add)
                    v.drain()
                    v.tensor_scalar(ovf[:], rankv[:], float(CAP_D), None, ALU.is_ge)
                    v.drain()
                    v.scalar_tensor_tensor(posv[:], ovf[:], 1.0e9, posv[:], ALU.mult, ALU.add)
                    v.drain()
                    v.tensor_copy(posi[:], posv[:])
                    v.drain()
                    v.tensor_add(basev[0:1, k * 8:(k + 1) * 8], basev[0:1, k * 8:(k + 1) * 8],
                                 pL[0:1, 48:56])
                    v.drain()
                    v.engine_nop().then_inc(dve, 1)
                ctr.inc(dve, 1)
                dvev = ctr.val(dve)
                @_gps(nc)
                def _(g, dvev=dvev):
                    g.wait_ge(zst, zst_all)
                    g.wait_ge(dve, dvev)
                    g.indirect_dma_start(
                        xstage, IndirectOffsetOnAxis(ap=posi[:], axis=0), x3r[:, 0:D], None,
                        bounds_check=rb_slot, oob_is_err=False).then_inc(sct, 16)
                    g.indirect_dma_start(
                        tokstage, IndirectOffsetOnAxis(ap=posi[:], axis=0), tcol[:], None,
                        bounds_check=rb_slot, oob_is_err=False).then_inc(sct, 16)
                    g.indirect_dma_start(
                        gstage, IndirectOffsetOnAxis(ap=posi[:], axis=0), gcol[:], None,
                        bounds_check=rb_slot, oob_is_err=False).then_inc(sct, 16)
                ctr.inc(sct, 48)
        sct_all = ctr.val(sct)

        es_sh = ExitStack()
        sw1s = es_sh.enter_context(nc.sbuf_tensor(_pref + "sw1s", [128, 8 * 2048], BF16))
        sw3s = es_sh.enter_context(nc.sbuf_tensor(_pref + "sw3s", [128, 8 * 2048], BF16))
        sw2s = es_sh.enter_context(nc.sbuf_tensor(_pref + "sw2s", [128, 16 * D], BF16))
        hssh = es_sh.enter_context(nc.sbuf_tensor(_pref + "hssh", [128, 2 * 512], F32))
        hgsh = es_sh.enter_context(nc.sbuf_tensor(_pref + "hgsh", [128, 16 * 512], BF16))
        @_gps(nc)
        def _(g):
            g.wait_ge(sct, sct_all)
            g.dma_start(out=sw1s[:].rearrange("p (k i) -> p k i", i=2048),
                        in_=sw1_t.rearrange("(k p) i -> p k i", p=128)).then_inc(ldg, 16)
            g.dma_start(out=sw3s[:].rearrange("p (k i) -> p k i", i=2048),
                        in_=sw3_t.rearrange("(k p) i -> p k i", p=128)).then_inc(ldg, 16)
            g.dma_start(out=sw2s[:].rearrange("p (k d) -> p k d", d=D),
                        in_=sw2_t.rearrange("(k p) d -> p k d", p=128)).then_inc(ldg, 16)
        ctr.inc(ldg, 48)
        ldg_sh = ctr.val(ldg)

        sh_mul = {}
        for it4 in range(4):
            for ic in range(16):
                pH1c = pH1 if ic % 2 == 0 else pH1b
                pH3c = pH3 if ic % 2 == 0 else pH3b
                hsl = hssh[:, (ic % 2) * 512: (ic % 2) * 512 + 512]
                dve_pre = ctr.val(dve)
                @_ten(nc)
                def _(t, ic=ic, it4=it4, dve_pre=dve_pre, pH1c=pH1c, pH3c=pH3c):
                    t.wait_ge(dve, dve_pre)
                    if ic == 0 and it4 == 0:
                        t.wait_ge(ldg, ldg_sh)
                    last = None
                    for kd in range(8):
                        t.matmul(pH1c[:, 0:512],
                                 sw1s[:, kd * 2048 + ic * 128: kd * 2048 + ic * 128 + 128],
                                 x3T[:, kd * TPC + it4 * 512: kd * TPC + (it4 + 1) * 512],
                                 start=(kd == 0), stop=(kd == 7))
                        last = t.matmul(pH3c[:, 0:512],
                                        sw3s[:, kd * 2048 + ic * 128: kd * 2048 + ic * 128 + 128],
                                        x3T[:, kd * TPC + it4 * 512: kd * TPC + (it4 + 1) * 512],
                                        start=(kd == 0), stop=(kd == 7))
                    last.then_inc(pe, 1)
                ctr.inc(pe, 1)
                pev = ctr.val(pe)
                sguard = sh_mul.get(ic - 2, 0)
                @_scl(nc)
                def _(s, pev=pev, pH1c=pH1c, hsl=hsl, sguard=sguard):
                    s.wait_ge(pe, pev)
                    if sguard > 0:
                        s.wait_ge(dve, sguard)
                    s.activation(hsl, pH1c[:, 0:512], AF.Silu).then_inc(act, 1)
                ctr.inc(act, 1)
                actv = ctr.val(act)
                @_vec(nc)
                def _(v, ic=ic, actv=actv, pH3c=pH3c, hsl=hsl):
                    v.wait_ge(act, actv)
                    v.tensor_mul(hgsh[:, ic * 512:(ic + 1) * 512], hsl,
                                 pH3c[:, 0:512]).then_inc(dve, 1)
                ctr.inc(dve, 1)
                sh_mul[ic] = ctr.val(dve)
            hg_done = ctr.val(dve)
            w2ev = []
            for tc in range(4):
                wpar = tc % 2
                bank_guard = w2ev[tc - 2] if tc >= 2 else 0
                @_ten(nc)
                def _(t, tc=tc, wpar=wpar, hg_done=hg_done, bank_guard=bank_guard):
                    if tc == 0:
                        t.wait_ge(dve, hg_done)
                    elif bank_guard > 0:
                        t.wait_ge(dve, bank_guard)
                    last = None
                    for dh in range(2):
                        if wpar == 0:
                            out = pQ[:, dh * 512: dh * 512 + 512]
                        else:
                            out = (pH1 if dh == 0 else pH3)[:, 0:512]
                        for ki in range(16):
                            last = t.matmul(out,
                                            hgsh[:, ki * 512 + tc * 128: ki * 512 + tc * 128 + 128],
                                            sw2s[:, ki * D + dh * 512: ki * D + dh * 512 + 512],
                                            start=(ki == 0), stop=(ki == 15))
                    last.then_inc(pe, 1)
                ctr.inc(pe, 1)
                pev = ctr.val(pe)
                st_att = ctr.val(st) - 16 if tc >= 2 or it4 > 0 else 0
                @_vec(nc)
                def _(v, tc=tc, wpar=wpar, pev=pev, st_att=st_att):
                    v.wait_ge(pe, pev)
                    if st_att > 0:
                        v.wait_ge(st, st_att)
                    zsl = att[:, (tc % 2) * D: (tc % 2) * D + D]
                    if wpar == 0:
                        v.tensor_copy(zsl, pQ[:, 0:1024])
                    else:
                        v.tensor_copy(zsl[:, 0:512], pH1[:, 0:512])
                        v.drain()
                        v.tensor_copy(zsl[:, 512:1024], pH3[:, 0:512])
                    v.drain()
                    v.engine_nop().then_inc(dve, 1)
                ctr.inc(dve, 1)
                w2ev.append(ctr.val(dve))
                dvev = ctr.val(dve)
                st_prev = ctr.val(st)
                @_gps(nc)
                def _(g, it4=it4, tc=tc, dvev=dvev, st_prev=st_prev):
                    g.wait_ge(dve, dvev)
                    if st_prev > 0:
                        g.wait_ge(st, st_prev)
                    g.dma_start(out=zdram[it4 * 512 + tc * 128: it4 * 512 + (tc + 1) * 128, :],
                                in_=att[:, (tc % 2) * D: (tc % 2) * D + D]).then_inc(st, 16)
                ctr.inc(st, 16)
        sh_st = ctr.val(st)
        pe_sh = ctr.val(pe)
        es_sh.close()

        w1sb = E(nc.sbuf_tensor(_pref + "w1sb", [128, 8 * 1024], BF16))
        w3sb = E(nc.sbuf_tensor(_pref + "w3sb", [128, 8 * 1024], BF16))
        w2sb = E(nc.sbuf_tensor(_pref + "w2sb", [128, 8 * D], BF16))
        xg4 = E(nc.sbuf_tensor(_pref + "xg4", [128, 8 * D], F32))
        xgT5 = E(nc.sbuf_tensor(_pref + "xgT5", [128, 8 * 512], BF16))
        hg5 = E(nc.sbuf_tensor(_pref + "hg5", [128, 8 * 512], BF16))
        hs5 = E(nc.sbuf_tensor(_pref + "hs5", [128, 2 * 512], F32))
        ysb = E(nc.sbuf_tensor(_pref + "ysb", [128, D], F32))
        tif = E(nc.sbuf_tensor(_pref + "tif", [128, 8], F32))
        gif = E(nc.sbuf_tensor(_pref + "gif", [128, 8], F32))
        ti32 = E(nc.sbuf_tensor(_pref + "ti32", [128, 1], I32))
        sub_tev = {}
        sub_w2e = {}
        st_shared_done = ctr.val(st)
        gsub = 0
        for e in range(8):
            dve_now = ctr.val(dve)
            st_now = ctr.val(st)
            @_gps(nc)
            def _(g, e=e, dve_now=dve_now, st_now=st_now):
                if e == 0:
                    g.wait_ge(st, st_now)
                else:
                    g.wait_ge(dve, dve_now)
                g.dma_start(out=w1sb[:].rearrange("p (k i) -> p k i", i=1024),
                            in_=w1_t[e].rearrange("(k p) i -> p k i", p=128)).then_inc(ldg, 16)
                g.dma_start(out=w3sb[:].rearrange("p (k i) -> p k i", i=1024),
                            in_=w3_t[e].rearrange("(k p) i -> p k i", p=128)).then_inc(ldg, 16)
                g.dma_start(out=w2sb[:].rearrange("p (k d) -> p k d", d=D),
                            in_=w2_t[e].rearrange("(k p) d -> p k d", p=128)).then_inc(ldg, 16)
            ctr.inc(ldg, 48)
            ldgv = ctr.val(ldg)
            for k in range(2):
                for sub in range(2):
                    par = gsub % 2
                    base = (k * 8 + e) * CAP_D + sub * 512
                    tguard = sub_tev.get(gsub - 2, 0)
                    wguard = sub_w2e.get(gsub - 2, 0)
                    @_syn(nc)
                    def _(s, base=base, par=par, tguard=tguard, wguard=wguard, gsub=gsub):
                        s.wait_ge(sct, sct_all)
                        if gsub < 2:
                            s.wait_ge(st, st_shared_done)
                        if tguard > 0:
                            s.wait_ge(dve, tguard)
                        if wguard > 0:
                            s.wait_ge(dve, wguard)
                        ldrx = ldr0 if par == 0 else ldr1
                        for tc in range(4):
                            s.dma_start(out=xg4[:, (par * 4 + tc) * D: (par * 4 + tc + 1) * D],
                                        in_=xstage[base + tc * 128: base + (tc + 1) * 128, :]
                                        ).then_inc(ldrx, 16)
                            s.dma_start(out=tif[:, par * 4 + tc: par * 4 + tc + 1],
                                        in_=tokstage[base + tc * 128: base + (tc + 1) * 128, :]
                                        ).then_inc(ldrx, 16)
                            s.dma_start(out=gif[:, par * 4 + tc: par * 4 + tc + 1],
                                        in_=gstage[base + tc * 128: base + (tc + 1) * 128, :]
                                        ).then_inc(ldrx, 16)
                    ctr.inc(ldr0 if par == 0 else ldr1, 16 * 12)
                    ldqv = ctr.val(ldr0 if par == 0 else ldr1)
                    ti = 0
                    ev_list = []
                    for tc in range(4):
                        for kd in range(8):
                            pLx = pL if ti % 2 == 0 else pL2
                            ev_pre = ev_list[ti - 2] if ti >= 2 else 0
                            @_ten(nc)
                            def _(t, tc=tc, kd=kd, par=par, pLx=pLx, ti=ti,
                                  ldqv=ldqv, ev_pre=ev_pre):
                                if ti == 0:
                                    t.wait_ge(ldr0 if par == 0 else ldr1, ldqv)
                                elif ev_pre > 0:
                                    t.wait_ge(dve, ev_pre)
                                t.transpose(pLx[:, 0:128],
                                            xg4[:, (par * 4 + tc) * D + kd * 128:
                                                (par * 4 + tc) * D + kd * 128 + 128],
                                            idn[:]).then_inc(pe, 1)
                            ctr.inc(pe, 1)
                            pev = ctr.val(pe)
                            @_vec(nc)
                            def _(v, tc=tc, kd=kd, pLx=pLx, pev=pev):
                                v.wait_ge(pe, pev)
                                v.tensor_copy(xgT5[:, kd * 512 + tc * 128: kd * 512 + tc * 128 + 128],
                                              pLx[:, 0:128]).then_inc(dve, 1)
                            ctr.inc(dve, 1)
                            ev_list.append(ctr.val(dve))
                            ti += 1
                    sub_tev[gsub] = ctr.val(dve)
                    tev = ctr.val(dve)
                    mul_dve = {}
                    for ic in range(8):
                        pH1c = pH1 if ic % 2 == 0 else pH1b
                        pH3c = pH3 if ic % 2 == 0 else pH3b
                        hsl = hs5[:, (ic % 2) * 512: (ic % 2) * 512 + 512]
                        dve_pre = ctr.val(dve)
                        @_ten(nc)
                        def _(t, ic=ic, tev=tev, dve_pre=dve_pre, ldgv=ldgv,
                              pH1c=pH1c, pH3c=pH3c):
                            if ic == 0:
                                t.wait_ge(dve, tev)
                                t.wait_ge(ldg, ldgv)
                            elif ic >= 2:
                                t.wait_ge(dve, dve_pre)
                            last = None
                            for kd in range(8):
                                t.matmul(pH1c[:, 0:512],
                                         w1sb[:, kd * 1024 + ic * 128: kd * 1024 + ic * 128 + 128],
                                         xgT5[:, kd * 512:(kd + 1) * 512],
                                         start=(kd == 0), stop=(kd == 7))
                                last = t.matmul(pH3c[:, 0:512],
                                                w3sb[:, kd * 1024 + ic * 128: kd * 1024 + ic * 128 + 128],
                                                xgT5[:, kd * 512:(kd + 1) * 512],
                                                start=(kd == 0), stop=(kd == 7))
                            last.then_inc(pe, 1)
                        ctr.inc(pe, 1)
                        pev = ctr.val(pe)
                        sguard = mul_dve.get(ic - 2, 0)
                        @_scl(nc)
                        def _(s, pev=pev, pH1c=pH1c, hsl=hsl, sguard=sguard):
                            s.wait_ge(pe, pev)
                            if sguard > 0:
                                s.wait_ge(dve, sguard)
                            s.activation(hsl, pH1c[:, 0:512], AF.Silu).then_inc(act, 1)
                        ctr.inc(act, 1)
                        actv = ctr.val(act)
                        @_vec(nc)
                        def _(v, ic=ic, actv=actv, pH3c=pH3c, hsl=hsl):
                            v.wait_ge(act, actv)
                            v.tensor_mul(hg5[:, ic * 512:(ic + 1) * 512], hsl,
                                         pH3c[:, 0:512]).then_inc(dve, 1)
                        ctr.inc(dve, 1)
                        mul_dve[ic] = ctr.val(dve)
                    hg_done = ctr.val(dve)
                    w2ev = []
                    for tc in range(4):
                        wpar = tc % 2
                        bank_guard = w2ev[tc - 2] if tc >= 2 else 0
                        @_ten(nc)
                        def _(t, tc=tc, wpar=wpar, hg_done=hg_done, bank_guard=bank_guard):
                            if tc == 0:
                                t.wait_ge(dve, hg_done)
                            elif bank_guard > 0:
                                t.wait_ge(dve, bank_guard)
                            last = None
                            for dh in range(2):
                                if wpar == 0:
                                    out = pQ[:, dh * 512: dh * 512 + 512]
                                else:
                                    out = (pH1 if dh == 0 else pH3)[:, 0:512]
                                for ki in range(8):
                                    last = t.matmul(out,
                                                    hg5[:, ki * 512 + tc * 128: ki * 512 + tc * 128 + 128],
                                                    w2sb[:, ki * D + dh * 512: ki * D + dh * 512 + 512],
                                                    start=(ki == 0), stop=(ki == 7))
                            last.then_inc(pe, 1)
                        ctr.inc(pe, 1)
                        pev = ctr.val(pe)
                        yst_prev = ctr.val(yst)
                        @_vec(nc)
                        def _(v, tc=tc, par=par, wpar=wpar, pev=pev, yst_prev=yst_prev):
                            v.wait_ge(pe, pev)
                            if yst_prev > 0:
                                v.wait_ge(yst, yst_prev)
                            gcolx = gif[:, par * 4 + tc: par * 4 + tc + 1]
                            if wpar == 0:
                                v.tensor_scalar(ysb[:], pQ[:, 0:1024], gcolx, None, ALU.mult)
                            else:
                                v.tensor_scalar(ysb[:, 0:512], pH1[:, 0:512], gcolx, None, ALU.mult)
                                v.drain()
                                v.tensor_scalar(ysb[:, 512:1024], pH3[:, 0:512], gcolx, None, ALU.mult)
                            v.drain()
                            v.tensor_copy(ti32[:], tif[:, par * 4 + tc: par * 4 + tc + 1])
                            v.drain()
                            v.engine_nop().then_inc(dve, 1)
                        ctr.inc(dve, 1)
                        w2ev.append(ctr.val(dve))
                        dvev = ctr.val(dve)
                        yst_p2 = ctr.val(yst)
                        @_gps(nc)
                        def _(g, k=k, dvev=dvev, yst_p2=yst_p2):
                            g.wait_ge(dve, dvev)
                            if yst_p2 > 0:
                                g.wait_ge(yst, yst_p2)
                            g.indirect_dma_start(
                                y1dram if k == 0 else y2dram,
                                IndirectOffsetOnAxis(ap=ti32[:], axis=0),
                                ysb[:], None,
                                bounds_check=rb_tok, oob_is_err=False).then_inc(yst, 16)
                        ctr.inc(yst, 16)
                    sub_w2e[gsub] = ctr.val(dve)
                    gsub += 1
        yst_all = ctr.val(yst)

        st_base_f = ctr.val(st)
        fin_dve = {}
        fin_act = {}
        for it in range(NT):
            par = it % 2
            ldq_prev = ctr.val(ldq)
            dguard = fin_dve.get(it - 2, 0)
            aguard = fin_act.get(it - 2, 0)
            @_syn(nc)
            def _(s, it=it, par=par, ldq_prev=ldq_prev, dguard=dguard, aguard=aguard):
                if it == 0:
                    s.wait_ge(yst, yst_all)
                else:
                    s.wait_ge(ldq, ldq_prev)
                if dguard > 0:
                    s.wait_ge(dve, dguard)
                if aguard > 0:
                    s.wait_ge(act, aguard)
                s.dma_start(out=yr1[:, par * D: par * D + D],
                            in_=y1dram[it * 128:(it + 1) * 128, :]).then_inc(ldq, 16)
                s.dma_start(out=yr2[:, par * D: par * D + D],
                            in_=y2dram[it * 128:(it + 1) * 128, :]).then_inc(ldq, 16)
                s.dma_start(out=x3r[:, par * D: par * D + D],
                            in_=x3s[it * 128:(it + 1) * 128, :]).then_inc(ldq, 16)
                s.dma_start(out=att[:, par * D: par * D + D],
                            in_=zdram[it * 128:(it + 1) * 128, :]).then_inc(ldq, 16)
            ctr.inc(ldq, 64)
            ldqv = ctr.val(ldq)
            st_w = st_base_f + (it - 1) * 16 if it >= 2 else 0
            @_vec(nc)
            def _(v, par=par, ldqv=ldqv, st_w=st_w):
                v.wait_ge(ldq, ldqv)
                if st_w > 0:
                    v.wait_ge(st, st_w)
                ap = att[:, par * D: par * D + D]
                v.tensor_add(ap, ap, yr1[:, par * D: par * D + D])
                v.drain()
                v.tensor_add(ap, ap, yr2[:, par * D: par * D + D])
                v.drain()
                v.tensor_add(ap, ap, x3r[:, par * D: par * D + D])
                v.drain()
                v.engine_nop().then_inc(dve, 1)
            ctr.inc(dve, 1)
            emit_ln_fast(nc, ctr, dve, act,
                         att[:, par * D: par * D + D],
                         x4t[:, par * D: par * D + D],
                         yr1[:, par * D: par * D + D],
                         mu[:, par: par + 1], ssq[:, par: par + 1],
                         rstd[:, par: par + 1], sqs, epsb,
                         mu[:, 2 + par: 3 + par])
            fin_dve[it] = ctr.val(dve)
            fin_act[it] = ctr.val(act)
            dvev = ctr.val(dve)
            st_prev = ctr.val(st)
            @_gps(nc)
            def _(g, it=it, par=par, dvev=dvev, st_prev=st_prev):
                g.wait_ge(dve, dvev)
                if it > 0:
                    g.wait_ge(st, st_prev)
                g.dma_start(out=x4s[it * 128:(it + 1) * 128, :],
                            in_=x4t[:, par * D: par * D + D]).then_inc(st, 16)
            ctr.inc(st, 16)
        @_gps(nc)
        def _(g):
            g.wait_ge(st, ctr.val(st))
    return nc


def build_stage_d():
    nc = bass.Bass("TRN2", target_bir_lowering=False, debug=False)
    NSLOT = 16 * CAP_D
    io = {}
    io["x3s"] = nc.dram_tensor("x3s", [TPC, D], F32, kind="ExternalInput").ap()
    io["gw_T"] = nc.dram_tensor("gw_T", [D, 8], F32, kind="ExternalInput").ap()
    io["gb_bc"] = nc.dram_tensor("gb_bc", [128, 8], F32, kind="ExternalInput").ap()
    io["iota8"] = nc.dram_tensor("iota8", [128, 8], F32, kind="ExternalInput").ap()
    io["iotac"] = nc.dram_tensor("iotac", [128, 1], F32, kind="ExternalInput").ap()
    io["lt128"] = nc.dram_tensor("lt128", [128, 128], F32, kind="ExternalInput").ap()
    io["ones_col"] = nc.dram_tensor("ones_col", [128, 1], F32, kind="ExternalInput").ap()
    io["ones_row"] = nc.dram_tensor("ones_row", [1, 128], F32, kind="ExternalInput").ap()
    io["ident"] = nc.dram_tensor("ident", [128, 128], F32, kind="ExternalInput").ap()
    io["w1_t"] = nc.dram_tensor("w1_t", [8, D, 1024], F32, kind="ExternalInput").ap()
    io["w3_t"] = nc.dram_tensor("w3_t", [8, D, 1024], F32, kind="ExternalInput").ap()
    io["w2_t"] = nc.dram_tensor("w2_t", [8, 1024, D], F32, kind="ExternalInput").ap()
    io["sw1_t"] = nc.dram_tensor("sw1_t", [D, 2048], F32, kind="ExternalInput").ap()
    io["sw3_t"] = nc.dram_tensor("sw3_t", [D, 2048], F32, kind="ExternalInput").ap()
    io["sw2_t"] = nc.dram_tensor("sw2_t", [2048, D], F32, kind="ExternalInput").ap()
    io["x4s"] = nc.dram_tensor("x4s", [TPC, D], F32, kind="ExternalOutput").ap()
    io["xstage"] = nc.dram_tensor("xstage", [NSLOT, D], F32, kind="Internal").ap()
    io["tokstage"] = nc.dram_tensor("tokstage", [NSLOT, 1], F32, kind="Internal").ap()
    io["gstage"] = nc.dram_tensor("gstage", [NSLOT, 1], F32, kind="Internal").ap()
    io["y1dram"] = nc.dram_tensor("y1dram", [TPC, D], F32, kind="Internal").ap()
    io["y2dram"] = nc.dram_tensor("y2dram", [TPC, D], F32, kind="Internal").ap()
    io["zdram"] = nc.dram_tensor("zdram", [TPC, D], F32, kind="Internal").ap()
    ctr = Ctr()
    sems = {}
    from contextlib import ExitStack
    with ExitStack() as ses:
        for sname in ["lds", "ldq", "ldg", "sct", "yst", "zst", "pe", "dve",
                      "act", "st", "ldsh", "ldr0", "ldr1"]:
            sems[sname] = ses.enter_context(nc.semaphore(sname))
        emit_stage_d(nc, ctr, sems, io)
    return nc


def run_stage_d(inp, x3_full):
    nc = build_stage_d()
    lt = np.tril(np.ones((128, 128), np.float32), -1)
    shared = {
        "gw_T": np.ascontiguousarray(inp["L1_ffn_gate_w"].T),
        "gb_bc": np.ascontiguousarray(np.broadcast_to(inp["L1_ffn_gate_bias"], (128, 8))),
        "iota8": np.ascontiguousarray(np.broadcast_to(np.arange(8, dtype=np.float32), (128, 8))),
        "iotac": np.arange(128, dtype=np.float32).reshape(128, 1),
        "lt128": lt,
        "ones_col": np.ones((128, 1), np.float32),
        "ones_row": np.ones((1, 128), np.float32),
        "ident": np.eye(128, dtype=np.float32),
        "w1_t": np.ascontiguousarray(inp["L1_ffn_w1"].transpose(0, 2, 1)),
        "w3_t": np.ascontiguousarray(inp["L1_ffn_w3"].transpose(0, 2, 1)),
        "w2_t": np.ascontiguousarray(inp["L1_ffn_w2"].transpose(0, 2, 1)),
        "sw1_t": np.ascontiguousarray(inp["L1_ffn_sw1"].T),
        "sw3_t": np.ascontiguousarray(inp["L1_ffn_sw3"].T),
        "sw2_t": np.ascontiguousarray(inp["L1_ffn_sw2"].T),
    }
    in_maps = []
    for c in range(NCORE):
        sl = x3_full[c * TPC:(c + 1) * TPC]
        in_maps.append(dict(shared, x3s=np.ascontiguousarray(sl)))
    res = run_bass_kernel_spmd(nc, in_maps, core_ids=list(range(NCORE)))
    return np.concatenate([np.asarray(r["x4s"]) for r in res.results], 0)




def emit_phase_barrier(nc, ctr, sems, phase_no):
    ph = sems["ph"]
    @_gps(nc)
    def _(g):
        g.wait_ge(sems["st"], ctr.val(sems["st"]))
        g.nop().then_inc(ph, 1)
    ctr.inc(ph, 1)
    tgt = ctr.val(ph)
    for eng in (nc.tensor, nc.vector, nc.scalar, nc.sync, nc.gpsimd):
        eng.wait_ge(ph, tgt)


def build_mega():
    nc = bass.Bass("TRN2", target_bir_lowering=False, debug=False)
    NTOK_B = TPC + 128
    NH = TPC + HALO
    NSLOT_B = 32 * CAP_B
    NSLOT_D = 16 * CAP_D

    def inp2(name, shape, dt=F32):
        return nc.dram_tensor(name, shape, dt, kind="ExternalInput").ap()
    def internal(name, shape, dt=F32):
        return nc.dram_tensor(name, shape, dt, kind="Internal").ap()

    ident = inp2("ident", [128, 128])
    iotac = inp2("iotac", [128, 1])
    lt128 = inp2("lt128", [128, 128])
    ones_col = inp2("ones_col", [128, 1])
    ones_row = inp2("ones_row", [1, 128])

    io_a = dict(
        fT=inp2("fT", [D, T]), f_tm=inp2("f_tm", [T, D]),
        wq_T=inp2("wq_T", [D, D]), w2col=inp2("w2col", [128, 8]),
        wqbcol=inp2("wqbcol", [128, 8]), maskM=inp2("maskM", [128, 3 * 256]),
        ones164=inp2("ones164", [1, 64]), onesr128=ones_row,
        w2b_in=inp2("w2b_in", [1, 1]), ident=ident,
        cc_init=inp2("cc_init", [64, UW]), cr_init=inp2("cr_init", [128, 65 * 65]),
        gbcA=inp2("gbcA", [128, D]), bbcA=inp2("bbcA", [128, D]),
        x1s=internal("x1full", [T, D]),
    )
    io_b = dict(
        ntok=NTOK_B, nx1=T,
        x1full=io_a["x1s"],
        bgidx=inp2("bgidx", [NTOK_B, 1], I32), bjunk=inp2("bjunk", [NTOK_B, 1]),
        rw_T=inp2("rw_T", [D, 32]), rb_bc=inp2("rb_bc", [128, 32]),
        iota32=inp2("iota32", [128, 32]), iotac=iotac, lt128=lt128,
        ones_col=ones_col, ones_row=ones_row, ident=ident,
        w1_t=inp2("b_w1t", [32, D, 256]), b1c=inp2("b_b1c", [32, 128, 2]),
        w2_t=inp2("b_w2t", [32, 256, D]), b2bc=inp2("b_b2bc", [32, 128, D]),
        x2s=internal("x2loc", [NTOK_B, D]),
        xstage=internal("b_xstage", [NSLOT_B, D]),
        tokstage=internal("b_tokstage", [NSLOT_B, 1]),
        ydram=internal("b_ydram", [NTOK_B, D]),
    )
    io_c = dict(
        x2full=io_b["x2s"], halo_off=96, ident=ident,
        A_T=inp2("A_T", [D, D]), B_T=inp2("B_T", [D, D]), C_T=inp2("C_T", [D, D]),
        gbc=inp2("gbc", [128, D]), bbc=inp2("bbc", [128, D]),
        x3s=internal("x3loc", [TPC, D]),
    )
    io_d = dict(
        x3s=io_c["x3s"],
        gw_T=inp2("gw_T", [D, 8]), gb_bc=inp2("gb_bc", [128, 8]),
        iota8=inp2("iota8", [128, 8]), iotac=iotac, lt128=lt128,
        ones_col=ones_col, ones_row=ones_row, ident=ident,
        w1_t=inp2("d_w1t", [8, D, 1024]), w3_t=inp2("d_w3t", [8, D, 1024]),
        w2_t=inp2("d_w2t", [8, 1024, D]),
        sw1_t=inp2("sw1_t", [D, 2048]), sw3_t=inp2("sw3_t", [D, 2048]),
        sw2_t=inp2("sw2_t", [2048, D]),
        x4s=nc.dram_tensor("x4s", [TPC, D], F32, kind="ExternalOutput").ap(),
        xstage=internal("d_xstage", [NSLOT_D, D]),
        tokstage=internal("d_tokstage", [NSLOT_D, 1]),
        gstage=internal("d_gstage", [NSLOT_D, 1]),
        y1dram=internal("d_y1dram", [TPC, D]),
        y2dram=internal("d_y2dram", [TPC, D]),
        zdram=internal("d_zdram", [TPC, D]),
    )

    ctr = Ctr()
    sems = {}
    from contextlib import ExitStack
    with ExitStack() as ses:
        for sname in ["ld", "lds", "ldx", "ldf0", "ldf1", "ldr0", "ldr1", "ldq",
                      "ldg", "ldsh", "sct", "yst", "zst", "gth", "pe", "dve",
                      "act", "st", "ph"]:
            sems[sname] = ses.enter_context(nc.semaphore(sname))
        emit_stage_a(nc, ctr, sems, io_a)
        emit_phase_barrier(nc, ctr, sems, 1)
        emit_stage_b(nc, ctr, sems, io_b)
        emit_phase_barrier(nc, ctr, sems, 2)
        emit_stage_c(nc, ctr, sems, io_c)
        emit_phase_barrier(nc, ctr, sems, 3)
        emit_stage_d(nc, ctr, sems, io_d)
    return nc


def mega_in_maps(inp):
    cc, cr, msk = make_stage_a_consts()
    lt = np.tril(np.ones((128, 128), np.float32), -1)
    f = inp["L0_attn_feature"]
    NTOK_B = TPC + 128
    shared = {
        "ident": np.eye(128, dtype=np.float32),
        "iotac": np.arange(128, dtype=np.float32).reshape(128, 1),
        "lt128": lt,
        "ones_col": np.ones((128, 1), np.float32),
        "ones_row": np.ones((1, 128), np.float32),
        "wq_T": np.ascontiguousarray(inp["L0_attn_wq_w"].T),
        "w2col": np.ascontiguousarray(inp["L0_attn_w2_w"].reshape(8, 128).T),
        "wqbcol": np.ascontiguousarray(inp["L0_attn_wq_b"].reshape(8, 128).T),
        "maskM": msk,
        "ones164": np.full((1, 64), np.float32(OMEGA), np.float32),
        "w2b_in": np.asarray(inp["L0_attn_w2_b"], dtype=np.float32).reshape(1, 1),
        "cc_init": cc, "cr_init": cr,
        "gbcA": np.ascontiguousarray(np.broadcast_to(inp["L0_attn_ln_g"], (128, D))),
        "bbcA": np.ascontiguousarray(np.broadcast_to(inp["L0_attn_ln_b"], (128, D))),
        "rw_T": np.ascontiguousarray(inp["L0_ffn_router_w"].T),
        "rb_bc": np.ascontiguousarray(np.broadcast_to(inp["L0_ffn_router_b"], (128, 32))),
        "iota32": np.ascontiguousarray(np.broadcast_to(np.arange(32, dtype=np.float32), (128, 32))),
        "b_w1t": np.ascontiguousarray(inp["L0_ffn_weights1"]),
        "b_b1c": np.ascontiguousarray(inp["L0_ffn_bias1"].reshape(32, 2, 128).transpose(0, 2, 1)),
        "b_w2t": np.ascontiguousarray(inp["L0_ffn_weights2"]),
        "b_b2bc": np.ascontiguousarray(np.broadcast_to(inp["L0_ffn_bias2"][:, None, :], (32, 128, D))),
        "A_T": np.ascontiguousarray(inp["L1_attn_A"].T),
        "B_T": np.ascontiguousarray(inp["L1_attn_B"].T),
        "C_T": np.ascontiguousarray(inp["L1_attn_C"].T),
        "gbc": np.ascontiguousarray(np.broadcast_to(inp["L1_attn_gamma"], (128, D))),
        "bbc": np.ascontiguousarray(np.broadcast_to(inp["L1_attn_beta"], (128, D))),
        "gw_T": np.ascontiguousarray(inp["L1_ffn_gate_w"].T),
        "gb_bc": np.ascontiguousarray(np.broadcast_to(inp["L1_ffn_gate_bias"], (128, 8))),
        "iota8": np.ascontiguousarray(np.broadcast_to(np.arange(8, dtype=np.float32), (128, 8))),
        "d_w1t": np.ascontiguousarray(inp["L1_ffn_w1"].transpose(0, 2, 1)),
        "d_w3t": np.ascontiguousarray(inp["L1_ffn_w3"].transpose(0, 2, 1)),
        "d_w2t": np.ascontiguousarray(inp["L1_ffn_w2"].transpose(0, 2, 1)),
        "sw1_t": np.ascontiguousarray(inp["L1_ffn_sw1"].T),
        "sw3_t": np.ascontiguousarray(inp["L1_ffn_sw3"].T),
        "sw2_t": np.ascontiguousarray(inp["L1_ffn_sw2"].T),
    }
    in_maps = []
    for c in range(NCORE):
        bb = c // 4
        start = (c % 4) * TPC - 128
        gi = start + np.arange(NTOK_B, dtype=np.int64)
        junk = (gi < 0)
        gidx = np.where(junk, 10 ** 9, gi).astype(np.int32).reshape(NTOK_B, 1)
        jk = np.where(junk, 1.0e9, 0.0).astype(np.float32).reshape(NTOK_B, 1)
        in_maps.append(dict(shared,
                            fT=np.ascontiguousarray(f[bb].T),
                            f_tm=np.ascontiguousarray(f[bb]),
                            bgidx=gidx, bjunk=jk))
    return in_maps


def kernel(**inputs):
    inp = {k: np.asarray(v) for k, v in inputs.items()}
    nc = build_mega()
    in_maps = mega_in_maps(inp)
    res = run_bass_kernel_spmd(nc, in_maps, core_ids=list(range(NCORE)))
    out = np.empty((B, T, D), np.float32)
    for c in range(NCORE):
        bb, s = c // 4, (c % 4) * TPC
        out[bb, s:s + TPC] = np.asarray(res.results[c]["x4s"])
    return out
